# version 07
# baseline (speedup 1.0000x reference)
import contextlib
import numpy as np
import concourse.bass as bass
import concourse.mybir as mybir
from concourse.bass import ts as dslice
from concourse.bass_utils import run_bass_kernel_spmd

F32 = mybir.dt.float32
BF16 = mybir.dt.bfloat16
I32 = mybir.dt.int32
AF = mybir.ActivationFunctionType
ALU = mybir.AluOpType
AXF = mybir.AxisListType.X

B, T, D = 2, 4096, 2048
DI, DS, DCONV, DTR = 4096, 16, 4, 128
I0, I1 = 5632, 8192
H, HD, G, GRP = 32, 64, 8, 4
NCORE = 8
TC = 512
NCH = T // TC
DSL = DI // 4
EPS_LN = 1e-5
EPS_Q = 1e-5
RMS_EPS = 1e-6
MAGIC = 12582912.0
NEGB = -60.0
RG4 = [[0, 1, 2, 3], [4, 5, 6, 7]]
RG8 = [[0, 1, 2, 3, 4, 5, 6, 7]]

def own_chunks(j):
    return (j, NCH - 1 - j)

def chunk_pos(c):
    return (c, 0) if c < 4 else (7 - c, 1)



class Prog:
    NDMASEM = 8

    def __init__(self):
        self.ops = []
        self.last_write = {}
        self.readers = {}
        self.fence_deps = []

    def op(self, eng, fn, reads=(), writes=(), kind='c', accum=False, tag=''):
        deps = set(self.fence_deps)
        for k in reads:
            deps.update(self.last_write.get(k, ()))
        for k in writes:
            if not accum:
                deps.update(self.last_write.get(k, ()))
            deps.update(self.readers.get(k, ()))
        i = len(self.ops)
        deps.discard(i)
        self.ops.append(dict(eng=eng, fn=fn, deps=deps, kind=kind, tag=tag,
                             inc=False))
        for k in reads:
            self.readers.setdefault(k, []).append(i)
        for k in writes:
            if accum and k in self.last_write:
                self.last_write[k].append(i)
            else:
                self.last_write[k] = [i]
                self.readers[k] = []
        return i

    def fence(self):
        last = {}
        for i, o in enumerate(self.ops):
            last[(o['eng'], o['kind'])] = i
        self.fence_deps = sorted(last.values())

    def emit(self, nc, block):
        ops = self.ops
        for o in ops:
            for d in o['deps']:
                ops[d]['inc'] = True
        engs = sorted({(o['eng'], o['kind']) for o in ops})
        sems = {}
        self._sem_cms = []
        for e, k in engs:
            if k == 'c':
                cm = nc.semaphore(f"sem_{e}_{k}")
                s = cm.__enter__(); self._sem_cms.append(cm)
                sems[(e, k)] = [s]
            elif k == 'd':
                lst = []
                for n in range(self.NDMASEM):
                    cm = nc.semaphore(f"sem_{e}_{k}{n}")
                    lst.append(cm.__enter__()); self._sem_cms.append(cm)
                sems[(e, k)] = lst
            else:
                lst = []
                for n in range(2):
                    cm = nc.semaphore(f"sem_{e}_cc{n}")
                    lst.append(cm.__enter__()); self._sem_cms.append(cm)
                sems[(e, k)] = lst
        counts = {}
        for i, o in enumerate(ops):
            key = (o['eng'], o['kind'])
            pool = sems[key]
            n = counts.get(key, 0)
            counts[key] = n + 1
            if o['kind'] == 'c':
                o['sem'] = pool[0]
                o['val'] = None
            else:
                o['sem'] = pool[n % len(pool)]
                o['use'] = n // len(pool)
                o['step'] = 16 if o['kind'] == 'd' else 1
        ccount = {}
        for o in ops:
            if o['kind'] == 'c':
                key = (o['eng'], o['kind'])
                c = ccount.get(key, 0)
                if o['inc']:
                    c += 1
                ccount[key] = c
                o['val'] = c
        for o in ops:
            if o['kind'] != 'c':
                o['inc'] = True
                o['val'] = (o['use'] + 1) * o['step']
        stream_clock = {}
        for i, o in enumerate(ops):
            skey = o['eng']
            clk = stream_clock.setdefault(skey, {})
            waits = []
            for d in sorted(o['deps']):
                od = ops[d]
                sem, val = od['sem'], od['val']
                if od['kind'] == 'c' and val == 0:
                    continue
                if clk.get(id(sem), -1) >= val:
                    continue
                waits.append((sem, val, d))
                clk[id(sem)] = val
            if o['kind'] != 'c' and o['use'] > 0:
                prev_val = o['use'] * o['step']
                if clk.get(id(o['sem']), -1) < prev_val:
                    waits.append((o['sem'], prev_val, -1))
                    clk[id(o['sem'])] = prev_val
            o['waits'] = waits
            for (_, _, d) in waits:
                if d >= 0:
                    pclk = ops[d].get('clock')
                    if pclk:
                        for sid, v in pclk.items():
                            if clk.get(sid, -1) < v:
                                clk[sid] = v
            o['clock'] = dict(clk)
        by_eng = {}
        for o in ops:
            by_eng.setdefault(o['eng'], []).append(o)
        emap = {'t': block.tensor, 'v': block.vector, 's': block.scalar,
                'g': block.gpsimd, 'q': block.sync}
        for ename, elist in by_eng.items():
            def make(elist_):
                def body(eng):
                    for o in elist_:
                        for (sem, val, _) in o['waits']:
                            eng.wait_ge(sem, val)
                        instr = o['fn'](eng)
                        if o['inc']:
                            if o['kind'] == 'c':
                                instr.then_inc(o['sem'], 1)
                            else:
                                instr.then_inc(o['sem'], 16 if o['kind'] == 'd' else 1)
                return body
            emap[ename](make(elist))



def stage_inputs(inp):
    f32 = np.float32
    asf = lambda a: np.asarray(a, f32)

    x = asf(inp['L0_attn_x'])
    xT = [np.ascontiguousarray(x[g].T) for g in range(B)]

    winT = np.ascontiguousarray(asf(inp['L0_attn_W_in']).T)
    wxT = np.ascontiguousarray(asf(inp['L0_attn_W_x']).T)
    wdt = asf(inp['L0_attn_W_dt'])
    woutT = np.ascontiguousarray(asf(inp['L0_attn_W_out']).T)
    conv_w = asf(inp['L0_attn_conv_w'])
    conv_b = asf(inp['L0_attn_conv_b'])
    b_dt = asf(inp['L0_attn_b_dt'])
    A_log = asf(inp['L0_attn_A_log'])
    Dvec = asf(inp['L0_attn_D'])

    g0T = np.ascontiguousarray(asf(inp['L0_ffn_gate_w']).T)
    u0T = np.ascontiguousarray(asf(inp['L0_ffn_up_w']).T)
    d0T = np.ascontiguousarray(asf(inp['L0_ffn_down_w']).T)
    g0s = asf(inp['L0_ffn_gate_s'])
    u0s = asf(inp['L0_ffn_up_s'])
    d0s = asf(inp['L0_ffn_down_s'])

    wq = asf(inp['L1_attn_Wq'])
    wk = asf(inp['L1_attn_Wk'])
    wv = asf(inp['L1_attn_Wv'])
    wo = asf(inp['L1_attn_Wo'])

    g1T = np.ascontiguousarray(asf(inp['L1_ffn_w_gate']).T)
    u1T = np.ascontiguousarray(asf(inp['L1_ffn_w_up']).T)
    d1T = np.ascontiguousarray(asf(inp['L1_ffn_w_down']).T)
    lnw = asf(inp['L1_ffn_ln_weight'])

    def lhsT_stage(wT, K, O):
        t = wT.reshape(K // 128, 128, O // 128, 128)
        return np.ascontiguousarray(t.transpose(2, 1, 0, 3).reshape(O // 128, 128, K))

    def win_stage(j):
        cols = np.concatenate([winT[:, j * DSL:(j + 1) * DSL],
                               winT[:, DI + j * DSL:DI + (j + 1) * DSL]], axis=1)
        return np.ascontiguousarray(cols.reshape(16, 128, 2048))

    half = HD // 2
    inv_freq = (1.0 / (10000.0 ** (np.arange(half, dtype=np.float64) * 2.0 / HD)))
    ang = np.arange(T, dtype=np.float64)[:, None] * inv_freq[None, :]
    cosT = np.cos(ang).astype(f32).T
    sinT = np.sin(ang).astype(f32).T
    rc = np.concatenate([cosT, cosT, cosT, cosT], 0)
    rs = np.concatenate([-sinT, sinT, -sinT, sinT], 0)

    perm = np.zeros((128, 128), f32)
    for m in range(128):
        perm[m ^ 32, m] = 1.0
    ident = np.eye(128, dtype=f32)
    selb = np.zeros((32, 32, 128), f32)
    for r in range(32):
        selb[r, r, :] = 1.0
    selb = np.ascontiguousarray(selb.transpose(1, 0, 2).reshape(32, 32 * 128))
    sel4 = np.zeros((4, 4, 128), f32)
    for r in range(4):
        sel4[r, r, :] = 1.0
    sel4 = np.ascontiguousarray(sel4.transpose(1, 0, 2).reshape(4, 4 * 128))

    kk = np.arange(128)[:, None]
    qq = np.arange(TC)[None, :]
    tri = np.stack([(128 * o + kk <= qq).astype(f32) for o in range(4)])

    in_maps = []
    for c in range(NCORE):
        g, j = c // 4, c % 4
        sl = slice(j * DSL, (j + 1) * DSL)
        ch0, ch1 = own_chunks(j)
        own_t = np.r_[ch0 * TC:(ch0 + 1) * TC, ch1 * TC:(ch1 + 1) * TC]

        qcols = []
        for t4 in range(4):
            qcols.append(np.arange((4 * ch0 + t4) * 64, (4 * ch0 + t4 + 1) * 64))
            qcols.append(np.arange((4 * ch1 + t4) * 64, (4 * ch1 + t4 + 1) * 64))
        qcols = np.concatenate(qcols)
        kvcols = np.r_[ch0 * 64:(ch0 + 1) * 64, ch1 * 64:(ch1 + 1) * 64]
        wq_sl = np.ascontiguousarray(wq[:, qcols])
        wk_sl = np.ascontiguousarray(wk[:, kvcols])
        wv_sl = np.ascontiguousarray(wv[:, kvcols])

        def q_stage(wT, OTs):
            t = lhsT_stage(wT, wT.shape[0], wT.shape[1])[OTs]
            return np.ascontiguousarray(t.transpose(1, 0, 2).reshape(128, -1))

        xo = x[g][own_t].T
        m = {
            'xT': xT[g],
            'xown': np.ascontiguousarray(xo.reshape(16, 128, 1024).transpose(1, 0, 2)),
            'winT': win_stage(j),
            'wx': np.ascontiguousarray(wxT[sl].reshape(8, 128, 160).transpose(1, 0, 2)),
            'wdt': np.ascontiguousarray(
                wdt[sl].reshape(8, 128, DTR).transpose(2, 0, 1)),
            'convw': np.ascontiguousarray(conv_w[sl].reshape(8, 128, 4).transpose(1, 0, 2)),
            'convb': np.ascontiguousarray(conv_b[sl].reshape(8, 128).T)[:, :, None],
            'bdt': np.ascontiguousarray(b_dt[sl].reshape(8, 128).T)[:, :, None],
            'alog': np.ascontiguousarray(A_log[sl].reshape(8, 128, DS).transpose(1, 0, 2)),
            'dvec': np.ascontiguousarray(Dvec[sl].reshape(8, 128).T)[:, :, None],
            'wout': lhsT_stage(woutT, DI, D),
            'g0': lhsT_stage(g0T, D, I0),
            'u0': lhsT_stage(u0T, D, I0),
            'd0': lhsT_stage(d0T, I0, D),
            'g0s': np.ascontiguousarray(g0s.reshape(44, 128).T),
            'u0s': np.ascontiguousarray(u0s.reshape(44, 128).T),
            'd0s': np.ascontiguousarray(d0s.reshape(16, 128).T),
            'wqh': lhsT_stage(wq_sl, D, 512),
            'wkh': lhsT_stage(wk_sl, D, 128)[0],
            'wo': lhsT_stage(wo, D, D),
            'wvh': np.ascontiguousarray(
                wv_sl.reshape(16, 128, 128).transpose(1, 0, 2)),
            'qg': q_stage(g1T, list(range(8 * c, 8 * c + 8))),
            'qu': q_stage(u1T, list(range(8 * c, 8 * c + 8))),
            'qd': q_stage(d1T, [2 * c, 2 * c + 1]),
            'lnw': np.ascontiguousarray(lnw.reshape(64, 128).T),
            'rc': rc,
            'rsn': rs,
            'perm': perm,
            'ident': ident,
            'selb': selb,
            'tri': tri,
            'sel4': sel4,
        }
        in_maps.append(m)
    return in_maps



def build_program(upto='full'):
    nc = bass.Bass("TRN2", target_bir_lowering=False, debug=False)
    p = Prog()
    OP = p.op

    def din(name, shape):
        return nc.dram_tensor(name, list(shape), F32, kind="ExternalInput").ap()

    xT = din('xT', [D, T])
    xown_d = din('xown', [128, 16, 1024])
    winT = din('winT', [16, 128, 2048])
    wx = din('wx', [128, 8, 160])
    wdt = din('wdt', [128, 8, 128])
    convw = din('convw', [128, 8, 4])
    convb = din('convb', [128, 8, 1])
    bdt = din('bdt', [128, 8, 1])
    alog = din('alog', [128, 8, DS])
    dvec = din('dvec', [128, 8, 1])
    wout_d = din('wout', [16, 128, 4096])
    g0_d = din('g0', [44, 128, 2048]); u0_d = din('u0', [44, 128, 2048])
    d0_d = din('d0', [16, 128, 5632])
    g0s_d = din('g0s', [128, 44]); u0s_d = din('u0s', [128, 44]); d0s_d = din('d0s', [128, 16])
    wqh_d = din('wqh', [4, 128, 2048]); wkh_d = din('wkh', [128, 2048])
    wo_d = din('wo', [16, 128, 2048]); wvh_d = din('wvh', [128, 16, 128])
    qg_d = din('qg', [128, 8 * 2048]); qu_d = din('qu', [128, 8 * 2048])
    qd_d = din('qd', [128, 2 * 8192])
    lnw_d = din('lnw', [128, 64])
    rc_d = din('rc', [128, T]); rs_d = din('rsn', [128, T])
    perm_d = din('perm', [128, 128])
    selb_d = din('selb', [32, 32 * 128])
    ident_d = din('ident', [128, 128])
    tri_d = din('tri', [4, 128, TC])
    sel4_d = din('sel4', [4, 4 * 128])

    ar_in = [nc.dram_tensor(f'ar_in{i}', [160, TC], BF16).ap() for i in range(NCH)]
    ar_out = [nc.dram_tensor(f'ar_out{i}', [160, TC], BF16).ap() for i in range(NCH)]
    yag_in = [nc.dram_tensor(f'yag_in{i}', [DSL, TC], BF16).ap() for i in range(NCH)]
    yag_all = nc.dram_tensor('yag_all', [NCH, DI, TC], BF16).ap()
    qs_in = nc.dram_tensor('qs_in', [1, 4], F32).ap()
    qs_out = nc.dram_tensor('qs_out', [1, 4], F32).ap()
    qg_i = nc.dram_tensor('qg_i', [128, 16384], BF16).ap()
    qu_i = nc.dram_tensor('qu_i', [128, 16384], BF16).ap()
    qd_i = nc.dram_tensor('qd_i', [128, 16384], BF16).ap()
    qg_o = nc.dram_tensor('qg_o', [1024, 16384], BF16).ap()
    qu_o = nc.dram_tensor('qu_o', [1024, 16384], BF16).ap()
    qd_o = nc.dram_tensor('qd_o', [1024, 16384], BF16).ap()
    sag_i = nc.dram_tensor('sag_i', [D, 1024], BF16).ap()
    sag_o = nc.dram_tensor('sag_o', [4 * D, 1024], BF16).ap()

    out_t = nc.dram_tensor('out', [1024, D], F32, kind="ExternalOutput").ap()
    dbg = None
    if upto == 'F1d':
        dbg = nc.dram_tensor('dbg', [128, 16, TC], BF16, kind="ExternalOutput").ap()
        dbg2 = nc.dram_tensor('dbg2', [128, 16, TC], BF16, kind="ExternalOutput").ap()
        dbg3 = nc.dram_tensor('dbg3', [128, 16, TC], BF16, kind="ExternalOutput").ap()
        dbg4 = nc.dram_tensor('dbg4', [128, 2048], F32, kind="ExternalOutput").ap()
        dbg5 = nc.dram_tensor('dbg5', [128, 2048], BF16, kind="ExternalOutput").ap()
    if upto == 'M':
        dbg = nc.dram_tensor('dbg', [DI, 1024], BF16, kind="ExternalOutput").ap()
    elif upto not in ('full', 'F1d'):
        dbg = nc.dram_tensor('dbg', [128, 16, 1024], BF16, kind="ExternalOutput").ap()
        if upto == 'A2':
            dbgq = nc.dram_tensor('dbgq', [128, 16, 1024], BF16, kind="ExternalOutput").ap()
            dbgk = nc.dram_tensor('dbgk', [128, 16, 1024], BF16, kind="ExternalOutput").ap()
            dbgv = nc.dram_tensor('dbgv', [128, 32, 520], BF16, kind="ExternalOutput").ap()

    stack = contextlib.ExitStack()

    def sb(name, shape, dt):
        return stack.enter_context(nc.sbuf_tensor(name, list(shape), dt))

    open_arenas = []

    def mk_arena():
        st = contextlib.ExitStack()
        def sba(name, shape, dt):
            return st.enter_context(nc.sbuf_tensor(name, list(shape), dt))
        def psa(name, shape):
            return st.enter_context(nc.psum_tensor(name, list(shape), F32))
        class A:
            def close(self):
                if st in open_arenas:
                    open_arenas.remove(st)
                st.close()
        open_arenas.append(st)
        return A(), sba, psa

    def dma(q, out, in_, reads, writes, tag='', accum=False):
        return OP(q, lambda e, o=out, i=in_: e.dma_start(out=o, in_=i),
                  reads=reads, writes=writes, kind='d', tag=tag, accum=accum)

    identf = sb('identf', [128, 128], F32)
    identb = sb('identb', [128, 128], BF16)
    onesb = sb('onesb', [1, 128], BF16)
    onescolb = sb('onescolb', [128, 1], BF16)
    onesrowF = sb('onesrowF', [1, 128], F32)
    ones127 = sb('ones127', [1, 128], F32)
    onescolF = sb('onescolF', [128, 1], F32)
    swcol = sb('swcol', [128, 4], F32)
    mcol = sb('mcol', [128, 4], F32)
    qrows = sb('qrows', [1, 16], F32)
    epsc = sb('epsc', [1, 2], F32)

    dma('q', identf[:, :], ident_d[:, :], ['identd'], ['identf'])
    OP('v', lambda e: e.tensor_copy(out=identb[:, :], in_=identf[:, :]),
       ['identf'], ['identb'])
    OP('v', lambda e: e.memset(onesb[:, :], 1.0), [], ['onesb'])
    OP('v', lambda e: e.memset(onescolb[:, :], 1.0), [], ['onescolb'])
    OP('v', lambda e: e.memset(onesrowF[:, :], 1.0), [], ['onesrowF'])
    OP('v', lambda e: e.memset(ones127[:, :], 127.0), [], ['ones127'])
    OP('v', lambda e: e.memset(onescolF[:, :], 1.0), [], ['onescolF'])
    OP('v', lambda e: e.memset(epsc[0:1, 0:1], EPS_LN), [], ['epsc'], accum=True)
    OP('v', lambda e: e.memset(epsc[0:1, 1:2], RMS_EPS), [], ['epsc'], accum=True)

    a1, sb1, ps1 = mk_arena()

    qbuf = [sb1('qbuf0', [128, 512], F32)] * 2
    qout = sb1('qout', [128, 512], BF16)
    qint = sb1('qint', [128, 512], F32)
    qred = sb1('qred', [128, 1], F32)
    qacc = sb1('qacc', [128, 4], F32)

    winT_sb = sb1('winT_sb', [128, 16, 2048], BF16)
    wdt_sb = sb1('wdt_sb', [128, 8, 128], BF16)
    wx_sb = sb1('wx_sb', [128, 8, 160], BF16)
    convw_sb = sb1('convw_sb', [128, 8, 4], F32)
    convb_sb = sb1('convb_sb', [128, 8, 1], F32)
    bdt_sb = sb1('bdt_sb', [128, 8, 1], F32)
    dvec_sb = sb1('dvec_sb', [128, 8, 1], F32)
    aneg = sb1('aneg', [128, 8, DS], F32)
    state = sb1('state', [128, 8, DS], F32)
    xbf = sb1('xbf', [128, 16, TC], BF16)
    xse = sb1('xse', [128, 8, TC + 3], BF16)
    tails = sb1('tails', [128, 8, 3], BF16)
    zbuf = [sb1(f'zb{i}', [128, 8, TC], BF16) for i in range(2)]
    ubuf = [sb1(f'ub{i}', [128, 8, TC], BF16) for i in range(2)]
    dtbuf = [sb1(f'dtb{i}', [128, 8, TC], BF16) for i in range(2)]
    vtile = [sb1('vt0', [128, TC], BF16)] * 2
    dtrb = [sb1('dtrb0', [128, TC], BF16)] * 2
    bcb = [sb1('bcb0', [32, TC], BF16)] * 2
    bca = sb1('bca', [128, 2 * DS, TC], BF16)
    selb_sb = sb1('selb_sb', [32, 32, 128], BF16)
    xdl_dt = sb1('xdl_dt', [128, TC], BF16)
    xdl_bc = sb1('xdl_bc', [32, TC], BF16)
    convt = [sb1('convt0', [128, TC], BF16)] * 2
    dAb = [sb1(f'dAb{i}', [128, TC], BF16) for i in range(3)]
    dBub = [sb1(f'dBub{i}', [128, TC], BF16) for i in range(3)]
    hb = [sb1(f'hb{i}', [128, TC], BF16) for i in range(2)]
    hcb = [sb1(f'hcb{i}', [128, TC], BF16) for i in range(2)]
    ygb = sb1('ygb', [128, 1, TC], BF16)
    yev = sb1('yev', [128, TC], BF16)

    ps_front = [ps1('ps_front0', [128, 512]), ps1('ps_front1', [128, 512])]
    ps_bc = [ps1('ps_bc0', [128, 512]), ps1('ps_bc1', [128, 512])]
    ps_y = [ps1('ps_y0', [128, 512]), ps1('ps_y1', [128, 512])]
    ps_misc = [ps1('ps_misc0', [128, 512]), ps1('ps_misc1', [128, 512])]

    OP('g', lambda e: e.memset(qacc[:, :], 0.0), [], ['qacc'])
    qsrcs = [(qg_d, qg_i, 0), (qu_d, qu_i, 1), (qd_d, qd_i, 2)]
    NQT = 32
    for (src, _, m) in qsrcs:
        for t in range(NQT):
            b = t % 2
            dma('q', qbuf[b][:, :], src[:, t * 512:(t + 1) * 512],
                [f'q_src{m}'], ['qbuf0'], tag=f'qld{m}_{t}')
            OP('s', lambda e, b=b: e.activation(
                out=qout[:, :], in_=qbuf[b][:, :], func=AF.Abs,
                accum_out=qred[:, 0:1]),
               ['qbuf0'], ['qred', 'qout'], tag=f'qsum{m}_{t}')
            OP('g', lambda e, m=m: e.tensor_tensor(
                out=qacc[:, m:m + 1], in0=qacc[:, m:m + 1], in1=qred[:, 0:1],
                op=ALU.add),
               ['qred', 'qacc'], ['qacc'], accum=True, tag=f'qadd{m}_{t}')
    OP('t', lambda e: e.matmul(ps_misc[0][0:1, 0:4], onescolF[:, :], qacc[:, :],
                               start=True, stop=True),
       ['onescolF', 'qacc'], ['psm0'], tag='qcross')
    OP('v', lambda e: e.tensor_copy(out=qrows[:, 0:4], in_=ps_misc[0][0:1, 0:4]),
       ['psm0'], ['qrows03'], tag='qrow')
    dma('q', qs_in[:, :], qrows[:, 0:4], ['qrows03'], ['qs_in_d'])
    OP('g', lambda e: e.collective_compute(
        "AllReduce", ALU.add, replica_groups=RG8,
        ins=[qs_in.opt()], outs=[qs_out.opt()]),
       ['qs_in_d'], ['qs_out_d'], kind='cc', tag='qAR')
    dma('q', qrows[:, 4:8], qs_out[:, :], ['qs_out_d'], ['qrows47'])
    OP('v', lambda e: e.tensor_scalar(
        out=qrows[:, 8:12], in0=qrows[:, 4:8], scalar1=1.0 / 16777216.0,
        scalar2=EPS_Q, op0=ALU.mult, op1=ALU.max),
       ['qrows47'], ['qrows8'], tag='qmean')
    OP('v', lambda e: e.reciprocal(out=qrows[:, 12:16], in_=qrows[:, 8:12]),
       ['qrows8'], ['qrows12'], tag='qrecip')
    OP('t', lambda e: e.matmul(ps_misc[0][:, 4:8], onesrowF[:, :], qrows[:, 8:12],
                               start=True, stop=True),
       ['onesrowF', 'qrows8'], ['psm0'], tag='qbc1')
    OP('v', lambda e: e.tensor_copy(out=mcol[:, :], in_=ps_misc[0][:, 4:8]),
       ['psm0'], ['mcol'], tag='qbc1c')
    OP('t', lambda e: e.matmul(ps_misc[0][:, 8:12], onesrowF[:, :], qrows[:, 12:16],
                               start=True, stop=True),
       ['onesrowF', 'qrows12'], ['psm0'], tag='qbc2')
    OP('v', lambda e: e.tensor_copy(out=swcol[:, :], in_=ps_misc[0][:, 8:12]),
       ['psm0'], ['swcol'], tag='qbc2c')
    for (src, dst, m) in qsrcs:
        for t in range(NQT):
            b = t % 2
            dma('q', qbuf[b][:, :], src[:, t * 512:(t + 1) * 512],
                [f'q_src{m}'], ['qbuf0'], tag=f'qld2_{m}_{t}')
            OP('g', lambda e, b=b, m=m: e.tensor_scalar(
                out=qint[:, :], in0=qbuf[b][:, :], scalar1=swcol[:, m:m + 1],
                scalar2=None, op0=ALU.mult),
               ['qbuf0', 'swcol'], ['qint'], tag=f'qmul{m}_{t}')
            OP('g', lambda e: e.tensor_scalar(
                out=qint[:, :], in0=qint[:, :], scalar1=MAGIC, scalar2=MAGIC,
                op0=ALU.add, op1=ALU.subtract),
               ['qint'], ['qint'], accum=True, tag=f'qrnd{m}_{t}')
            OP('g', lambda e: e.tensor_scalar(
                out=qout[:, :], in0=qint[:, :], scalar1=1.0, scalar2=-1.0,
                op0=ALU.min, op1=ALU.max),
               ['qint'], ['qout'], tag=f'qclip{m}_{t}')
            dma('q', dst[:, t * 512:(t + 1) * 512], qout[:, :],
                ['qout'], [f'qst{m}'], tag=f'qst{m}_{t}', accum=(t > 0))

    for kc in range(16):
        dma('g', winT_sb[:, kc, :], winT[kc, :, :], ['winT_d'], [f'winsb{kc}'],
            tag=f'ldwin{kc}')
    dma('g', wdt_sb[:, :, :], wdt[:, :, :], ['wdt_d'], ['wdtsb'])
    dma('g', wx_sb[:, :, :], wx[:, :, :], ['wx_d'], ['wxsb'])
    dma('g', selb_sb[:, :, :],
        selb_d.rearrange("p (r c) -> p r c", c=128), ['selb_d'], ['selsb'])
    dma('q', convw_sb[:, :, :], convw[:, :, :], ['cw_d'], ['cwsb'])
    dma('q', convb_sb[:, :, :], convb[:, :, :], ['cb_d'], ['cbsb'])
    dma('q', bdt_sb[:, :, :], bdt[:, :, :], ['bdt_d'], ['bdtsb'])
    dma('q', dvec_sb[:, :, :], dvec[:, :, :], ['dv_d'], ['dvsb'])
    dma('q', aneg[:, :, :], alog[:, :, :], ['al_d'], ['alsb'])
    OP('s', lambda e: e.activation(out=aneg[:, :, :], in_=aneg[:, :, :], func=AF.Exp),
       ['alsb'], ['anegp'], accum=True)
    OP('v', lambda e: e.tensor_scalar(out=aneg[:, :, :], in0=aneg[:, :, :],
                                      scalar1=-1.0, scalar2=None, op0=ALU.mult),
       ['anegp'], ['aneg'], accum=True)
    OP('v', lambda e: e.memset(state[:, :, :], 0.0), [], ['state'])
    OP('v', lambda e: e.memset(xse[:, :, 0:3], 0.0), [], ['xse_h'])

    def front(tc):
        b = tc % 2
        if tc > 0:
            OP('v', lambda e: e.tensor_copy(
                out=tails[:, :, :], in_=xse[:, :, TC:TC + 3]),
               ['xse_x'], ['tails'], tag=f'tail{tc}')
        dma('g', xbf[:, :, :],
            xT.rearrange("(kc p) t -> p kc t", p=128)[:, :, tc * TC:(tc + 1) * TC],
            ['xT_d'], ['xbf'], tag=f'ldx{tc}')
        for o in range(16):
            bank = ps_front[o % 2]
            pk = f'psf{o % 2}'
            for k in range(16):
                OP('t', lambda e, o=o, k=k, bank=bank: e.matmul(
                    bank[:, :], winT_sb[:, k, o * 128:(o + 1) * 128],
                    xbf[:, k, :], start=(k == 0), stop=(k == 15)),
                   reads=[f'winsb{k}', 'xbf'], writes=[pk], kind='c',
                   accum=(k > 0), tag=f'proj{tc}_{o}')
            if o < 8:
                OP('v', lambda e, o=o, bank=bank: e.tensor_copy(
                    out=xse[:, o, 3:3 + TC], in_=bank[:, :]),
                   [pk, 'tails'], ['xse_x'], accum=(tc > 0 or o > 0), tag=f'evxs{tc}_{o}')
            else:
                OP('s', lambda e, o=o, bank=bank, b=b: e.activation(
                    out=zbuf[b][:, o - 8, :], in_=bank[:, :], func=AF.Silu),
                   [pk], [f'zb{b}_{o-8}'], tag=f'evz{tc}_{o}')
        if tc > 0:
            OP('v', lambda e: e.tensor_copy(
                out=xse[:, :, 0:3], in_=tails[:, :, :]),
               ['tails', 'xse_x'], ['xse_h'], accum=True, tag=f'head{tc}')
        for d in range(8):
            OP('v', lambda e, d=d: e.tensor_scalar(
                out=convt[0][:, :], in0=xse[:, d, 0:TC],
                scalar1=convw_sb[:, d, 0:1], scalar2=None, op0=ALU.mult),
               ['xse_x', 'xse_h', 'cwsb'], ['cvt0'], tag=f'cv{tc}_{d}_0')
            for k in (1, 2, 3):
                OP('v', lambda e, d=d, k=k: e.scalar_tensor_tensor(
                    out=convt[0][:, :], in0=xse[:, d, k:k + TC],
                    scalar=convw_sb[:, d, k:k + 1], in1=convt[0][:, :],
                    op0=ALU.mult, op1=ALU.add),
                   ['xse_x', 'xse_h', 'cvt0', 'cwsb'], ['cvt0'], accum=True,
                   tag=f'cv{tc}_{d}_{k}')
            OP('s', lambda e, d=d, b=b: e.activation(
                out=ubuf[b][:, d, :], in_=convt[0][:, :], func=AF.Silu,
                bias=convb_sb[:, d, 0:1]),
               ['cvt0', 'cbsb'], [f'ub{b}_{d}'], tag=f'silu_u{tc}_{d}')
        for k in range(8):
            OP('t', lambda e, k=k, b=b: e.matmul(
                ps_misc[0][:, :], wx_sb[:, k, 0:128], ubuf[b][:, k, :],
                start=(k == 0), stop=(k == 7)),
               [f'ub{b}_{k}', 'wxsb'], ['psm0'], kind='c', accum=(k > 0), tag=f'xdl{tc}')
        for k in range(8):
            OP('t', lambda e, k=k, b=b: e.matmul(
                ps_misc[1][0:32, :], wx_sb[:, k, 128:160], ubuf[b][:, k, :],
                start=(k == 0), stop=(k == 7)),
               [f'ub{b}_{k}', 'wxsb'], ['psm1'], kind='c', accum=(k > 0), tag=f'xdlbc{tc}')
        OP('v', lambda e: e.tensor_copy(out=xdl_dt[:, :], in_=ps_misc[0][:, :]),
           ['psm0'], ['xdl_dt'], tag=f'evxdl{tc}')
        OP('v', lambda e: e.tensor_copy(out=xdl_bc[:, :], in_=ps_misc[1][0:32, :]),
           ['psm1'], ['xdl_bc'], tag=f'evxdlbc{tc}')
        dma('q', ar_in[tc][0:128, :], xdl_dt[:, :], ['xdl_dt'], [f'arin{tc}a'])
        dma('q', ar_in[tc][128:160, :], xdl_bc[:, :], ['xdl_bc'], [f'arin{tc}b'])
        OP('g', lambda e, tc=tc: e.collective_compute(
            "AllReduce", ALU.add, replica_groups=RG4,
            ins=[ar_in[tc].opt()], outs=[ar_out[tc].opt()]),
           [f'arin{tc}a', f'arin{tc}b'], [f'arout{tc}'], kind='cc', tag=f'AR{tc}')
        dma('q', dtrb[b][:, :], ar_out[tc][0:128, :], [f'arout{tc}'], ['dtrb0', 'dtrb1'])
        dma('q', bcb[b][:, :], ar_out[tc][128:160, :], [f'arout{tc}'], ['bcb0', 'bcb1'])
        for o in range(8):
            bank = ps_front[o % 2]
            OP('t', lambda e, o=o, bank=bank, b=b: e.matmul(
                bank[:, :], wdt_sb[:, o, :], dtrb[b][:, :], start=True, stop=True),
               [f'dtrb{b}', 'wdtsb'], [f'psf{o % 2}'], kind='c', tag=f'dtmm{tc}_{o}')
            OP('s', lambda e, o=o, bank=bank: e.activation(
                out=convt[0][:, :], in_=bank[:, :], func=AF.Exp,
                bias=bdt_sb[:, o, 0:1]),
               [f'psf{o % 2}', 'bdtsb', 'cvt0'], ['cvt0'], accum=True,
               tag=f'spe{tc}_{o}')
            OP('s', lambda e, o=o, b=b: e.activation(
                out=dtbuf[b][:, o, :], in_=convt[0][:, :], func=AF.Ln,
                bias=onescolF[:, 0:1]),
               ['cvt0', 'onescolF'], [f'dtb{b}_{o}'], tag=f'softp{tc}_{o}')

    def scanpart(tc):
        b = tc % 2
        for srow in range(32):
            bank = ps_bc[srow % 2]
            OP('t', lambda e, srow=srow, bank=bank, b=b: e.matmul(
                bank[:, :], selb_sb[:, srow, :], bcb[b][:, :],
                start=True, stop=True),
               [f'bcb{b}', 'selsb'], [f'psbc{srow % 2}'], kind='c', tag=f'bc{tc}_{srow}')
            OP('s', lambda e, srow=srow, bank=bank: e.copy(
                out=bca[:, srow, :], in_=bank[:, :]),
               [f'psbc{srow % 2}'], [f'bca_{srow}'], tag=f'bcev{tc}_{srow}')
        for d in range(8):
            OP('v', lambda e, d=d, b=b: e.tensor_tensor(
                out=vtile[d % 2][:, :], in0=dtbuf[b][:, d, :], in1=ubuf[b][:, d, :],
                op=ALU.mult),
               [f'dtb{b}_{d}', f'ub{b}_{d}'], ['vt0', 'vt1'], tag=f'vmul{tc}_{d}')
            ybank = ps_y[d % 2]
            ypk = f'psy{d % 2}'
            for s in range(DS):
                i = s % 3
                OP('s', lambda e, d=d, s=s, i=i, b=b: e.activation(
                    out=dAb[i][:, :], in_=dtbuf[b][:, d, :], func=AF.Exp,
                    scale=aneg[:, d, s:s + 1]),
                   [f'dtb{b}_{d}', 'aneg'], [f'dA{i}'], tag=f'dA{tc}_{d}_{s}')
                OP('g', lambda e, d=d, s=s, i=i: e.tensor_tensor(
                    out=dBub[i][:, :], in0=vtile[d % 2][:, :],
                    in1=bca[:, s, :], op=ALU.mult),
                   [f'vt{d % 2}', f'bca_{s}'], [f'dBu{i}'], tag=f'dBu{tc}_{d}_{s}')
                ih = s % 2
                OP('v', lambda e, d=d, s=s, i=i, ih=ih: e.tensor_tensor_scan(
                    out=hb[ih][:, :], data0=dAb[i][:, :], data1=dBub[i][:, :],
                    initial=state[:, d, s:s + 1], op0=ALU.mult, op1=ALU.add),
                   [f'dA{i}', f'dBu{i}', 'state'], [f'hb{ih}'], tag=f'scan{tc}_{d}_{s}')
                OP('s', lambda e, d=d, s=s, ih=ih: e.copy(
                    out=state[:, d, s:s + 1], in_=hb[ih][:, TC - 1:TC]),
                   [f'hb{ih}'], ['state'], accum=True, tag=f'stup{tc}_{d}_{s}')
                eng_hc = 'v' if s % 3 else 'g'
                i2 = s % 2
                OP(eng_hc, lambda e, d=d, s=s, ih=ih, i2=i2: e.tensor_tensor(
                    out=hcb[i2][:, :], in0=hb[ih][:, :], in1=bca[:, DS + s, :],
                    op=ALU.mult),
                   [f'hb{ih}', f'bca_{DS+s}'], [f'hcb{i2}'], tag=f'ymul{tc}_{d}_{s}')
                OP('t', lambda e, d=d, s=s, i2=i2, ybank=ybank: e.matmul(
                    ybank[:, :], identb[:, :], hcb[i2][:, :],
                    start=(s == 0), stop=(s == DS - 1)),
                   [f'hcb{i2}', 'identb'], [ypk], kind='c', accum=(s > 0),
                   tag=f'yacc{tc}_{d}_{s}')
            OP('v', lambda e, d=d, b=b, ybank=ybank: e.scalar_tensor_tensor(
                out=yev[:, :], in0=ubuf[b][:, d, :], scalar=dvec_sb[:, d, 0:1],
                in1=ybank[:, :], op0=ALU.mult, op1=ALU.add),
               [f'ub{b}_{d}', ypk, 'dvsb'], ['yev'], tag=f'yg1{tc}_{d}')
            OP('v', lambda e, d=d, b=b: e.tensor_tensor(
                out=ygb[:, 0, :], in0=yev[:, :], in1=zbuf[b][:, d, :],
                op=ALU.mult),
               ['yev', f'zb{b}_{d}'], ['ygb_0'], tag=f'yg2{tc}_{d}')
            dma('q', yag_in[tc][d * 128:(d + 1) * 128, :], ygb[:, 0, :],
                ['ygb_0'], [f'ygst{tc}_{d}'], tag=f'styg{tc}_{d}')
        OP('g', lambda e, tc=tc: e.collective_compute(
            "AllGather", ALU.bypass, replica_groups=RG4,
            ins=[yag_in[tc].opt()], outs=[yag_all[tc, :, :].opt()]),
           [f'ygst{tc}_{hb2}' for hb2 in range(8)], [f'yago{tc}'], kind='cc',
           tag=f'AGy{tc}')

    front(0)
    for tc in range(NCH):
        if tc + 1 < NCH:
            front(tc + 1)
        scanpart(tc)

    for (nm, i_t, o_t, mi) in (('g', qg_i, qg_o, 0), ('u', qu_i, qu_o, 1),
                               ('d', qd_i, qd_o, 2)):
        OP('g', lambda e, i_t=i_t, o_t=o_t: e.collective_compute(
            "AllGather", ALU.bypass, replica_groups=RG8,
            ins=[i_t.opt()], outs=[o_t.opt()]),
           [f'qst{mi}'], [f'qAG{nm}'], kind='cc', tag=f'qAG{nm}')

    def finish(last_keys):
        OP('v', lambda e: e.memset(qrows[0:1, 0:1], 0.0), list(last_keys), [],
           tag='tailwait')
        with nc.Block() as block:
            p.emit(nc, block)
        for st in reversed(open_arenas):
            st.close()
        stack.close()
        return nc

    if upto == 'M':
        def ydump(e, half):
            rank = e.cc_rank(RG8)
            j = e.compute_val(rank % 4)
            ch = e.compute_val(j if half == 0 else 7 - j)
            return e.dma_start(out=dbg[:, half * TC:(half + 1) * TC],
                               in_=yag_all[dslice(ch, 1), :, :])
        OP('g', lambda e: ydump(e, 0), [f'yago{c}' for c in range(NCH)],
           ['dbg0'], kind='d', tag='ydump0')
        OP('g', lambda e: ydump(e, 1), [f'yago{c}' for c in range(NCH)],
           ['dbg1'], kind='d', tag='ydump1')
        a1.close()
        return finish(['dbg0', 'dbg1'])

    def dump_stream(stream):
        dma('q', dbg[:, :, :], stream[:, :, :],
            [f'stream_{kt}' for kt in range(16)], ['dbgd'], tag='dumpstream')
        return ['dbgd']

    def emit_ln(tag, rkey, rbuf, out_fn, lnps):
        ps_s, ps_m, rb_bf, rb_sq, rowbuf = lnps
        for kt in range(16):
            OP('s', lambda e, kt=kt: e.copy(out=rb_bf[:, :], in_=rbuf[:, kt, :]),
               [rkey], [f'{tag}_bf'], tag=f'{tag}_cp{kt}')
            OP('s', lambda e, kt=kt: e.activation(out=rb_sq[:, :], in_=rbuf[:, kt, :],
                                                  func=AF.Square),
               [rkey], [f'{tag}_sq'], tag=f'{tag}_sqr{kt}')
            OP('t', lambda e, kt=kt: e.matmul(ps_s[0:1, :], onescolb[:, :], rb_bf[:, :],
                                              start=(kt == 0), stop=(kt == 15)),
               [f'{tag}_bf', 'onescolb'], [f'{tag}_pss'], kind='c', accum=(kt > 0),
               tag=f'{tag}_mms{kt}')
            OP('t', lambda e, kt=kt: e.matmul(ps_m[0:1, :], onescolb[:, :], rb_sq[:, :],
                                              start=(kt == 0), stop=(kt == 15)),
               [f'{tag}_sq', 'onescolb'], [f'{tag}_psm'], kind='c', accum=(kt > 0),
               tag=f'{tag}_mmq{kt}')
        OP('v', lambda e: e.tensor_scalar(out=rowbuf[0:1, 0:512], in0=ps_s[0:1, :],
                                          scalar1=1.0 / D, scalar2=None, op0=ALU.mult),
           [f'{tag}_pss'], [f'{tag}_mrow'], tag=f'{tag}_mean')
        OP('v', lambda e: e.tensor_tensor(out=rowbuf[0:1, 512:1024],
                                          in0=rowbuf[0:1, 0:512],
                                          in1=rowbuf[0:1, 0:512], op=ALU.mult),
           [f'{tag}_mrow'], [f'{tag}_m2'], tag=f'{tag}_msq')
        OP('v', lambda e: e.tensor_scalar(out=rowbuf[0:1, 1024:1536], in0=ps_m[0:1, :],
                                          scalar1=1.0 / D, scalar2=None, op0=ALU.mult),
           [f'{tag}_psm'], [f'{tag}_vr1'], tag=f'{tag}_v1')
        OP('v', lambda e: e.tensor_tensor(out=rowbuf[0:1, 1024:1536],
                                          in0=rowbuf[0:1, 1024:1536],
                                          in1=rowbuf[0:1, 512:1024], op=ALU.subtract),
           [f'{tag}_vr1', f'{tag}_m2'], [f'{tag}_var'], accum=True, tag=f'{tag}_v2')
        OP('s', lambda e: e.activation(out=rowbuf[0:1, 1536:2048],
                                       in_=rowbuf[0:1, 1024:1536], func=AF.Sqrt,
                                       bias=epsc[0:1, 0:1]),
           [f'{tag}_var', 'epsc'], [f'{tag}_sd'], tag=f'{tag}_rsq')
        OP('v', lambda e: e.reciprocal(out=rowbuf[0:1, 512:1024],
                                       in_=rowbuf[0:1, 1536:2048]),
           [f'{tag}_sd', f'{tag}_m2'], [f'{tag}_rstd'], tag=f'{tag}_rcp')
        OP('t', lambda e: e.matmul(ps_s[:, :], onesrowF[:, :], rowbuf[0:1, 0:512],
                                   start=True, stop=True),
           [f'{tag}_mrow', 'onesrowF'], [f'{tag}_pss'], kind='c', tag=f'{tag}_bm')
        OP('t', lambda e: e.matmul(ps_m[:, :], onesrowF[:, :], rowbuf[0:1, 512:1024],
                                   start=True, stop=True),
           [f'{tag}_rstd', 'onesrowF'], [f'{tag}_psm'], kind='c', tag=f'{tag}_br')
        for kt in range(16):
            OP('v', lambda e, kt=kt: e.tensor_tensor(out=rbuf[:, kt, :],
                                                     in0=rbuf[:, kt, :],
                                                     in1=ps_s[:, :], op=ALU.subtract),
               [rkey, f'{tag}_pss'], [rkey], accum=True, tag=f'{tag}_sub{kt}')
            out_fn(kt)

    p.fence()
    a1.close()
    stream = sb('stream', [128, 16, 1024], BF16)
    aW, sbW, psW = mk_arena()
    ybuf = sbW('ybuf', [128, 32, 2 * TC], BF16)
    xres = sbW('xres', [128, 16, 2 * TC], BF16)
    wob = [sbW(f'wob{i}', [128, 32, 128], BF16) for i in range(2)]
    rbuf = sbW('rbuf', [128, 16, TC], F32)
    rb_bf = sbW('rb_bf', [128, TC], BF16)
    rb_sq = sbW('rb_sq', [128, TC], BF16)
    rowbuf = sbW('rowbuf', [1, 2048], F32)
    psA = psW('psA', [128, 512]); psB = psW('psB', [128, 512])
    psS = psW('psS', [128, 512]); psM = psW('psM', [128, 512])
    lnps = (psS, psM, rb_bf, rb_sq, rowbuf)

    def ypick(e, half):
        rank = e.cc_rank(RG8)
        j = e.compute_val(rank % 4)
        ch = e.compute_val(j if half == 0 else 7 - j)
        return e.dma_start(
            out=ybuf[:, :, half * TC:(half + 1) * TC],
            in_=yag_all[dslice(ch, 1), :, :]
            .rearrange("o (dt p) t -> p (o dt) t", p=128))
    OP('g', lambda e: ypick(e, 0), [f'yago{c}' for c in range(NCH)], ['ybuf'],
       kind='d', tag='ypick0')
    OP('g', lambda e: ypick(e, 1), [f'yago{c}' for c in range(NCH)], ['ybuf'],
       kind='d', accum=True, tag='ypick1')
    dma('g', xres[:, :, :], xown_d[:, :, :], ['xown_d'], ['xres'], tag='ldxres')

    for tc in range(2):
        for ot in range(16):
            wb = wob[ot % 2]
            dma('g', wb[:, :, :],
                wout_d[ot, :, :].rearrange("p (kt c) -> p kt c", c=128),
                ['wout_d'], [f'wob{ot % 2}'], tag=f'ldwo{tc}_{ot}')
            bank, pk = (psA, 'psA') if ot % 2 == 0 else (psB, 'psB')
            for kt in range(32):
                OP('t', lambda e, kt=kt, tc=tc, bank=bank, wb=wb, ot=ot: e.matmul(
                    bank[:, :], wb[:, kt, :], ybuf[:, kt, tc * TC:(tc + 1) * TC],
                    start=(kt == 0), stop=(kt == 31)),
                   [f'wob{ot % 2}', 'ybuf'], [pk], kind='c', accum=(kt > 0),
                   tag=f'womm{tc}_{ot}_{kt}')
            OP('v', lambda e, ot=ot, tc=tc, bank=bank, rbuf=rbuf: e.tensor_tensor(
                out=rbuf[:, ot, :], in0=bank[:, :],
                in1=xres[:, ot, tc * TC:(tc + 1) * TC], op=ALU.add),
               [pk, 'xres'], ['rW'], accum=(ot > 0), tag=f'woev{tc}_{ot}')

        def out_fn(kt, tc=tc, rbuf=rbuf, psM=psM):
            OP('v', lambda e, kt=kt, tc=tc, rbuf=rbuf, psM=psM: e.tensor_tensor(
                out=stream[:, kt, tc * TC:(tc + 1) * TC], in0=rbuf[:, kt, :],
                in1=psM[:, :], op=ALU.mult),
               ['rW', 'ln1_psm'], [f'stream_{kt}'], accum=(tc > 0),
               tag=f'ln1o{tc}_{kt}')
        emit_ln('ln1', 'rW', rbuf, out_fn, lnps)

    if upto == 'W':
        k = dump_stream(stream)
        return finish(k)

    p.fence()
    aW.close()
    aF, sbF, psF = mk_arena()
    g0s_sb = sbF('g0s_sb', [128, 44], F32)
    u0s_sb = sbF('u0s_sb', [128, 44], F32)
    d0s_sb = sbF('d0s_sb', [128, 16], F32)
    dma('q', g0s_sb[:, :], g0s_d[:, :], ['g0s_d'], ['g0s_sb'])
    dma('q', u0s_sb[:, :], u0s_d[:, :], ['u0s_d'], ['u0s_sb'])
    dma('q', d0s_sb[:, :], d0s_d[:, :], ['d0s_d'], ['d0s_sb'])
    gwb = [sbF(f'gwb{i}', [128, 16, 128], BF16) for i in range(2)]
    uwb = [sbF(f'uwb{i}', [128, 16, 128], BF16) for i in range(2)]
    dwb = [sbF(f'dwb{i}', [128, 44, 128], BF16) for i in range(2)]
    hbuf = sbF('hbuf', [128, 44, 2 * TC], BF16)
    gsil = sbF('gsil', [128, TC], BF16)
    rbuf = sbF('rbufF', [128, 16, TC], F32)
    rb_bf = sbF('rb_bfF', [128, TC], BF16)
    rb_sq = sbF('rb_sqF', [128, TC], BF16)
    rowbuf = sbF('rowbufF', [1, 2048], F32)
    psA = psF('psA_F', [128, 512]); psB = psF('psB_F', [128, 512])
    psC = psF('psC_F', [128, 512]); psD = psF('psD_F', [128, 512])
    psS = psF('psS_F', [128, 512]); psM = psF('psM_F', [128, 512])
    lnps = (psS, psM, rb_bf, rb_sq, rowbuf)

    for ot in range(44):
        gw, uw = gwb[ot % 2], uwb[ot % 2]
        dma('g', gw[:, :, :], g0_d[ot, :, :].rearrange("p (kt c) -> p kt c", c=128),
            ['g0_d'], [f'gwb{ot % 2}'], tag=f'ldg0_{ot}')
        dma('g', uw[:, :, :], u0_d[ot, :, :].rearrange("p (kt c) -> p kt c", c=128),
            ['u0_d'], [f'uwb{ot % 2}'], tag=f'ldu0_{ot}')
        for tc in range(2):
            bg, pg = (psA, 'psA_F') if ot % 2 == 0 else (psB, 'psB_F')
            bu, pu = (psC, 'psC_F') if ot % 2 == 0 else (psD, 'psD_F')
            for kt in range(16):
                OP('t', lambda e, kt=kt, tc=tc, bg=bg, gw=gw, ot=ot: e.matmul(
                    bg[:, :], gw[:, kt, :], stream[:, kt, tc * TC:(tc + 1) * TC],
                    start=(kt == 0), stop=(kt == 15)),
                   [f'gwb{ot % 2}', f'stream_{kt}'], [pg], kind='c', accum=(kt > 0),
                   tag=f'g0mm{ot}_{tc}_{kt}')
            for kt in range(16):
                OP('t', lambda e, kt=kt, tc=tc, bu=bu, uw=uw, ot=ot: e.matmul(
                    bu[:, :], uw[:, kt, :], stream[:, kt, tc * TC:(tc + 1) * TC],
                    start=(kt == 0), stop=(kt == 15)),
                   [f'uwb{ot % 2}', f'stream_{kt}'], [pu], kind='c', accum=(kt > 0),
                   tag=f'u0mm{ot}_{tc}_{kt}')
            OP('s', lambda e, ot=ot, bg=bg: e.activation(
                out=gsil[:, :], in_=bg[:, :], func=AF.Silu,
                scale=g0s_sb[:, ot:ot + 1]),
               [pg, 'g0s_sb'], ['gsil'], tag=f'g0sil{ot}_{tc}')
            OP('v', lambda e, ot=ot, tc=tc, bu=bu: e.scalar_tensor_tensor(
                out=hbuf[:, ot, tc * TC:(tc + 1) * TC], in0=bu[:, :],
                scalar=u0s_sb[:, ot:ot + 1], in1=gsil[:, :],
                op0=ALU.mult, op1=ALU.mult),
               [pu, 'gsil', 'u0s_sb'], [f'hb0_{ot}'], accum=(tc > 0),
               tag=f'h0{ot}_{tc}')
    for tc in range(2):
        for ot in range(16):
            dw = dwb[ot % 2]
            dma('g', dw[:, :, :], d0_d[ot, :, :].rearrange("p (kt c) -> p kt c", c=128),
                ['d0_d'], [f'dwb{ot % 2}'], tag=f'ldd0_{tc}_{ot}')
            bank, pk = (psA, 'psA_F') if ot % 2 == 0 else (psB, 'psB_F')
            for kt in range(44):
                OP('t', lambda e, kt=kt, tc=tc, bank=bank, dw=dw, ot=ot: e.matmul(
                    bank[:, :], dw[:, kt, :], hbuf[:, kt, tc * TC:(tc + 1) * TC],
                    start=(kt == 0), stop=(kt == 43)),
                   [f'dwb{ot % 2}', f'hb0_{kt}'], [pk], kind='c', accum=(kt > 0),
                   tag=f'd0mm{ot}_{tc}_{kt}')
            OP('v', lambda e, ot=ot, tc=tc, bank=bank, rbuf=rbuf: e.scalar_tensor_tensor(
                out=rbuf[:, ot, :], in0=bank[:, :], scalar=d0s_sb[:, ot:ot + 1],
                in1=stream[:, ot, tc * TC:(tc + 1) * TC],
                op0=ALU.mult, op1=ALU.add),
               [pk, f'stream_{ot}', 'd0s_sb'], ['rF'], accum=(ot > 0),
               tag=f'd0ev{ot}_{tc}')

        def out_fn(kt, tc=tc, rbuf=rbuf, psM=psM):
            OP('v', lambda e, kt=kt, tc=tc, rbuf=rbuf, psM=psM: e.tensor_tensor(
                out=stream[:, kt, tc * TC:(tc + 1) * TC], in0=rbuf[:, kt, :],
                in1=psM[:, :], op=ALU.mult),
               ['rF', 'ln2_psm'], [f'stream_{kt}'], tag=f'ln2o{tc}_{kt}')
        emit_ln('ln2', 'rF', rbuf, out_fn, lnps)

    if upto == 'F0':
        k = dump_stream(stream)
        return finish(k)

    p.fence()
    aF.close()
    aA, sbA, psAo = mk_arena()
    ctq = [sbA(f'ctq{i}', [128, 16, TC], BF16) for i in range(2)]
    dma('q', sag_i.rearrange("(kt p) t -> p kt t", p=128), stream[:, :, :],
        [f'stream_{kt}' for kt in range(16)], ['sag_i_d'], tag='stsag')
    OP('g', lambda e: e.collective_compute(
        "AllGather", ALU.bypass, replica_groups=RG4,
        ins=[sag_i.opt()], outs=[sag_o.opt()]),
       ['sag_i_d'], ['sAG'], kind='cc', tag='sAG')

    aA1, sbA1, psA1 = mk_arena()
    rc_sb = sbA1('rc_sb', [128, T], BF16)
    rs_sb = sbA1('rs_sb', [128, T], BF16)
    permb = sbA1('permb', [128, 128], BF16)
    trib = sbA1('trib', [128, 4, TC], BF16)
    wqh_sb = sbA1('wqh_sb', [128, 4, 16, 128], BF16)
    wkh_sb = sbA1('wkh_sb', [128, 16, 128], BF16)
    wvh_sb = sbA1('wvh_sb', [128, 16, 128], BF16)
    dma('g', rc_sb[:, :], rc_d[:, :], ['rc_d'], ['rc_sb'])
    dma('g', rs_sb[:, :], rs_d[:, :], ['rs_d'], ['rs_sb'])
    dma('g', permb[:, :], perm_d[:, :], ['perm_d'], ['permb'])
    dma('g', trib[:, :, :], tri_d.rearrange("kt p t -> p kt t"), ['tri_d'], ['trib'])
    for t4 in range(4):
        dma('g', wqh_sb[:, t4, :, :],
            wqh_d[t4, :, :].rearrange("p (kt c) -> p kt c", c=128),
            ['wqh_d'], ['wqh_sb'], tag=f'ldwqh{t4}', accum=(t4 > 0))
    dma('g', wkh_sb[:, :, :], wkh_d.rearrange("p (kt c) -> p kt c", c=128),
        ['wkh_d'], ['wkh_sb'])
    dma('g', wvh_sb[:, :, :], wvh_d[:, :, :], ['wvh_d'], ['wvh_sb'])
    aqb = sbA1('aqb_a', [128, 4, T], BF16)
    kbuf = sbA1('kbuf_a', [128, T], BF16)
    vtm = sbA1('vtm_a', [128, 32, 130], BF16)
    axa = [sbA1(f'xa{i}', [128, 16, TC], BF16) for i in range(2)]
    qtmp = sbA1('qtmp', [128, TC], BF16)
    t1b = sbA1('t1b', [128, TC], BF16)
    ptb = [sbA1(f'ptb{i}', [128, TC], BF16) for i in range(3)]
    ctmp = sbA1('ctmp', [64, TC], F32)
    rcb = sbA1('rcb', [64, TC], F32)
    rrow = sbA1('rrow', [1, TC], F32)
    psQ1 = psA1('psQ1', [128, 512]); psQ2 = psA1('psQ2', [128, 512])
    psP = psA1('psP', [128, 512])
    psSc = [psA1('psSc0', [128, 512]), psA1('psSc1', [128, 512])]
    psCx = [psA1('psCx0', [128, 512]), psA1('psCx1', [128, 512])]
    psB2 = psA1('psB2', [128, 512])
    OP('v', lambda e: e.memset(vtm[:, :, 64:65], 1.0), [], ['vtm'], accum=True,
       tag='vone0')
    OP('v', lambda e: e.memset(vtm[:, :, 129:130], 1.0), [], ['vtm'], accum=True,
       tag='vone1')

    def rope_tile(nm, qc, bank, pk, dst, dkey):
        OP('s', lambda e, bank=bank: e.copy(out=qtmp[:, :], in_=bank[:, :]),
           [pk], ['qtmp'], tag=f'{nm}cp{qc}')
        OP('t', lambda e: e.matmul(psP[:, :], permb[:, :], qtmp[:, :],
                                   start=True, stop=True),
           ['qtmp', 'permb'], ['psP'], kind='c', tag=f'{nm}pm{qc}')
        OP('v', lambda e, qc=qc: e.tensor_tensor(
            out=t1b[:, :], in0=qtmp[:, :], in1=rc_sb[:, qc * TC:(qc + 1) * TC],
            op=ALU.mult),
           ['qtmp', 'rc_sb'], ['t1b'], tag=f'{nm}t1{qc}')
        OP('v', lambda e, qc=qc: e.tensor_tensor(
            out=qtmp[:, :], in0=psP[:, :], in1=rs_sb[:, qc * TC:(qc + 1) * TC],
            op=ALU.mult),
           ['psP', 'rs_sb', 'qtmp'], ['qtmp'], accum=True, tag=f'{nm}t2{qc}')
        OP('v', lambda e, dst=dst: e.tensor_tensor(
            out=dst, in0=t1b[:, :], in1=qtmp[:, :], op=ALU.add),
           ['t1b', 'qtmp'], [dkey], accum=True, tag=f'{nm}t3{qc}')

    for qc in range(NCH):
        b = qc % 2
        rank, halfk = chunk_pos(qc)
        dma('q', axa[b][:, :, :],
            sag_o[rank * D:(rank + 1) * D, halfk * TC:(halfk + 1) * TC]
            .rearrange("(kt p) t -> p kt t", p=128),
            ['sAG'], [f'xa{b}'], tag=f'ldxa{qc}')
        for kt in range(16):
            OP('t', lambda e, kt=kt, b=b: e.matmul(
                psQ1[:, :], wkh_sb[:, kt, :], axa[b][:, kt, :],
                start=(kt == 0), stop=(kt == 15)),
               ['wkh_sb', f'xa{b}'], ['psQ1'], kind='c', accum=(kt > 0),
               tag=f'kmm{qc}_{kt}')
        rope_tile('k', qc, psQ1, 'psQ1', kbuf[:, qc * TC:(qc + 1) * TC], 'kbuf')
        for t4 in range(4):
            bank, pk = (psQ1, 'psQ1') if t4 % 2 else (psQ2, 'psQ2')
            for kt in range(16):
                OP('t', lambda e, kt=kt, t4=t4, b=b, bank=bank: e.matmul(
                    bank[:, :], wqh_sb[:, t4, kt, :], axa[b][:, kt, :],
                    start=(kt == 0), stop=(kt == 15)),
                   ['wqh_sb', f'xa{b}'], [pk], kind='c', accum=(kt > 0),
                   tag=f'qmm{qc}_{t4}_{kt}')
            rope_tile(f'q{t4}', qc, bank, pk,
                      aqb[:, t4, qc * TC:(qc + 1) * TC], 'aqb')
        for tb in range(4):
            bank, pk = (psQ1, 'psQ1') if tb % 2 else (psQ2, 'psQ2')
            for kt in range(16):
                OP('t', lambda e, kt=kt, tb=tb, b=b, bank=bank: e.matmul(
                    bank[:, 0:128], axa[b][:, kt, tb * 128:(tb + 1) * 128],
                    wvh_sb[:, kt, :], start=(kt == 0), stop=(kt == 15)),
                   ['wvh_sb', f'xa{b}'], [pk], kind='c', accum=(kt > 0),
                   tag=f'vmm{qc}_{tb}_{kt}')
            OP('s', lambda e, qc=qc, tb=tb, bank=bank: e.copy(
                out=vtm[:, qc * 4 + tb, 0:64], in_=bank[:, 0:64]),
               [pk, 'vtm'], ['vtm'], accum=True, tag=f'vev{qc}_{tb}a')
            OP('s', lambda e, qc=qc, tb=tb, bank=bank: e.copy(
                out=vtm[:, qc * 4 + tb, 65:129], in_=bank[:, 64:128]),
               [pk, 'vtm'], ['vtm'], accum=True, tag=f'vev{qc}_{tb}b')
        par = 1 if qc >= 4 else 0
        kt16b = (qc * 4) % 16
        for h8 in range(8):
            qbase = 0 if h8 < 4 else 64
            qtile = h8 % 4
            vb = 0 if h8 < 4 else 65
            cbank, ck = psCx[h8 % 2], f'psCx{h8 % 2}'
            nk = 0
            NK = 4 * (qc + 1)
            for kc in range(qc + 1):
                for kt in range(4):
                    i = nk % 3
                    sbank, sk = psSc[nk % 2], f'psSc{nk % 2}'
                    OP('t', lambda e, qbase=qbase, qtile=qtile, kc=kc, kt=kt,
                       qc=qc, sbank=sbank: e.matmul(
                        sbank[:, :],
                        kbuf[qbase:qbase + 64, kc * TC + kt * 128:kc * TC + (kt + 1) * 128],
                        aqb[qbase:qbase + 64, qtile, qc * TC:(qc + 1) * TC],
                        start=True, stop=True),
                       ['kbuf', 'aqb'], [sk], kind='c', tag=f'smm{qc}_{h8}_{kc}_{kt}')
                    OP('s', lambda e, i=i, sbank=sbank: e.activation(
                        out=ptb[i][:, :], in_=sbank[:, :], func=AF.Exp, scale=0.125),
                       [sk], [f'ptb{i}'], tag=f'exp{qc}_{h8}_{kc}_{kt}')
                    if kc == qc:
                        OP('g', lambda e, i=i, kt=kt: e.tensor_tensor(
                            out=ptb[i][:, :], in0=ptb[i][:, :],
                            in1=trib[:, kt, :], op=ALU.mult),
                           [f'ptb{i}', 'trib'], [f'ptb{i}'], accum=True,
                           tag=f'msk{qc}_{h8}_{kt}')
                    OP('t', lambda e, i=i, kc=kc, kt=kt, vb=vb, cbank=cbank,
                       first=(nk == 0), last=(nk == NK - 1): e.matmul(
                        cbank[0:65, :], vtm[:, kc * 4 + kt, vb:vb + 65],
                        ptb[i][:, :], start=first, stop=last),
                       [f'ptb{i}', 'vtm'], [ck], kind='c', accum=(nk > 0),
                       tag=f'av{qc}_{h8}_{kc}_{kt}')
                    nk += 1
            OP('v', lambda e, cbank=cbank: e.reciprocal(
                out=rrow[0:1, :], in_=cbank[64:65, :]),
               [ck], ['rrow'], tag=f'rec{qc}_{h8}')
            OP('t', lambda e: e.matmul(psB2[0:64, :], onesrowF[:, 0:64],
                                       rrow[0:1, :], start=True, stop=True),
               ['rrow', 'onesrowF'], ['psB2'], kind='c', tag=f'bcr{qc}_{h8}')
            OP('s', lambda e: e.copy(out=rcb[:, :], in_=psB2[0:64, :]),
               ['psB2'], ['rcb'], tag=f'rcb{qc}_{h8}')
            OP('v', lambda e, cbank=cbank: e.tensor_tensor(
                out=ctmp[:, :], in0=cbank[0:64, :], in1=rcb[:, :],
                op=ALU.mult),
               [ck, 'rcb'], ['ctmp'], tag=f'cev{qc}_{h8}')
            hg = 0 if h8 < 4 else 1
            hloc = h8 % 4
            for kt in range(4):
                OP('t', lambda e, kt=kt: e.transpose(
                    out=psP[0:128, 0:64], in_=ctmp[:, kt * 128:(kt + 1) * 128],
                    identity=identf[0:64, 0:64]),
                   ['ctmp', 'identb', 'psP'], ['psP'], kind='c',
                   tag=f'ctp{qc}_{h8}_{kt}')
                OP('v', lambda e, par=par, kt16b=kt16b, kt=kt, hg=hg,
                   hloc=hloc: e.tensor_copy(
                    out=ctq[par][:, kt16b + kt, hg * 256 + hloc * 64:
                                 hg * 256 + (hloc + 1) * 64],
                    in_=psP[0:128, 0:64]),
                   ['psP'], [f'ctq{par}'], accum=True, tag=f'ctc{qc}_{h8}_{kt}')

    p.fence()
    aA1.close()
    aA3, sbA3, psA3 = mk_arena()
    wob2 = [sbA3(f'wob2{i}', [128, 16, 128], BF16) for i in range(2)]
    rbuf = sbA3('rbufA', [128, 16, TC], F32)
    rb_bf = sbA3('rb_bfA', [128, TC], BF16)
    rb_sq = sbA3('rb_sqA', [128, TC], BF16)
    rowbuf = sbA3('rowbufA', [1, 2048], F32)
    psWo = psA3('psWo', [128, 512]); psWo2 = psA3('psWo2', [128, 512])
    psS = psA3('psS_A', [128, 512]); psM = psA3('psM_A', [128, 512])
    lnps = (psS, psM, rb_bf, rb_sq, rowbuf)
    for tc in range(2):
        for ot in range(16):
            wb = wob2[ot % 2]
            dma('g', wb[:, :, :], wo_d[ot, :, :].rearrange("p (kt c) -> p kt c", c=128),
                ['wo_d'], [f'wob2{ot % 2}'], tag=f'ldwo2_{tc}_{ot}')
            bank, pk = (psWo, 'psWo') if ot % 2 == 0 else (psWo2, 'psWo2')
            for par in range(2):
                for kt in range(16):
                    OP('t', lambda e, kt=kt, tc=tc, par=par, bank=bank, wb=wb:
                       e.matmul(
                        bank[:, par * 256:(par + 1) * 256], wb[:, kt, :],
                        ctq[par][:, kt, tc * 256:(tc + 1) * 256],
                        start=(kt == 0), stop=(kt == 15)),
                       [f'wob2{ot % 2}', f'ctq{par}'], [pk], kind='c',
                       accum=(par > 0 or kt > 0), tag=f'womm2{tc}_{ot}_{par}_{kt}')
            for par in range(2):
                OP('v', lambda e, ot=ot, tc=tc, par=par, bank=bank, rbuf=rbuf:
                   e.tensor_tensor(
                    out=rbuf[:, ot, par * 256:(par + 1) * 256],
                    in0=bank[:, par * 256:(par + 1) * 256],
                    in1=stream[:, ot, tc * TC + par:(tc + 1) * TC:2],
                    op=ALU.add),
                   [pk, f'stream_{ot}'], ['rA'], accum=(ot > 0 or par > 0),
                   tag=f'woev2{tc}_{ot}_{par}')

        def out_fn(kt, tc=tc, rbuf=rbuf, psM=psM):
            for par in range(2):
                OP('v', lambda e, kt=kt, tc=tc, par=par, rbuf=rbuf, psM=psM:
                   e.tensor_tensor(
                    out=stream[:, kt, tc * TC + par:(tc + 1) * TC:2],
                    in0=rbuf[:, kt, par * 256:(par + 1) * 256],
                    in1=psM[:, par * 256:(par + 1) * 256], op=ALU.mult),
                   ['rA', 'ln3_psm'], [f'stream_{kt}'], accum=(par > 0),
                   tag=f'ln3o{tc}_{kt}_{par}')
        emit_ln('ln3', 'rA', rbuf, out_fn, lnps)

    if upto == 'A':
        k = dump_stream(stream)
        return finish(k)

    p.fence()
    aA3.close()
    aA.close()
    aB, sbB, psB_ = mk_arena()
    lnw_sb = sbB('lnw_sb', [128, 64], F32)
    scol = sbB('scol', [128, 4], F32)
    dma('q', lnw_sb[:, :], lnw_d[:, :], ['lnw_d'], ['lnw_sb'])
    OP('v', lambda e: e.tensor_scalar(out=scol[:, :], in0=mcol[:, :],
                                      scalar1=1.0 / 127.0, scalar2=None,
                                      op0=ALU.mult),
       ['mcol'], ['scol'], tag='scol')
    qxb = sbB('qxb', [128, 16, TC], BF16)
    qgb = [sbB(f'qgb{i}', [128, 2048], BF16) for i in range(2)]
    qub = [sbB(f'qub{i}', [128, 2048], BF16) for i in range(2)]
    qdb = [sbB(f'qdb{i}', [128, 2048], BF16) for i in range(2)]
    hbuf1 = sbB('hbuf1', [128, 64, TC], BF16)
    guq = sbB('guq', [128, TC], F32)
    gsl = sbB('gsl', [128, TC], BF16)
    xa = [sbB(f'xaB{i}', [128, TC], BF16) for i in range(2)]
    xmax = sbB('xmax', [128, TC], F32)
    mxt = sbB('mxt', [128, 4], F32)
    mx4 = sbB('mx4', [4, 128], F32)
    sx4 = sbB('sx4', [4, 128], F32)
    sel4_sb = sbB('sel4_sb', [4, 4 * 128], F32)
    vrt = sbB('vrt', [128, 4], F32)
    rstd4 = sbB('rstd4', [128, 4], F32)
    ct4 = sbB('ct4', [128, 4], F32)
    epsB = sbB('epsB', [128, 1], F32)
    OP('v', lambda e: e.memset(epsB[:, :], RMS_EPS), [], ['epsB'])
    dma('q', sel4_sb[:, :], sel4_d[:, :], ['sel4_d'], ['sel4_sb'])
    hsqb = sbB('hsqb', [128, TC], BF16)
    bc1 = sbB('bc1', [128, TC], F32)
    bc2 = sbB('bc2', [128, TC], F32)
    qt = sbB('qt', [128, TC], F32)
    rowq = sbB('rowq', [1, TC], F32)
    rbuf = sbB('rbufB', [128, 16, TC], F32)
    rb_bf = sbB('rb_bfB', [128, TC], BF16)
    rb_sq = sbB('rb_sqB', [128, TC], BF16)
    rowbuf = sbB('rowbufB', [1, 2048], F32)
    obuf = sbB('obuf', [128, 4, 128], F32)
    psG = [psB_('psG0', [128, 512]), psB_('psG1', [128, 512])]
    psU = [psB_('psU0', [128, 512]), psB_('psU1', [128, 512])]
    psR = psB_('psR', [128, 512])
    psX = psB_('psX', [128, 512])
    psS = psB_('psS_B', [128, 512]); psM = psB_('psM_B', [128, 512])
    lnps = (psS, psM, rb_bf, rb_sq, rowbuf)

    def cross_max_bcasts(tag, rec_dst, rec_key, mx_dst, mx_key, rec_scale_tag):
        for tb in range(4):
            OP('t', lambda e, tb=tb: e.transpose(
                out=psX[:, tb * 128:(tb + 1) * 128],
                in_=xmax[:, tb * 128:(tb + 1) * 128],
                identity=identf[:, :]),
               ['xmax', 'identf', 'psX'], ['psX'], kind='c', accum=(tb > 0),
               tag=f'{tag}tp{tb}')
        for tb in range(4):
            OP('v', lambda e, tb=tb: e.reduce_max(
                out=mxt[:, tb:tb + 1], in_=psX[:, tb * 128:(tb + 1) * 128], axis=AXF),
               ['psX'], ['mxt'], accum=(tb > 0), tag=f'{tag}rm{tb}')
        OP('v', lambda e: e.tensor_scalar(out=mxt[:, :], in0=mxt[:, :],
                                          scalar1=EPS_Q, scalar2=None, op0=ALU.max),
           ['mxt'], ['mxt'], accum=True, tag=f'{tag}clip')
        OP('t', lambda e: e.transpose(out=psX[0:4, 0:128], in_=mxt[:, :],
                                      identity=identf[:, :]),
           ['mxt', 'identf', 'psX'], ['psX'], tag=f'{tag}tpm')
        OP('v', lambda e: e.tensor_copy(out=mx4[:, :], in_=psX[0:4, 0:128]),
           ['psX'], ['mx4'], tag=f'{tag}mx4')
        OP('v', lambda e: e.reciprocal(out=sx4[:, :], in_=mx4[:, :]),
           ['mx4'], ['sx4'], tag=f'{tag}rcp')
        for tb in range(4):
            OP('t', lambda e, tb=tb: e.matmul(
                psX[:, tb * 128:(tb + 1) * 128],
                sel4_sb[:, tb * 128:(tb + 1) * 128], sx4[:, :],
                start=True, stop=True),
               ['sx4', 'sel4_sb', 'psX'], ['psX'], kind='c', accum=(tb > 0),
               tag=f'{tag}bcr{tb}')
        OP('v', lambda e: e.tensor_scalar(out=rec_dst[:, :], in0=psX[:, :],
                                          scalar1=127.0, scalar2=None, op0=ALU.mult),
           ['psX'], [rec_key], tag=f'{tag}recb')
        for tb in range(4):
            OP('t', lambda e, tb=tb: e.matmul(
                psX[:, tb * 128:(tb + 1) * 128],
                sel4_sb[:, tb * 128:(tb + 1) * 128], mx4[:, :],
                start=True, stop=True),
               ['mx4', 'sel4_sb', 'psX'], ['psX'], kind='c', accum=(tb > 0),
               tag=f'{tag}bcm{tb}')
        OP('v', lambda e: e.tensor_copy(out=mx_dst[:, :], in_=psX[:, :]),
           ['psX'], [mx_key], tag=f'{tag}mxb')

    RA = slice(0, TC)

    outks = []
    for tc in range(2):
        for kt in range(16):
            OP('s', lambda e, kt=kt, tc=tc: e.activation(
                out=xa[kt % 2][:, :], in_=stream[:, kt, tc * TC:(tc + 1) * TC],
                func=AF.Abs),
               [f'stream_{kt}'], [f'xa{kt % 2}'], tag=f'xabs{tc}_{kt}')
            if kt == 0:
                OP('v', lambda e: e.tensor_copy(out=xmax[:, :], in_=xa[0][:, :]),
                   ['xa0'], ['xmax'], tag=f'xm0_{tc}')
            else:
                OP('v', lambda e, kt=kt: e.tensor_tensor(
                    out=xmax[:, :], in0=xmax[:, :], in1=xa[kt % 2][:, :],
                    op=ALU.max),
                   [f'xa{kt % 2}', 'xmax'], ['xmax'], accum=True, tag=f'xm{tc}_{kt}')
        cross_max_bcasts(f'xq{tc}_', bc1, 'bc1', bc2, 'bc2', tc)
        for kt in range(16):
            OP('v', lambda e, kt=kt, tc=tc: e.tensor_tensor(
                out=qt[:, :], in0=stream[:, kt, tc * TC:(tc + 1) * TC],
                in1=bc1[:, :], op=ALU.mult),
               [f'stream_{kt}', 'bc1'], ['qt'], tag=f'qxm{tc}_{kt}')
            OP('v', lambda e, kt=kt: e.tensor_scalar(
                out=qxb[:, kt, :], in0=qt[:, :], scalar1=MAGIC, scalar2=MAGIC,
                op0=ALU.add, op1=ALU.subtract),
               ['qt'], ['qxb'], accum=(kt > 0), tag=f'qxr{tc}_{kt}')
        if upto == 'F1d' and tc == 0:
            dma('q', dbg[:, :, :], qxb[:, :, :], ['qxb'], ['dbgd1'], tag='dqx')
            OP('v', lambda e: e.tensor_tensor(
                out=bc2[:, :], in0=bc2[:, :], in1=bc2[:, :], op=ALU.bypass)
               if False else e.tensor_copy(out=qt[:, :], in_=bc1[:, :]),
               ['bc1'], ['qt'], tag='dbc1c')
            dma('q', dbg4[:, 0:512], qt[:, :], ['qt'], ['dbgd4a'], tag='dbc1')
            OP('v', lambda e: e.tensor_copy(out=qt[:, :], in_=bc2[:, :]),
               ['bc2', 'qt'], ['qt'], tag='dbc2c')
            dma('q', dbg4[:, 512:1024], qt[:, :], ['qt'], ['dbgd4b'], tag='dbc2')
            dma('q', dbg4[:, 1024:1028], swcol[:, :], ['swcol'], ['dbgd4c'])
            dma('q', dbg4[:, 1028:1032], mcol[:, :], ['mcol'], ['dbgd4d'])
            dma('q', dbg4[:, 1032:1036], scol[:, :], ['scol'], ['dbgd4e'])
            dma('q', dbg5[:, :], qg_o[0:128, 0:2048], ['qAGg'], ['dbgd5'])
        for ot in range(64):
            rnk, ot8 = ot // 8, ot % 8
            gw, uw = qgb[ot % 2], qub[ot % 2]
            dma('q', gw[:, :], qg_o[rnk * 128:(rnk + 1) * 128,
                                    ot8 * 2048:(ot8 + 1) * 2048],
                ['qAGg'], [f'qgb{ot % 2}'], tag=f'ldqg{tc}_{ot}')
            dma('q', uw[:, :], qu_o[rnk * 128:(rnk + 1) * 128,
                                    ot8 * 2048:(ot8 + 1) * 2048],
                ['qAGu'], [f'qub{ot % 2}'], tag=f'ldqu{tc}_{ot}')
            bg, pg = psG[ot % 2], f'psG{ot % 2}'
            bu, pu = psU[ot % 2], f'psU{ot % 2}'
            for kt in range(16):
                OP('t', lambda e, kt=kt, bg=bg, gw=gw, ot=ot: e.matmul(
                    bg[:, :], gw[:, kt * 128:(kt + 1) * 128], qxb[:, kt, :],
                    start=(kt == 0), stop=(kt == 15)),
                   [f'qgb{ot % 2}', 'qxb'], [pg], kind='c', accum=(kt > 0),
                   tag=f'gmm{tc}_{ot}_{kt}')
            for kt in range(16):
                OP('t', lambda e, kt=kt, bu=bu, uw=uw, ot=ot: e.matmul(
                    bu[:, :], uw[:, kt * 128:(kt + 1) * 128], qxb[:, kt, :],
                    start=(kt == 0), stop=(kt == 15)),
                   [f'qub{ot % 2}', 'qxb'], [pu], kind='c', accum=(kt > 0),
                   tag=f'umm{tc}_{ot}_{kt}')
            OP('v', lambda e, bg=bg: e.scalar_tensor_tensor(
                out=guq[:, :], in0=bg[:, :], scalar=scol[:, 0:1], in1=bc2[:, :],
                op0=ALU.mult, op1=ALU.mult),
               [pg, 'scol', 'bc2'], ['guq'], tag=f'gev{tc}_{ot}')
            OP('s', lambda e: e.activation(out=gsl[:, :], in_=guq[:, :],
                                           func=AF.Silu),
               ['guq'], ['gsl'], tag=f'gsil{tc}_{ot}')
            OP('v', lambda e, bu=bu: e.scalar_tensor_tensor(
                out=guq[:, :], in0=bu[:, :], scalar=scol[:, 1:2], in1=bc2[:, :],
                op0=ALU.mult, op1=ALU.mult),
               [pu, 'scol', 'bc2', 'gsl'], ['guq'], tag=f'uev{tc}_{ot}')
            OP('v', lambda e, ot=ot: e.tensor_tensor(
                out=hbuf1[:, ot, :], in0=gsl[:, :], in1=guq[:, :], op=ALU.mult),
               ['gsl', 'guq'], [f'hb1_{ot}'], tag=f'hmul{tc}_{ot}')
        for ot in range(64):
            OP('s', lambda e, ot=ot: e.activation(
                out=xa[ot % 2][:, :], in_=hbuf1[:, ot, :], func=AF.Abs,
                scale=lnw_sb[:, ot:ot + 1]),
               [f'hb1_{ot}', 'lnw_sb'], [f'xa{ot % 2}'], tag=f'habs{tc}_{ot}')
            if ot == 0:
                OP('v', lambda e: e.tensor_copy(out=xmax[:, :], in_=xa[0][:, :]),
                   ['xa0'], ['xmax'], tag=f'hm0_{tc}')
            else:
                OP('v', lambda e, ot=ot: e.tensor_tensor(
                    out=xmax[:, :], in0=xmax[:, :], in1=xa[ot % 2][:, :],
                    op=ALU.max),
                   [f'xa{ot % 2}', 'xmax'], ['xmax'], accum=True, tag=f'hm{tc}_{ot}')
            OP('s', lambda e, ot=ot: e.activation(
                out=hsqb[:, :], in_=hbuf1[:, ot, :], func=AF.Square),
               [f'hb1_{ot}'], ['hsq'], tag=f'hsqr{tc}_{ot}')
            OP('t', lambda e, ot=ot: e.matmul(
                psR[0:1, :], onescolb[:, :], hsqb[:, :],
                start=(ot == 0), stop=(ot == 63)),
               ['hsq', 'onescolb'], ['psR'], kind='c', accum=(ot > 0),
               tag=f'hsmm{tc}_{ot}')
        for tb in range(4):
            OP('t', lambda e, tb=tb: e.transpose(
                out=psX[:, tb * 128:(tb + 1) * 128],
                in_=xmax[:, tb * 128:(tb + 1) * 128],
                identity=identf[:, :]),
               ['xmax', 'identf', 'psX'], ['psX'], kind='c', accum=(tb > 0),
               tag=f'htp{tc}_{tb}')
        for tb in range(4):
            OP('v', lambda e, tb=tb: e.reduce_max(
                out=mxt[:, tb:tb + 1], in_=psX[:, tb * 128:(tb + 1) * 128], axis=AXF),
               ['psX'], ['mxt'], accum=(tb > 0), tag=f'hrm{tc}_{tb}')
        OP('v', lambda e: e.tensor_scalar(out=rowq[0:1, RA], in0=psR[0:1, :],
                                          scalar1=1.0 / I1, scalar2=None,
                                          op0=ALU.mult),
           ['psR'], ['vrow'], tag=f'vrow{tc}')
        for tb in range(4):
            OP('t', lambda e, tb=tb: e.transpose(
                out=psR[:, tb:tb + 1],
                in_=rowq[0:1, tb * 128:(tb + 1) * 128],
                identity=identf[0:1, 0:1]),
               ['vrow', 'identf', 'psR'], ['psR'], kind='c', accum=True,
               tag=f'vtp{tc}_{tb}')
        OP('v', lambda e: e.tensor_copy(out=vrt[:, :], in_=psR[:, 0:4]),
           ['psR'], ['vrt'], tag=f'vrt{tc}')
        OP('s', lambda e: e.activation(out=vrt[:, :], in_=vrt[:, :],
                                       func=AF.Sqrt, bias=epsB[:, 0:1]),
           ['vrt', 'epsB'], ['vrt'], accum=True, tag=f'hsd{tc}')
        OP('v', lambda e: e.reciprocal(out=rstd4[:, :], in_=vrt[:, :]),
           ['vrt'], ['rstd4'], tag=f'hrstd{tc}')
        OP('v', lambda e: e.tensor_tensor(out=mxt[:, :], in0=mxt[:, :],
                                          in1=rstd4[:, :], op=ALU.mult),
           ['mxt', 'rstd4'], ['mxt'], accum=True, tag=f'hmh1_{tc}')
        OP('v', lambda e: e.tensor_scalar(out=mxt[:, :], in0=mxt[:, :],
                                          scalar1=EPS_Q, scalar2=None, op0=ALU.max),
           ['mxt'], ['mxt'], accum=True, tag=f'hmh2_{tc}')
        OP('v', lambda e: e.reciprocal(out=ct4[:, :], in_=mxt[:, :]),
           ['mxt'], ['ct4'], tag=f'hsh{tc}')
        OP('v', lambda e: e.tensor_tensor(out=ct4[:, :], in0=ct4[:, :],
                                          in1=rstd4[:, :], op=ALU.mult),
           ['ct4', 'rstd4'], ['ct4'], accum=True, tag=f'hct{tc}')
        OP('t', lambda e: e.transpose(out=psX[0:4, 0:128], in_=ct4[:, :],
                                      identity=identf[:, :]),
           ['ct4', 'identf', 'psX'], ['psX'], tag=f'hctp{tc}')
        OP('v', lambda e: e.tensor_copy(out=sx4[:, :], in_=psX[0:4, 0:128]),
           ['psX'], ['sx4'], tag=f'hc4_{tc}')
        OP('t', lambda e: e.transpose(out=psX[0:4, 0:128], in_=mxt[:, :],
                                      identity=identf[:, :]),
           ['mxt', 'identf', 'psX'], ['psX'], tag=f'hmtp{tc}')
        OP('v', lambda e: e.tensor_copy(out=mx4[:, :], in_=psX[0:4, 0:128]),
           ['psX'], ['mx4'], tag=f'hm4_{tc}')
        for tb in range(4):
            OP('t', lambda e, tb=tb: e.matmul(
                psX[:, tb * 128:(tb + 1) * 128],
                sel4_sb[:, tb * 128:(tb + 1) * 128], sx4[:, :],
                start=True, stop=True),
               ['sx4', 'sel4_sb', 'psX'], ['psX'], kind='c', accum=(tb > 0),
               tag=f'hbcr{tc}_{tb}')
        OP('v', lambda e: e.tensor_scalar(out=bc1[:, :], in0=psX[:, :],
                                          scalar1=127.0, scalar2=None, op0=ALU.mult),
           ['psX'], ['bc1'], tag=f'hbc1_{tc}')
        for tb in range(4):
            OP('t', lambda e, tb=tb: e.matmul(
                psX[:, tb * 128:(tb + 1) * 128],
                sel4_sb[:, tb * 128:(tb + 1) * 128], mx4[:, :],
                start=True, stop=True),
               ['mx4', 'sel4_sb', 'psX'], ['psX'], kind='c', accum=(tb > 0),
               tag=f'hbcm{tc}_{tb}')
        OP('v', lambda e: e.tensor_copy(out=bc2[:, :], in_=psX[:, :]),
           ['psX'], ['bc2'], tag=f'hbc2_{tc}')
        if upto == 'F1d' and tc == 0:
            dma('q', dbg2[:, :, :], hbuf1[:, 0:16, :],
                [f'hb1_{o}' for o in range(16)], ['dbgd2'], tag='dhpre')
        for ot in range(64):
            OP('v', lambda e, ot=ot: e.scalar_tensor_tensor(
                out=qt[:, :], in0=hbuf1[:, ot, :], scalar=lnw_sb[:, ot:ot + 1],
                in1=bc1[:, :], op0=ALU.mult, op1=ALU.mult),
               [f'hb1_{ot}', 'lnw_sb', 'bc1'], ['qt'], tag=f'qhm{tc}_{ot}')
            OP('v', lambda e, ot=ot: e.tensor_scalar(
                out=hbuf1[:, ot, :], in0=qt[:, :], scalar1=MAGIC, scalar2=MAGIC,
                op0=ALU.add, op1=ALU.subtract),
               ['qt'], [f'hb1_{ot}'], tag=f'qhr{tc}_{ot}')
        if upto == 'F1d' and tc == 0:
            dma('q', dbg3[:, :, :], hbuf1[:, 0:16, :],
                [f'hb1_{o}' for o in range(16)], ['dbgd3'], tag='dhpost')
        for ot in range(16):
            rnk, ot2 = ot // 2, ot % 2
            bank, pk = psG[ot % 2], f'psG{ot % 2}'
            for half in range(4):
                dw = qdb[half % 2]
                dma('q', dw[:, :], qd_o[rnk * 128:(rnk + 1) * 128,
                                        ot2 * 8192 + half * 2048:
                                        ot2 * 8192 + (half + 1) * 2048],
                    ['qAGd'], [f'qdb{half % 2}'], tag=f'ldqd{tc}_{ot}_{half}')
                for kt in range(16):
                    gk = half * 16 + kt
                    OP('t', lambda e, kt=kt, gk=gk, bank=bank, dw=dw,
                       first=(gk == 0), last=(gk == 63): e.matmul(
                        bank[:, :], dw[:, kt * 128:(kt + 1) * 128],
                        hbuf1[:, gk, :], start=first, stop=last),
                       [f'qdb{half % 2}', f'hb1_{gk}'], [pk], kind='c', accum=(gk > 0),
                       tag=f'dmm{tc}_{ot}_{gk}')
            OP('v', lambda e, bank=bank: e.scalar_tensor_tensor(
                out=qt[:, :], in0=bank[:, :], scalar=scol[:, 2:3], in1=bc2[:, :],
                op0=ALU.mult, op1=ALU.mult),
               [pk, 'scol', 'bc2'], ['qt'], tag=f'dev{tc}_{ot}')
            OP('v', lambda e, ot=ot, tc=tc: e.tensor_tensor(
                out=rbuf[:, ot, :], in0=qt[:, :],
                in1=stream[:, ot, tc * TC:(tc + 1) * TC], op=ALU.add),
               ['qt', f'stream_{ot}'], ['rB'], accum=(ot > 0), tag=f'drs{tc}_{ot}')

        def out_fn(kt, tc=tc):
            OP('v', lambda e, kt=kt: e.tensor_tensor(
                out=rbuf[:, kt, :], in0=rbuf[:, kt, :], in1=psM[:, :],
                op=ALU.mult),
               ['rB', 'ln4_psm'], ['rB'], accum=True, tag=f'ln4o{tc}_{kt}')
        emit_ln('ln4', 'rB', rbuf, out_fn, lnps)
        for tb in range(4):
            for hh in range(4):
                for kt in range(4):
                    gkt = hh * 4 + kt
                    OP('t', lambda e, gkt=gkt, tb=tb: e.transpose(
                        out=psX[:, 0:128],
                        in_=rbuf[:, gkt, tb * 128:(tb + 1) * 128],
                        identity=identf[:, :]),
                       ['rB', 'identf'], ['psX'], kind='c', tag=f'tp{tc}_{tb}_{gkt}')
                    OP('v', lambda e, kt=kt: e.tensor_copy(
                        out=obuf[:, kt, :], in_=psX[:, 0:128]),
                       ['psX'], ['obuf'], accum=(kt > 0), tag=f'tpc{tc}_{tb}_{gkt}')
                okey = f'outst{tc}_{tb}_{hh}'
                dma('q', out_t[(tc * 4 + tb) * 128:(tc * 4 + tb + 1) * 128,
                               hh * 512:(hh + 1) * 512]
                    .rearrange("p (kt c) -> p kt c", c=128),
                    obuf[:, :, :], ['obuf'], [okey], tag=okey)
                outks.append(okey)

    if upto == 'F1d':
        outks += ['dbgd1', 'dbgd2', 'dbgd3', 'dbgd4a', 'dbgd4b', 'dbgd4c',
                  'dbgd4d', 'dbgd4e', 'dbgd5']
    return finish(outks)



def kernel(**inputs):
    in_maps = stage_inputs(inputs)
    nc = build_program()
    res = run_bass_kernel_spmd(nc, in_maps, core_ids=list(range(NCORE)))
    out = np.empty((B, T, D), np.float32)
    for c in range(NCORE):
        g, j = c // 4, c % 4
        ch0, ch1 = own_chunks(j)
        o = np.asarray(res.results[c]['out'])
        out[g, ch0 * TC:(ch0 + 1) * TC] = o[:TC]
        out[g, ch1 * TC:(ch1 + 1) * TC] = o[TC:]
    return out


# version 08
# speedup vs baseline: 1.2095x; 1.2095x over previous
import contextlib
import numpy as np
import concourse.bass as bass
import concourse.mybir as mybir
from concourse.bass import ts as dslice
from concourse.bass_utils import run_bass_kernel_spmd

F32 = mybir.dt.float32
BF16 = mybir.dt.bfloat16
I32 = mybir.dt.int32
AF = mybir.ActivationFunctionType
ALU = mybir.AluOpType
AXF = mybir.AxisListType.X

B, T, D = 2, 4096, 2048
DI, DS, DCONV, DTR = 4096, 16, 4, 128
I0, I1 = 5632, 8192
H, HD, G, GRP = 32, 64, 8, 4
NCORE = 8
TC = 512
NCH = T // TC
DSL = DI // 4
EPS_LN = 1e-5
EPS_Q = 1e-5
RMS_EPS = 1e-6
MAGIC = 12582912.0
NEGB = -60.0
RG4 = [[0, 1, 2, 3], [4, 5, 6, 7]]
RG8 = [[0, 1, 2, 3, 4, 5, 6, 7]]

def own_chunks(j):
    return (j, NCH - 1 - j)

def chunk_pos(c):
    return (c, 0) if c < 4 else (7 - c, 1)



class Prog:
    NDMASEM = 8

    def __init__(self):
        self.ops = []
        self.last_write = {}
        self.readers = {}
        self.fence_deps = []

    def op(self, eng, fn, reads=(), writes=(), kind='c', accum=False, tag=''):
        deps = set(self.fence_deps)
        for k in reads:
            deps.update(self.last_write.get(k, ()))
        for k in writes:
            if not accum:
                deps.update(self.last_write.get(k, ()))
            deps.update(self.readers.get(k, ()))
        i = len(self.ops)
        deps.discard(i)
        self.ops.append(dict(eng=eng, fn=fn, deps=deps, kind=kind, tag=tag,
                             inc=False))
        for k in reads:
            self.readers.setdefault(k, []).append(i)
        for k in writes:
            if accum and k in self.last_write:
                self.last_write[k].append(i)
            else:
                self.last_write[k] = [i]
                self.readers[k] = []
        return i

    def fence(self):
        last = {}
        for i, o in enumerate(self.ops):
            last[(o['eng'], o['kind'])] = i
        self.fence_deps = sorted(last.values())

    def emit(self, nc, block):
        ops = self.ops
        for o in ops:
            for d in o['deps']:
                ops[d]['inc'] = True
        engs = sorted({(o['eng'], o['kind']) for o in ops})
        sems = {}
        self._sem_cms = []
        for e, k in engs:
            if k == 'c':
                cm = nc.semaphore(f"sem_{e}_{k}")
                s = cm.__enter__(); self._sem_cms.append(cm)
                sems[(e, k)] = [s]
            elif k == 'd':
                lst = []
                for n in range(self.NDMASEM):
                    cm = nc.semaphore(f"sem_{e}_{k}{n}")
                    lst.append(cm.__enter__()); self._sem_cms.append(cm)
                sems[(e, k)] = lst
            else:
                lst = []
                for n in range(2):
                    cm = nc.semaphore(f"sem_{e}_cc{n}")
                    lst.append(cm.__enter__()); self._sem_cms.append(cm)
                sems[(e, k)] = lst
        counts = {}
        for i, o in enumerate(ops):
            key = (o['eng'], o['kind'])
            pool = sems[key]
            n = counts.get(key, 0)
            counts[key] = n + 1
            if o['kind'] == 'c':
                o['sem'] = pool[0]
                o['val'] = None
            else:
                o['sem'] = pool[n % len(pool)]
                o['use'] = n // len(pool)
                o['step'] = 16 if o['kind'] == 'd' else 1
        ccount = {}
        for o in ops:
            if o['kind'] == 'c':
                key = (o['eng'], o['kind'])
                c = ccount.get(key, 0)
                if o['inc']:
                    c += 1
                ccount[key] = c
                o['val'] = c
        for o in ops:
            if o['kind'] != 'c':
                o['inc'] = True
                o['val'] = (o['use'] + 1) * o['step']
        stream_clock = {}
        for i, o in enumerate(ops):
            skey = o['eng']
            clk = stream_clock.setdefault(skey, {})
            waits = []
            for d in sorted(o['deps']):
                od = ops[d]
                sem, val = od['sem'], od['val']
                if od['kind'] == 'c' and val == 0:
                    continue
                if clk.get(id(sem), -1) >= val:
                    continue
                waits.append((sem, val, d))
                clk[id(sem)] = val
            if o['kind'] != 'c' and o['use'] > 0:
                prev_val = o['use'] * o['step']
                if clk.get(id(o['sem']), -1) < prev_val:
                    waits.append((o['sem'], prev_val, -1))
                    clk[id(o['sem'])] = prev_val
            o['waits'] = waits
            for (_, _, d) in waits:
                if d >= 0:
                    pclk = ops[d].get('clock')
                    if pclk:
                        for sid, v in pclk.items():
                            if clk.get(sid, -1) < v:
                                clk[sid] = v
            o['clock'] = dict(clk)
        by_eng = {}
        for o in ops:
            by_eng.setdefault(o['eng'], []).append(o)
        emap = {'t': block.tensor, 'v': block.vector, 's': block.scalar,
                'g': block.gpsimd, 'q': block.sync}
        for ename, elist in by_eng.items():
            def make(elist_):
                def body(eng):
                    for o in elist_:
                        for (sem, val, _) in o['waits']:
                            eng.wait_ge(sem, val)
                        instr = o['fn'](eng)
                        if o['inc']:
                            if o['kind'] == 'c':
                                instr.then_inc(o['sem'], 1)
                            else:
                                instr.then_inc(o['sem'], 16 if o['kind'] == 'd' else 1)
                return body
            emap[ename](make(elist))



def stage_inputs(inp):
    f32 = np.float32
    asf = lambda a: np.asarray(a, f32)

    x = asf(inp['L0_attn_x'])
    xT = [np.ascontiguousarray(x[g].T) for g in range(B)]

    winT = np.ascontiguousarray(asf(inp['L0_attn_W_in']).T)
    wxT = np.ascontiguousarray(asf(inp['L0_attn_W_x']).T)
    wdt = asf(inp['L0_attn_W_dt'])
    woutT = np.ascontiguousarray(asf(inp['L0_attn_W_out']).T)
    conv_w = asf(inp['L0_attn_conv_w'])
    conv_b = asf(inp['L0_attn_conv_b'])
    b_dt = asf(inp['L0_attn_b_dt'])
    A_log = asf(inp['L0_attn_A_log'])
    Dvec = asf(inp['L0_attn_D'])

    g0T = np.ascontiguousarray(asf(inp['L0_ffn_gate_w']).T)
    u0T = np.ascontiguousarray(asf(inp['L0_ffn_up_w']).T)
    d0T = np.ascontiguousarray(asf(inp['L0_ffn_down_w']).T)
    g0s = asf(inp['L0_ffn_gate_s'])
    u0s = asf(inp['L0_ffn_up_s'])
    d0s = asf(inp['L0_ffn_down_s'])

    wq = asf(inp['L1_attn_Wq'])
    wk = asf(inp['L1_attn_Wk'])
    wv = asf(inp['L1_attn_Wv'])
    wo = asf(inp['L1_attn_Wo'])

    g1T = np.ascontiguousarray(asf(inp['L1_ffn_w_gate']).T)
    u1T = np.ascontiguousarray(asf(inp['L1_ffn_w_up']).T)
    d1T = np.ascontiguousarray(asf(inp['L1_ffn_w_down']).T)
    lnw = asf(inp['L1_ffn_ln_weight'])

    def lhsT_stage(wT, K, O):
        t = wT.reshape(K // 128, 128, O // 128, 128)
        return np.ascontiguousarray(t.transpose(2, 1, 0, 3).reshape(O // 128, 128, K))

    def win_stage(j):
        cols = np.concatenate([winT[:, j * DSL:(j + 1) * DSL],
                               winT[:, DI + j * DSL:DI + (j + 1) * DSL]], axis=1)
        return np.ascontiguousarray(cols.reshape(16, 128, 2048))

    half = HD // 2
    inv_freq = (1.0 / (10000.0 ** (np.arange(half, dtype=np.float64) * 2.0 / HD)))
    ang = np.arange(T, dtype=np.float64)[:, None] * inv_freq[None, :]
    cosT = np.cos(ang).astype(f32).T
    sinT = np.sin(ang).astype(f32).T
    rc = np.concatenate([cosT, cosT, cosT, cosT], 0)
    rs = np.concatenate([-sinT, sinT, -sinT, sinT], 0)

    perm = np.zeros((128, 128), f32)
    for m in range(128):
        perm[m ^ 32, m] = 1.0
    ident = np.eye(128, dtype=f32)
    selb = np.zeros((32, 32, 128), f32)
    for r in range(32):
        selb[r, r, :] = 1.0
    selb = np.ascontiguousarray(selb.transpose(1, 0, 2).reshape(32, 32 * 128))
    sel4 = np.zeros((4, 4, 128), f32)
    for r in range(4):
        sel4[r, r, :] = 1.0
    sel4 = np.ascontiguousarray(sel4.transpose(1, 0, 2).reshape(4, 4 * 128))

    kk = np.arange(128)[:, None]
    qq = np.arange(TC)[None, :]
    tri = np.stack([(128 * o + kk <= qq).astype(f32) for o in range(4)])

    in_maps = []
    for c in range(NCORE):
        g, j = c // 4, c % 4
        sl = slice(j * DSL, (j + 1) * DSL)
        ch0, ch1 = own_chunks(j)
        own_t = np.r_[ch0 * TC:(ch0 + 1) * TC, ch1 * TC:(ch1 + 1) * TC]

        qcols = []
        for t4 in range(4):
            qcols.append(np.arange((4 * ch0 + t4) * 64, (4 * ch0 + t4 + 1) * 64))
            qcols.append(np.arange((4 * ch1 + t4) * 64, (4 * ch1 + t4 + 1) * 64))
        qcols = np.concatenate(qcols)
        kvcols = np.r_[ch0 * 64:(ch0 + 1) * 64, ch1 * 64:(ch1 + 1) * 64]
        wq_sl = np.ascontiguousarray(wq[:, qcols])
        wk_sl = np.ascontiguousarray(wk[:, kvcols])
        wv_sl = np.ascontiguousarray(wv[:, kvcols])

        def q_stage(wT, OTs):
            t = lhsT_stage(wT, wT.shape[0], wT.shape[1])[OTs]
            return np.ascontiguousarray(t.transpose(1, 0, 2).reshape(128, -1))

        xo = x[g][own_t].T
        m = {
            'xT': xT[g],
            'xown': np.ascontiguousarray(xo.reshape(16, 128, 1024).transpose(1, 0, 2)),
            'winT': win_stage(j),
            'wx': np.ascontiguousarray(wxT[sl].reshape(8, 128, 160).transpose(1, 0, 2)),
            'wdt': np.ascontiguousarray(
                wdt[sl].reshape(8, 128, DTR).transpose(2, 0, 1)),
            'convw': np.ascontiguousarray(conv_w[sl].reshape(8, 128, 4).transpose(1, 0, 2)),
            'convb': np.ascontiguousarray(conv_b[sl].reshape(8, 128).T)[:, :, None],
            'bdt': np.ascontiguousarray(b_dt[sl].reshape(8, 128).T)[:, :, None],
            'alog': np.ascontiguousarray(A_log[sl].reshape(8, 128, DS).transpose(1, 0, 2)),
            'dvec': np.ascontiguousarray(Dvec[sl].reshape(8, 128).T)[:, :, None],
            'wout': lhsT_stage(woutT, DI, D),
            'g0': lhsT_stage(g0T, D, I0),
            'u0': lhsT_stage(u0T, D, I0),
            'd0': lhsT_stage(d0T, I0, D),
            'g0s': np.ascontiguousarray(g0s.reshape(44, 128).T),
            'u0s': np.ascontiguousarray(u0s.reshape(44, 128).T),
            'd0s': np.ascontiguousarray(d0s.reshape(16, 128).T),
            'wqh': lhsT_stage(wq_sl, D, 512),
            'wkh': lhsT_stage(wk_sl, D, 128)[0],
            'wo': lhsT_stage(wo, D, D),
            'wvh': np.ascontiguousarray(
                wv_sl.reshape(16, 128, 128).transpose(1, 0, 2)),
            'qg': q_stage(g1T, list(range(8 * c, 8 * c + 8))),
            'qu': q_stage(u1T, list(range(8 * c, 8 * c + 8))),
            'qd': q_stage(d1T, [2 * c, 2 * c + 1]),
            'lnw': np.ascontiguousarray(lnw.reshape(64, 128).T),
            'rc': rc,
            'rsn': rs,
            'perm': perm,
            'ident': ident,
            'selb': selb,
            'tri': tri,
            'sel4': sel4,
        }
        in_maps.append(m)
    return in_maps



def build_program(upto='full'):
    nc = bass.Bass("TRN2", target_bir_lowering=False, debug=False)
    p = Prog()
    OP = p.op

    def din(name, shape):
        return nc.dram_tensor(name, list(shape), F32, kind="ExternalInput").ap()

    xT = din('xT', [D, T])
    xown_d = din('xown', [128, 16, 1024])
    winT = din('winT', [16, 128, 2048])
    wx = din('wx', [128, 8, 160])
    wdt = din('wdt', [128, 8, 128])
    convw = din('convw', [128, 8, 4])
    convb = din('convb', [128, 8, 1])
    bdt = din('bdt', [128, 8, 1])
    alog = din('alog', [128, 8, DS])
    dvec = din('dvec', [128, 8, 1])
    wout_d = din('wout', [16, 128, 4096])
    g0_d = din('g0', [44, 128, 2048]); u0_d = din('u0', [44, 128, 2048])
    d0_d = din('d0', [16, 128, 5632])
    g0s_d = din('g0s', [128, 44]); u0s_d = din('u0s', [128, 44]); d0s_d = din('d0s', [128, 16])
    wqh_d = din('wqh', [4, 128, 2048]); wkh_d = din('wkh', [128, 2048])
    wo_d = din('wo', [16, 128, 2048]); wvh_d = din('wvh', [128, 16, 128])
    qg_d = din('qg', [128, 8 * 2048]); qu_d = din('qu', [128, 8 * 2048])
    qd_d = din('qd', [128, 2 * 8192])
    lnw_d = din('lnw', [128, 64])
    rc_d = din('rc', [128, T]); rs_d = din('rsn', [128, T])
    perm_d = din('perm', [128, 128])
    selb_d = din('selb', [32, 32 * 128])
    ident_d = din('ident', [128, 128])
    tri_d = din('tri', [4, 128, TC])
    sel4_d = din('sel4', [4, 4 * 128])

    ar_in = [nc.dram_tensor(f'ar_in{i}', [160, TC], BF16).ap() for i in range(NCH)]
    ar_out = [nc.dram_tensor(f'ar_out{i}', [160, TC], BF16).ap() for i in range(NCH)]
    yag_in = [nc.dram_tensor(f'yag_in{i}', [DSL, TC], BF16).ap() for i in range(NCH)]
    yag_all = nc.dram_tensor('yag_all', [NCH, DI, TC], BF16).ap()
    qs_in = nc.dram_tensor('qs_in', [1, 4], F32).ap()
    qs_out = nc.dram_tensor('qs_out', [1, 4], F32).ap()
    qg_i = nc.dram_tensor('qg_i', [128, 16384], BF16).ap()
    qu_i = nc.dram_tensor('qu_i', [128, 16384], BF16).ap()
    qd_i = nc.dram_tensor('qd_i', [128, 16384], BF16).ap()
    qg_o = nc.dram_tensor('qg_o', [1024, 16384], BF16).ap()
    qu_o = nc.dram_tensor('qu_o', [1024, 16384], BF16).ap()
    qd_o = nc.dram_tensor('qd_o', [1024, 16384], BF16).ap()
    sag_i = nc.dram_tensor('sag_i', [D, 1024], BF16).ap()
    sag_o = nc.dram_tensor('sag_o', [4 * D, 1024], BF16).ap()

    out_t = nc.dram_tensor('out', [1024, D], F32, kind="ExternalOutput").ap()
    dbg = None
    if upto == 'F1d':
        dbg = nc.dram_tensor('dbg', [128, 16, TC], BF16, kind="ExternalOutput").ap()
        dbg2 = nc.dram_tensor('dbg2', [128, 16, TC], BF16, kind="ExternalOutput").ap()
        dbg3 = nc.dram_tensor('dbg3', [128, 16, TC], BF16, kind="ExternalOutput").ap()
        dbg4 = nc.dram_tensor('dbg4', [128, 2048], F32, kind="ExternalOutput").ap()
        dbg5 = nc.dram_tensor('dbg5', [128, 2048], BF16, kind="ExternalOutput").ap()
    if upto == 'M':
        dbg = nc.dram_tensor('dbg', [DI, 1024], BF16, kind="ExternalOutput").ap()
    elif upto not in ('full', 'F1d'):
        dbg = nc.dram_tensor('dbg', [128, 16, 1024], BF16, kind="ExternalOutput").ap()
        if upto == 'A2':
            dbgq = nc.dram_tensor('dbgq', [128, 16, 1024], BF16, kind="ExternalOutput").ap()
            dbgk = nc.dram_tensor('dbgk', [128, 16, 1024], BF16, kind="ExternalOutput").ap()
            dbgv = nc.dram_tensor('dbgv', [128, 32, 520], BF16, kind="ExternalOutput").ap()

    stack = contextlib.ExitStack()

    def sb(name, shape, dt):
        return stack.enter_context(nc.sbuf_tensor(name, list(shape), dt))

    open_arenas = []

    def mk_arena():
        st = contextlib.ExitStack()
        def sba(name, shape, dt):
            return st.enter_context(nc.sbuf_tensor(name, list(shape), dt))
        def psa(name, shape):
            return st.enter_context(nc.psum_tensor(name, list(shape), F32))
        class A:
            def close(self):
                if st in open_arenas:
                    open_arenas.remove(st)
                st.close()
        open_arenas.append(st)
        return A(), sba, psa

    def dma(q, out, in_, reads, writes, tag='', accum=False):
        return OP(q, lambda e, o=out, i=in_: e.dma_start(out=o, in_=i),
                  reads=reads, writes=writes, kind='d', tag=tag, accum=accum)

    identf = sb('identf', [128, 128], F32)
    identb = sb('identb', [128, 128], BF16)
    onesb = sb('onesb', [1, 128], BF16)
    onescolb = sb('onescolb', [128, 1], BF16)
    onesrowF = sb('onesrowF', [1, 128], F32)
    ones127 = sb('ones127', [1, 128], F32)
    onescolF = sb('onescolF', [128, 1], F32)
    swcol = sb('swcol', [128, 4], F32)
    mcol = sb('mcol', [128, 4], F32)
    qrows = sb('qrows', [1, 16], F32)
    epsc = sb('epsc', [1, 2], F32)

    dma('q', identf[:, :], ident_d[:, :], ['identd'], ['identf'])
    OP('v', lambda e: e.tensor_copy(out=identb[:, :], in_=identf[:, :]),
       ['identf'], ['identb'])
    OP('v', lambda e: e.memset(onesb[:, :], 1.0), [], ['onesb'])
    OP('v', lambda e: e.memset(onescolb[:, :], 1.0), [], ['onescolb'])
    OP('v', lambda e: e.memset(onesrowF[:, :], 1.0), [], ['onesrowF'])
    OP('v', lambda e: e.memset(ones127[:, :], 127.0), [], ['ones127'])
    OP('v', lambda e: e.memset(onescolF[:, :], 1.0), [], ['onescolF'])
    OP('v', lambda e: e.memset(epsc[0:1, 0:1], EPS_LN), [], ['epsc'], accum=True)
    OP('v', lambda e: e.memset(epsc[0:1, 1:2], RMS_EPS), [], ['epsc'], accum=True)

    a1, sb1, ps1 = mk_arena()

    qbuf = [sb1('qbuf0', [128, 512], F32)] * 2
    qout = sb1('qout', [128, 512], BF16)
    qint = sb1('qint', [128, 512], F32)
    qred = sb1('qred', [128, 1], F32)
    qacc = sb1('qacc', [128, 4], F32)

    winT_sb = sb1('winT_sb', [128, 16, 2048], BF16)
    wdt_sb = sb1('wdt_sb', [128, 8, 128], BF16)
    wx_sb = sb1('wx_sb', [128, 8, 160], BF16)
    convw_sb = sb1('convw_sb', [128, 8, 4], F32)
    convb_sb = sb1('convb_sb', [128, 8, 1], F32)
    bdt_sb = sb1('bdt_sb', [128, 8, 1], F32)
    dvec_sb = sb1('dvec_sb', [128, 8, 1], F32)
    aneg = sb1('aneg', [128, 8, DS], F32)
    state = sb1('state', [128, 8, DS], F32)
    xbf = sb1('xbf', [128, 16, TC], BF16)
    xse = sb1('xse', [128, 8, TC + 3], BF16)
    tails = sb1('tails', [128, 8, 3], BF16)
    zbuf = [sb1(f'zb{i}', [128, 8, TC], BF16) for i in range(2)]
    ubuf = [sb1(f'ub{i}', [128, 8, TC], BF16) for i in range(2)]
    dtbuf = [sb1(f'dtb{i}', [128, 8, TC], BF16) for i in range(2)]
    vtile = [sb1('vt0', [128, TC], BF16)] * 2
    dtrb = [sb1('dtrb0', [128, TC], BF16)] * 2
    bcb = [sb1('bcb0', [32, TC], BF16)] * 2
    bca = sb1('bca', [128, 2 * DS, TC], BF16)
    selb_sb = sb1('selb_sb', [32, 32, 128], BF16)
    xdl_dt = sb1('xdl_dt', [128, TC], BF16)
    xdl_bc = sb1('xdl_bc', [32, TC], BF16)
    convt = [sb1('convt0', [128, TC], BF16)] * 2
    dAb = [sb1(f'dAb{i}', [128, TC], BF16) for i in range(3)]
    dBub = [sb1(f'dBub{i}', [128, TC], BF16) for i in range(3)]
    hb = [sb1(f'hb{i}', [128, TC], BF16) for i in range(2)]
    hcb = [sb1(f'hcb{i}', [128, TC], BF16) for i in range(2)]
    ygb = sb1('ygb', [128, 1, TC], BF16)
    yev = sb1('yev', [128, TC], BF16)

    ps_front = [ps1('ps_front0', [128, 512]), ps1('ps_front1', [128, 512])]
    ps_bc = [ps1('ps_bc0', [128, 512]), ps1('ps_bc1', [128, 512])]
    ps_y = [ps1('ps_y0', [128, 512]), ps1('ps_y1', [128, 512])]
    ps_misc = [ps1('ps_misc0', [128, 512]), ps1('ps_misc1', [128, 512])]

    OP('v', lambda e: e.memset(qacc[:, :], 0.0), [], ['qacc'])
    qsrcs = [(qg_d, qg_i, 0), (qu_d, qu_i, 1), (qd_d, qd_i, 2)]
    NQT = 32
    for (src, _, m) in qsrcs:
        for t in range(NQT):
            b = t % 2
            dma('q', qbuf[b][:, :], src[:, t * 512:(t + 1) * 512],
                [f'q_src{m}'], ['qbuf0'], tag=f'qld{m}_{t}')
            OP('s', lambda e, b=b: e.activation(
                out=qout[:, :], in_=qbuf[b][:, :], func=AF.Abs,
                accum_out=qred[:, 0:1]),
               ['qbuf0'], ['qred', 'qout'], tag=f'qsum{m}_{t}')
            OP('v', lambda e, m=m: e.tensor_tensor(
                out=qacc[:, m:m + 1], in0=qacc[:, m:m + 1], in1=qred[:, 0:1],
                op=ALU.add),
               ['qred', 'qacc'], ['qacc'], accum=True, tag=f'qadd{m}_{t}')
    OP('t', lambda e: e.matmul(ps_misc[0][0:1, 0:4], onescolF[:, :], qacc[:, :],
                               start=True, stop=True),
       ['onescolF', 'qacc'], ['psm0'], tag='qcross')
    OP('v', lambda e: e.tensor_copy(out=qrows[:, 0:4], in_=ps_misc[0][0:1, 0:4]),
       ['psm0'], ['qrows03'], tag='qrow')
    dma('q', qs_in[:, :], qrows[:, 0:4], ['qrows03'], ['qs_in_d'])
    OP('g', lambda e: e.collective_compute(
        "AllReduce", ALU.add, replica_groups=RG8,
        ins=[qs_in.opt()], outs=[qs_out.opt()]),
       ['qs_in_d'], ['qs_out_d'], kind='cc', tag='qAR')
    dma('q', qrows[:, 4:8], qs_out[:, :], ['qs_out_d'], ['qrows47'])
    OP('v', lambda e: e.tensor_scalar(
        out=qrows[:, 8:12], in0=qrows[:, 4:8], scalar1=1.0 / 16777216.0,
        scalar2=EPS_Q, op0=ALU.mult, op1=ALU.max),
       ['qrows47'], ['qrows8'], tag='qmean')
    OP('v', lambda e: e.reciprocal(out=qrows[:, 12:16], in_=qrows[:, 8:12]),
       ['qrows8'], ['qrows12'], tag='qrecip')
    OP('t', lambda e: e.matmul(ps_misc[0][:, 4:8], onesrowF[:, :], qrows[:, 8:12],
                               start=True, stop=True),
       ['onesrowF', 'qrows8'], ['psm0'], tag='qbc1')
    OP('v', lambda e: e.tensor_copy(out=mcol[:, :], in_=ps_misc[0][:, 4:8]),
       ['psm0'], ['mcol'], tag='qbc1c')
    OP('t', lambda e: e.matmul(ps_misc[0][:, 8:12], onesrowF[:, :], qrows[:, 12:16],
                               start=True, stop=True),
       ['onesrowF', 'qrows12'], ['psm0'], tag='qbc2')
    OP('v', lambda e: e.tensor_copy(out=swcol[:, :], in_=ps_misc[0][:, 8:12]),
       ['psm0'], ['swcol'], tag='qbc2c')
    for (src, dst, m) in qsrcs:
        for t in range(NQT):
            b = t % 2
            dma('q', qbuf[b][:, :], src[:, t * 512:(t + 1) * 512],
                [f'q_src{m}'], ['qbuf0'], tag=f'qld2_{m}_{t}')
            OP('s', lambda e, b=b, m=m: e.activation(
                out=qint[:, :], in_=qbuf[b][:, :], func=AF.Copy,
                scale=swcol[:, m:m + 1]),
               ['qbuf0', 'swcol'], ['qint'], tag=f'qmul{m}_{t}')
            OP('v', lambda e: e.tensor_scalar(
                out=qint[:, :], in0=qint[:, :], scalar1=MAGIC, scalar2=MAGIC,
                op0=ALU.add, op1=ALU.subtract),
               ['qint'], ['qint'], accum=True, tag=f'qrnd{m}_{t}')
            OP('v', lambda e: e.tensor_scalar(
                out=qout[:, :], in0=qint[:, :], scalar1=1.0, scalar2=-1.0,
                op0=ALU.min, op1=ALU.max),
               ['qint'], ['qout'], tag=f'qclip{m}_{t}')
            dma('q', dst[:, t * 512:(t + 1) * 512], qout[:, :],
                ['qout'], [f'qst{m}'], tag=f'qst{m}_{t}', accum=(t > 0))

    for kc in range(16):
        dma('g', winT_sb[:, kc, :], winT[kc, :, :], ['winT_d'], [f'winsb{kc}'],
            tag=f'ldwin{kc}')
    dma('g', wdt_sb[:, :, :], wdt[:, :, :], ['wdt_d'], ['wdtsb'])
    dma('g', wx_sb[:, :, :], wx[:, :, :], ['wx_d'], ['wxsb'])
    dma('g', selb_sb[:, :, :],
        selb_d.rearrange("p (r c) -> p r c", c=128), ['selb_d'], ['selsb'])
    dma('q', convw_sb[:, :, :], convw[:, :, :], ['cw_d'], ['cwsb'])
    dma('q', convb_sb[:, :, :], convb[:, :, :], ['cb_d'], ['cbsb'])
    dma('q', bdt_sb[:, :, :], bdt[:, :, :], ['bdt_d'], ['bdtsb'])
    dma('q', dvec_sb[:, :, :], dvec[:, :, :], ['dv_d'], ['dvsb'])
    dma('q', aneg[:, :, :], alog[:, :, :], ['al_d'], ['alsb'])
    OP('s', lambda e: e.activation(out=aneg[:, :, :], in_=aneg[:, :, :], func=AF.Exp),
       ['alsb'], ['anegp'], accum=True)
    OP('v', lambda e: e.tensor_scalar(out=aneg[:, :, :], in0=aneg[:, :, :],
                                      scalar1=-1.0, scalar2=None, op0=ALU.mult),
       ['anegp'], ['aneg'], accum=True)
    OP('v', lambda e: e.memset(state[:, :, :], 0.0), [], ['state'])
    OP('v', lambda e: e.memset(xse[:, :, 0:3], 0.0), [], ['xse_h'])

    def front(tc):
        b = tc % 2
        if tc > 0:
            OP('v', lambda e: e.tensor_copy(
                out=tails[:, :, :], in_=xse[:, :, TC:TC + 3]),
               ['xse_x'], ['tails'], tag=f'tail{tc}')
        dma('g', xbf[:, :, :],
            xT.rearrange("(kc p) t -> p kc t", p=128)[:, :, tc * TC:(tc + 1) * TC],
            ['xT_d'], ['xbf'], tag=f'ldx{tc}')
        for o in range(16):
            bank = ps_front[o % 2]
            pk = f'psf{o % 2}'
            for k in range(16):
                OP('t', lambda e, o=o, k=k, bank=bank: e.matmul(
                    bank[:, :], winT_sb[:, k, o * 128:(o + 1) * 128],
                    xbf[:, k, :], start=(k == 0), stop=(k == 15)),
                   reads=[f'winsb{k}', 'xbf'], writes=[pk], kind='c',
                   accum=(k > 0), tag=f'proj{tc}_{o}')
            if o < 8:
                OP('v', lambda e, o=o, bank=bank: e.tensor_copy(
                    out=xse[:, o, 3:3 + TC], in_=bank[:, :]),
                   [pk, 'tails'], ['xse_x'], accum=(tc > 0 or o > 0), tag=f'evxs{tc}_{o}')
            else:
                OP('s', lambda e, o=o, bank=bank, b=b: e.activation(
                    out=zbuf[b][:, o - 8, :], in_=bank[:, :], func=AF.Silu),
                   [pk], [f'zb{b}_{o-8}'], tag=f'evz{tc}_{o}')
        if tc > 0:
            OP('v', lambda e: e.tensor_copy(
                out=xse[:, :, 0:3], in_=tails[:, :, :]),
               ['tails', 'xse_x'], ['xse_h'], accum=True, tag=f'head{tc}')
        for d in range(8):
            OP('v', lambda e, d=d: e.tensor_scalar(
                out=convt[0][:, :], in0=xse[:, d, 0:TC],
                scalar1=convw_sb[:, d, 0:1], scalar2=None, op0=ALU.mult),
               ['xse_x', 'xse_h', 'cwsb'], ['cvt0'], tag=f'cv{tc}_{d}_0')
            for k in (1, 2, 3):
                OP('v', lambda e, d=d, k=k: e.scalar_tensor_tensor(
                    out=convt[0][:, :], in0=xse[:, d, k:k + TC],
                    scalar=convw_sb[:, d, k:k + 1], in1=convt[0][:, :],
                    op0=ALU.mult, op1=ALU.add),
                   ['xse_x', 'xse_h', 'cvt0', 'cwsb'], ['cvt0'], accum=True,
                   tag=f'cv{tc}_{d}_{k}')
            OP('s', lambda e, d=d, b=b: e.activation(
                out=ubuf[b][:, d, :], in_=convt[0][:, :], func=AF.Silu,
                bias=convb_sb[:, d, 0:1]),
               ['cvt0', 'cbsb'], [f'ub{b}_{d}'], tag=f'silu_u{tc}_{d}')
        for k in range(8):
            OP('t', lambda e, k=k, b=b: e.matmul(
                ps_misc[0][:, :], wx_sb[:, k, 0:128], ubuf[b][:, k, :],
                start=(k == 0), stop=(k == 7)),
               [f'ub{b}_{k}', 'wxsb'], ['psm0'], kind='c', accum=(k > 0), tag=f'xdl{tc}')
        for k in range(8):
            OP('t', lambda e, k=k, b=b: e.matmul(
                ps_misc[1][0:32, :], wx_sb[:, k, 128:160], ubuf[b][:, k, :],
                start=(k == 0), stop=(k == 7)),
               [f'ub{b}_{k}', 'wxsb'], ['psm1'], kind='c', accum=(k > 0), tag=f'xdlbc{tc}')
        OP('v', lambda e: e.tensor_copy(out=xdl_dt[:, :], in_=ps_misc[0][:, :]),
           ['psm0'], ['xdl_dt'], tag=f'evxdl{tc}')
        OP('v', lambda e: e.tensor_copy(out=xdl_bc[:, :], in_=ps_misc[1][0:32, :]),
           ['psm1'], ['xdl_bc'], tag=f'evxdlbc{tc}')
        dma('q', ar_in[tc][0:128, :], xdl_dt[:, :], ['xdl_dt'], [f'arin{tc}a'])
        dma('q', ar_in[tc][128:160, :], xdl_bc[:, :], ['xdl_bc'], [f'arin{tc}b'])
        OP('g', lambda e, tc=tc: e.collective_compute(
            "AllReduce", ALU.add, replica_groups=RG4,
            ins=[ar_in[tc].opt()], outs=[ar_out[tc].opt()]),
           [f'arin{tc}a', f'arin{tc}b'], [f'arout{tc}'], kind='cc', tag=f'AR{tc}')
    def front_b(tc):
        b = tc % 2
        dma('q', dtrb[b][:, :], ar_out[tc][0:128, :], [f'arout{tc}'], ['dtrb0', 'dtrb1'])
        dma('q', bcb[b][:, :], ar_out[tc][128:160, :], [f'arout{tc}'], ['bcb0', 'bcb1'])
        for o in range(8):
            bank = ps_front[o % 2]
            OP('t', lambda e, o=o, bank=bank, b=b: e.matmul(
                bank[:, :], wdt_sb[:, o, :], dtrb[b][:, :], start=True, stop=True),
               [f'dtrb{b}', 'wdtsb'], [f'psf{o % 2}'], kind='c', tag=f'dtmm{tc}_{o}')
            OP('s', lambda e, o=o, bank=bank: e.activation(
                out=convt[0][:, :], in_=bank[:, :], func=AF.Exp,
                bias=bdt_sb[:, o, 0:1]),
               [f'psf{o % 2}', 'bdtsb', 'cvt0'], ['cvt0'], accum=True,
               tag=f'spe{tc}_{o}')
            OP('s', lambda e, o=o, b=b: e.activation(
                out=dtbuf[b][:, o, :], in_=convt[0][:, :], func=AF.Ln,
                bias=onescolF[:, 0:1]),
               ['cvt0', 'onescolF'], [f'dtb{b}_{o}'], tag=f'softp{tc}_{o}')

    def scanpart(tc):
        b = tc % 2
        for srow in range(32):
            bank = ps_bc[srow % 2]
            OP('t', lambda e, srow=srow, bank=bank, b=b: e.matmul(
                bank[:, :], selb_sb[:, srow, :], bcb[b][:, :],
                start=True, stop=True),
               [f'bcb{b}', 'selsb'], [f'psbc{srow % 2}'], kind='c', tag=f'bc{tc}_{srow}')
            OP('s', lambda e, srow=srow, bank=bank: e.copy(
                out=bca[:, srow, :], in_=bank[:, :]),
               [f'psbc{srow % 2}'], [f'bca_{srow}'], tag=f'bcev{tc}_{srow}')
        for d in range(8):
            OP('v', lambda e, d=d, b=b: e.tensor_tensor(
                out=vtile[d % 2][:, :], in0=dtbuf[b][:, d, :], in1=ubuf[b][:, d, :],
                op=ALU.mult),
               [f'dtb{b}_{d}', f'ub{b}_{d}'], ['vt0', 'vt1'], tag=f'vmul{tc}_{d}')
            ybank = ps_y[d % 2]
            ypk = f'psy{d % 2}'
            for s in range(DS):
                i = s % 3
                OP('s', lambda e, d=d, s=s, i=i, b=b: e.activation(
                    out=dAb[i][:, :], in_=dtbuf[b][:, d, :], func=AF.Exp,
                    scale=aneg[:, d, s:s + 1]),
                   [f'dtb{b}_{d}', 'aneg'], [f'dA{i}'], tag=f'dA{tc}_{d}_{s}')
                OP('g', lambda e, d=d, s=s, i=i: e.tensor_tensor(
                    out=dBub[i][:, :], in0=vtile[d % 2][:, :],
                    in1=bca[:, s, :], op=ALU.mult),
                   [f'vt{d % 2}', f'bca_{s}'], [f'dBu{i}'], tag=f'dBu{tc}_{d}_{s}')
                ih = s % 2
                OP('v', lambda e, d=d, s=s, i=i, ih=ih: e.tensor_tensor_scan(
                    out=hb[ih][:, :], data0=dAb[i][:, :], data1=dBub[i][:, :],
                    initial=state[:, d, s:s + 1], op0=ALU.mult, op1=ALU.add),
                   [f'dA{i}', f'dBu{i}', 'state'], [f'hb{ih}'], tag=f'scan{tc}_{d}_{s}')
                OP('s', lambda e, d=d, s=s, ih=ih: e.copy(
                    out=state[:, d, s:s + 1], in_=hb[ih][:, TC - 1:TC]),
                   [f'hb{ih}'], ['state'], accum=True, tag=f'stup{tc}_{d}_{s}')
                eng_hc = 'v' if s % 3 else 'g'
                i2 = s % 2
                OP(eng_hc, lambda e, d=d, s=s, ih=ih, i2=i2: e.tensor_tensor(
                    out=hcb[i2][:, :], in0=hb[ih][:, :], in1=bca[:, DS + s, :],
                    op=ALU.mult),
                   [f'hb{ih}', f'bca_{DS+s}'], [f'hcb{i2}'], tag=f'ymul{tc}_{d}_{s}')
                OP('t', lambda e, d=d, s=s, i2=i2, ybank=ybank: e.matmul(
                    ybank[:, :], identb[:, :], hcb[i2][:, :],
                    start=(s == 0), stop=(s == DS - 1)),
                   [f'hcb{i2}', 'identb'], [ypk], kind='c', accum=(s > 0),
                   tag=f'yacc{tc}_{d}_{s}')
            OP('v', lambda e, d=d, b=b, ybank=ybank: e.scalar_tensor_tensor(
                out=yev[:, :], in0=ubuf[b][:, d, :], scalar=dvec_sb[:, d, 0:1],
                in1=ybank[:, :], op0=ALU.mult, op1=ALU.add),
               [f'ub{b}_{d}', ypk, 'dvsb'], ['yev'], tag=f'yg1{tc}_{d}')
            OP('v', lambda e, d=d, b=b: e.tensor_tensor(
                out=ygb[:, 0, :], in0=yev[:, :], in1=zbuf[b][:, d, :],
                op=ALU.mult),
               ['yev', f'zb{b}_{d}'], ['ygb_0'], tag=f'yg2{tc}_{d}')
            dma('q', yag_in[tc][d * 128:(d + 1) * 128, :], ygb[:, 0, :],
                ['ygb_0'], [f'ygst{tc}_{d}'], tag=f'styg{tc}_{d}')
        OP('g', lambda e, tc=tc: e.collective_compute(
            "AllGather", ALU.bypass, replica_groups=RG4,
            ins=[yag_in[tc].opt()], outs=[yag_all[tc, :, :].opt()]),
           [f'ygst{tc}_{hb2}' for hb2 in range(8)], [f'yago{tc}'], kind='cc',
           tag=f'AGy{tc}')

    front(0)
    front_b(0)
    for tc in range(NCH):
        if tc + 1 < NCH:
            front(tc + 1)
        scanpart(tc)
        if tc + 1 < NCH:
            front_b(tc + 1)

    for (nm, i_t, o_t, mi) in (('g', qg_i, qg_o, 0), ('u', qu_i, qu_o, 1),
                               ('d', qd_i, qd_o, 2)):
        OP('g', lambda e, i_t=i_t, o_t=o_t: e.collective_compute(
            "AllGather", ALU.bypass, replica_groups=RG8,
            ins=[i_t.opt()], outs=[o_t.opt()]),
           [f'qst{mi}'], [f'qAG{nm}'], kind='cc', tag=f'qAG{nm}')

    def finish(last_keys):
        OP('v', lambda e: e.memset(qrows[0:1, 0:1], 0.0), list(last_keys), [],
           tag='tailwait')
        with nc.Block() as block:
            p.emit(nc, block)
        for st in reversed(open_arenas):
            st.close()
        stack.close()
        return nc

    if upto == 'M':
        def ydump(e, half):
            rank = e.cc_rank(RG8)
            j = e.compute_val(rank % 4)
            ch = e.compute_val(j if half == 0 else 7 - j)
            return e.dma_start(out=dbg[:, half * TC:(half + 1) * TC],
                               in_=yag_all[dslice(ch, 1), :, :])
        OP('g', lambda e: ydump(e, 0), [f'yago{c}' for c in range(NCH)],
           ['dbg0'], kind='d', tag='ydump0')
        OP('g', lambda e: ydump(e, 1), [f'yago{c}' for c in range(NCH)],
           ['dbg1'], kind='d', tag='ydump1')
        a1.close()
        return finish(['dbg0', 'dbg1'])

    def dump_stream(stream):
        dma('q', dbg[:, :, :], stream[:, :, :],
            [f'stream_{kt}' for kt in range(16)], ['dbgd'], tag='dumpstream')
        return ['dbgd']

    def emit_ln(tag, rkey, rbuf, out_fn, lnps):
        ps_s, ps_m, rb_bf, rb_sq, rowbuf = lnps
        for kt in range(16):
            OP('s', lambda e, kt=kt: e.copy(out=rb_bf[:, :], in_=rbuf[:, kt, :]),
               [rkey], [f'{tag}_bf'], tag=f'{tag}_cp{kt}')
            OP('s', lambda e, kt=kt: e.activation(out=rb_sq[:, :], in_=rbuf[:, kt, :],
                                                  func=AF.Square),
               [rkey], [f'{tag}_sq'], tag=f'{tag}_sqr{kt}')
            OP('t', lambda e, kt=kt: e.matmul(ps_s[0:1, :], onescolb[:, :], rb_bf[:, :],
                                              start=(kt == 0), stop=(kt == 15)),
               [f'{tag}_bf', 'onescolb'], [f'{tag}_pss'], kind='c', accum=(kt > 0),
               tag=f'{tag}_mms{kt}')
            OP('t', lambda e, kt=kt: e.matmul(ps_m[0:1, :], onescolb[:, :], rb_sq[:, :],
                                              start=(kt == 0), stop=(kt == 15)),
               [f'{tag}_sq', 'onescolb'], [f'{tag}_psm'], kind='c', accum=(kt > 0),
               tag=f'{tag}_mmq{kt}')
        OP('v', lambda e: e.tensor_scalar(out=rowbuf[0:1, 0:512], in0=ps_s[0:1, :],
                                          scalar1=1.0 / D, scalar2=None, op0=ALU.mult),
           [f'{tag}_pss'], [f'{tag}_mrow'], tag=f'{tag}_mean')
        OP('v', lambda e: e.tensor_tensor(out=rowbuf[0:1, 512:1024],
                                          in0=rowbuf[0:1, 0:512],
                                          in1=rowbuf[0:1, 0:512], op=ALU.mult),
           [f'{tag}_mrow'], [f'{tag}_m2'], tag=f'{tag}_msq')
        OP('v', lambda e: e.tensor_scalar(out=rowbuf[0:1, 1024:1536], in0=ps_m[0:1, :],
                                          scalar1=1.0 / D, scalar2=None, op0=ALU.mult),
           [f'{tag}_psm'], [f'{tag}_vr1'], tag=f'{tag}_v1')
        OP('v', lambda e: e.tensor_tensor(out=rowbuf[0:1, 1024:1536],
                                          in0=rowbuf[0:1, 1024:1536],
                                          in1=rowbuf[0:1, 512:1024], op=ALU.subtract),
           [f'{tag}_vr1', f'{tag}_m2'], [f'{tag}_var'], accum=True, tag=f'{tag}_v2')
        OP('s', lambda e: e.activation(out=rowbuf[0:1, 1536:2048],
                                       in_=rowbuf[0:1, 1024:1536], func=AF.Sqrt,
                                       bias=epsc[0:1, 0:1]),
           [f'{tag}_var', 'epsc'], [f'{tag}_sd'], tag=f'{tag}_rsq')
        OP('v', lambda e: e.reciprocal(out=rowbuf[0:1, 512:1024],
                                       in_=rowbuf[0:1, 1536:2048]),
           [f'{tag}_sd', f'{tag}_m2'], [f'{tag}_rstd'], tag=f'{tag}_rcp')
        OP('t', lambda e: e.matmul(ps_s[:, :], onesrowF[:, :], rowbuf[0:1, 0:512],
                                   start=True, stop=True),
           [f'{tag}_mrow', 'onesrowF'], [f'{tag}_pss'], kind='c', tag=f'{tag}_bm')
        OP('t', lambda e: e.matmul(ps_m[:, :], onesrowF[:, :], rowbuf[0:1, 512:1024],
                                   start=True, stop=True),
           [f'{tag}_rstd', 'onesrowF'], [f'{tag}_psm'], kind='c', tag=f'{tag}_br')
        for kt in range(16):
            OP('v', lambda e, kt=kt: e.tensor_tensor(out=rbuf[:, kt, :],
                                                     in0=rbuf[:, kt, :],
                                                     in1=ps_s[:, :], op=ALU.subtract),
               [rkey, f'{tag}_pss'], [rkey], accum=True, tag=f'{tag}_sub{kt}')
            out_fn(kt)

    p.fence()
    a1.close()
    stream = sb('stream', [128, 16, 1024], BF16)
    aW, sbW, psW = mk_arena()
    ybuf = sbW('ybuf', [128, 32, 2 * TC], BF16)
    xres = sbW('xres', [128, 16, 2 * TC], BF16)
    wob = [sbW(f'wob{i}', [128, 32, 128], BF16) for i in range(2)]
    rbuf = sbW('rbuf', [128, 16, TC], F32)
    rb_bf = sbW('rb_bf', [128, TC], BF16)
    rb_sq = sbW('rb_sq', [128, TC], BF16)
    rowbuf = sbW('rowbuf', [1, 2048], F32)
    psA = psW('psA', [128, 512]); psB = psW('psB', [128, 512])
    psS = psW('psS', [128, 512]); psM = psW('psM', [128, 512])
    lnps = (psS, psM, rb_bf, rb_sq, rowbuf)

    def ypick(e, half):
        rank = e.cc_rank(RG8)
        j = e.compute_val(rank % 4)
        ch = e.compute_val(j if half == 0 else 7 - j)
        return e.dma_start(
            out=ybuf[:, :, half * TC:(half + 1) * TC],
            in_=yag_all[dslice(ch, 1), :, :]
            .rearrange("o (dt p) t -> p (o dt) t", p=128))
    OP('g', lambda e: ypick(e, 0), [f'yago{c}' for c in range(NCH)], ['ybuf'],
       kind='d', tag='ypick0')
    OP('g', lambda e: ypick(e, 1), [f'yago{c}' for c in range(NCH)], ['ybuf'],
       kind='d', accum=True, tag='ypick1')
    dma('g', xres[:, :, :], xown_d[:, :, :], ['xown_d'], ['xres'], tag='ldxres')

    for tc in range(2):
        for ot in range(16):
            wb = wob[ot % 2]
            dma('g', wb[:, :, :],
                wout_d[ot, :, :].rearrange("p (kt c) -> p kt c", c=128),
                ['wout_d'], [f'wob{ot % 2}'], tag=f'ldwo{tc}_{ot}')
            bank, pk = (psA, 'psA') if ot % 2 == 0 else (psB, 'psB')
            for kt in range(32):
                OP('t', lambda e, kt=kt, tc=tc, bank=bank, wb=wb, ot=ot: e.matmul(
                    bank[:, :], wb[:, kt, :], ybuf[:, kt, tc * TC:(tc + 1) * TC],
                    start=(kt == 0), stop=(kt == 31)),
                   [f'wob{ot % 2}', 'ybuf'], [pk], kind='c', accum=(kt > 0),
                   tag=f'womm{tc}_{ot}_{kt}')
            OP('v', lambda e, ot=ot, tc=tc, bank=bank, rbuf=rbuf: e.tensor_tensor(
                out=rbuf[:, ot, :], in0=bank[:, :],
                in1=xres[:, ot, tc * TC:(tc + 1) * TC], op=ALU.add),
               [pk, 'xres'], ['rW'], accum=(ot > 0), tag=f'woev{tc}_{ot}')

        def out_fn(kt, tc=tc, rbuf=rbuf, psM=psM):
            OP('v', lambda e, kt=kt, tc=tc, rbuf=rbuf, psM=psM: e.tensor_tensor(
                out=stream[:, kt, tc * TC:(tc + 1) * TC], in0=rbuf[:, kt, :],
                in1=psM[:, :], op=ALU.mult),
               ['rW', 'ln1_psm'], [f'stream_{kt}'], accum=(tc > 0),
               tag=f'ln1o{tc}_{kt}')
        emit_ln('ln1', 'rW', rbuf, out_fn, lnps)

    if upto == 'W':
        k = dump_stream(stream)
        return finish(k)

    p.fence()
    aW.close()
    aF, sbF, psF = mk_arena()
    g0s_sb = sbF('g0s_sb', [128, 44], F32)
    u0s_sb = sbF('u0s_sb', [128, 44], F32)
    d0s_sb = sbF('d0s_sb', [128, 16], F32)
    dma('q', g0s_sb[:, :], g0s_d[:, :], ['g0s_d'], ['g0s_sb'])
    dma('q', u0s_sb[:, :], u0s_d[:, :], ['u0s_d'], ['u0s_sb'])
    dma('q', d0s_sb[:, :], d0s_d[:, :], ['d0s_d'], ['d0s_sb'])
    gwb = [sbF(f'gwb{i}', [128, 16, 128], BF16) for i in range(2)]
    uwb = [sbF(f'uwb{i}', [128, 16, 128], BF16) for i in range(2)]
    dwb = [sbF(f'dwb{i}', [128, 44, 128], BF16) for i in range(2)]
    hbuf = sbF('hbuf', [128, 44, 2 * TC], BF16)
    gsil = sbF('gsil', [128, TC], BF16)
    rbuf = sbF('rbufF', [128, 16, TC], F32)
    rb_bf = sbF('rb_bfF', [128, TC], BF16)
    rb_sq = sbF('rb_sqF', [128, TC], BF16)
    rowbuf = sbF('rowbufF', [1, 2048], F32)
    psA = psF('psA_F', [128, 512]); psB = psF('psB_F', [128, 512])
    psC = psF('psC_F', [128, 512]); psD = psF('psD_F', [128, 512])
    psS = psF('psS_F', [128, 512]); psM = psF('psM_F', [128, 512])
    lnps = (psS, psM, rb_bf, rb_sq, rowbuf)

    for ot in range(44):
        gw, uw = gwb[ot % 2], uwb[ot % 2]
        dma('g', gw[:, :, :], g0_d[ot, :, :].rearrange("p (kt c) -> p kt c", c=128),
            ['g0_d'], [f'gwb{ot % 2}'], tag=f'ldg0_{ot}')
        dma('g', uw[:, :, :], u0_d[ot, :, :].rearrange("p (kt c) -> p kt c", c=128),
            ['u0_d'], [f'uwb{ot % 2}'], tag=f'ldu0_{ot}')
        for tc in range(2):
            bg, pg = (psA, 'psA_F') if ot % 2 == 0 else (psB, 'psB_F')
            bu, pu = (psC, 'psC_F') if ot % 2 == 0 else (psD, 'psD_F')
            for kt in range(16):
                OP('t', lambda e, kt=kt, tc=tc, bg=bg, gw=gw, ot=ot: e.matmul(
                    bg[:, :], gw[:, kt, :], stream[:, kt, tc * TC:(tc + 1) * TC],
                    start=(kt == 0), stop=(kt == 15)),
                   [f'gwb{ot % 2}', f'stream_{kt}'], [pg], kind='c', accum=(kt > 0),
                   tag=f'g0mm{ot}_{tc}_{kt}')
            for kt in range(16):
                OP('t', lambda e, kt=kt, tc=tc, bu=bu, uw=uw, ot=ot: e.matmul(
                    bu[:, :], uw[:, kt, :], stream[:, kt, tc * TC:(tc + 1) * TC],
                    start=(kt == 0), stop=(kt == 15)),
                   [f'uwb{ot % 2}', f'stream_{kt}'], [pu], kind='c', accum=(kt > 0),
                   tag=f'u0mm{ot}_{tc}_{kt}')
            OP('s', lambda e, ot=ot, bg=bg: e.activation(
                out=gsil[:, :], in_=bg[:, :], func=AF.Silu,
                scale=g0s_sb[:, ot:ot + 1]),
               [pg, 'g0s_sb'], ['gsil'], tag=f'g0sil{ot}_{tc}')
            OP('v', lambda e, ot=ot, tc=tc, bu=bu: e.scalar_tensor_tensor(
                out=hbuf[:, ot, tc * TC:(tc + 1) * TC], in0=bu[:, :],
                scalar=u0s_sb[:, ot:ot + 1], in1=gsil[:, :],
                op0=ALU.mult, op1=ALU.mult),
               [pu, 'gsil', 'u0s_sb'], [f'hb0_{ot}'], accum=(tc > 0),
               tag=f'h0{ot}_{tc}')
    for tc in range(2):
        for ot in range(16):
            dw = dwb[ot % 2]
            dma('g', dw[:, :, :], d0_d[ot, :, :].rearrange("p (kt c) -> p kt c", c=128),
                ['d0_d'], [f'dwb{ot % 2}'], tag=f'ldd0_{tc}_{ot}')
            bank, pk = (psA, 'psA_F') if ot % 2 == 0 else (psB, 'psB_F')
            for kt in range(44):
                OP('t', lambda e, kt=kt, tc=tc, bank=bank, dw=dw, ot=ot: e.matmul(
                    bank[:, :], dw[:, kt, :], hbuf[:, kt, tc * TC:(tc + 1) * TC],
                    start=(kt == 0), stop=(kt == 43)),
                   [f'dwb{ot % 2}', f'hb0_{kt}'], [pk], kind='c', accum=(kt > 0),
                   tag=f'd0mm{ot}_{tc}_{kt}')
            OP('v', lambda e, ot=ot, tc=tc, bank=bank, rbuf=rbuf: e.scalar_tensor_tensor(
                out=rbuf[:, ot, :], in0=bank[:, :], scalar=d0s_sb[:, ot:ot + 1],
                in1=stream[:, ot, tc * TC:(tc + 1) * TC],
                op0=ALU.mult, op1=ALU.add),
               [pk, f'stream_{ot}', 'd0s_sb'], ['rF'], accum=(ot > 0),
               tag=f'd0ev{ot}_{tc}')

        def out_fn(kt, tc=tc, rbuf=rbuf, psM=psM):
            OP('v', lambda e, kt=kt, tc=tc, rbuf=rbuf, psM=psM: e.tensor_tensor(
                out=stream[:, kt, tc * TC:(tc + 1) * TC], in0=rbuf[:, kt, :],
                in1=psM[:, :], op=ALU.mult),
               ['rF', 'ln2_psm'], [f'stream_{kt}'], tag=f'ln2o{tc}_{kt}')
        emit_ln('ln2', 'rF', rbuf, out_fn, lnps)

    if upto == 'F0':
        k = dump_stream(stream)
        return finish(k)

    p.fence()
    aF.close()
    aA, sbA, psAo = mk_arena()
    ctq = [sbA(f'ctq{i}', [128, 16, TC], BF16) for i in range(2)]
    dma('q', sag_i.rearrange("(kt p) t -> p kt t", p=128), stream[:, :, :],
        [f'stream_{kt}' for kt in range(16)], ['sag_i_d'], tag='stsag')
    OP('g', lambda e: e.collective_compute(
        "AllGather", ALU.bypass, replica_groups=RG4,
        ins=[sag_i.opt()], outs=[sag_o.opt()]),
       ['sag_i_d'], ['sAG'], kind='cc', tag='sAG')

    aA1, sbA1, psA1 = mk_arena()
    rc_sb = sbA1('rc_sb', [128, T], BF16)
    rs_sb = sbA1('rs_sb', [128, T], BF16)
    permb = sbA1('permb', [128, 128], BF16)
    trib = sbA1('trib', [128, 4, TC], BF16)
    wqh_sb = sbA1('wqh_sb', [128, 4, 16, 128], BF16)
    wkh_sb = sbA1('wkh_sb', [128, 16, 128], BF16)
    wvh_sb = sbA1('wvh_sb', [128, 16, 128], BF16)
    dma('g', rc_sb[:, :], rc_d[:, :], ['rc_d'], ['rc_sb'])
    dma('g', rs_sb[:, :], rs_d[:, :], ['rs_d'], ['rs_sb'])
    dma('g', permb[:, :], perm_d[:, :], ['perm_d'], ['permb'])
    dma('g', trib[:, :, :], tri_d.rearrange("kt p t -> p kt t"), ['tri_d'], ['trib'])
    for t4 in range(4):
        dma('g', wqh_sb[:, t4, :, :],
            wqh_d[t4, :, :].rearrange("p (kt c) -> p kt c", c=128),
            ['wqh_d'], ['wqh_sb'], tag=f'ldwqh{t4}', accum=(t4 > 0))
    dma('g', wkh_sb[:, :, :], wkh_d.rearrange("p (kt c) -> p kt c", c=128),
        ['wkh_d'], ['wkh_sb'])
    dma('g', wvh_sb[:, :, :], wvh_d[:, :, :], ['wvh_d'], ['wvh_sb'])
    aqb = sbA1('aqb_a', [128, 4, T], BF16)
    kbuf = sbA1('kbuf_a', [128, T], BF16)
    vtm = sbA1('vtm_a', [128, 32, 130], BF16)
    axa = [sbA1(f'xa{i}', [128, 16, TC], BF16) for i in range(2)]
    qtmp = sbA1('qtmp', [128, TC], BF16)
    t1b = sbA1('t1b', [128, TC], BF16)
    ptb = [sbA1(f'ptb{i}', [128, TC], BF16) for i in range(3)]
    ctmp = sbA1('ctmp', [64, TC], F32)
    rcb = sbA1('rcb', [64, TC], F32)
    rrow = sbA1('rrow', [1, TC], F32)
    psQ1 = psA1('psQ1', [128, 512]); psQ2 = psA1('psQ2', [128, 512])
    psP = psA1('psP', [128, 512])
    psSc = [psA1('psSc0', [128, 512]), psA1('psSc1', [128, 512])]
    psCx = [psA1('psCx0', [128, 512]), psA1('psCx1', [128, 512])]
    psB2 = psA1('psB2', [128, 512])
    OP('v', lambda e: e.memset(vtm[:, :, 64:65], 1.0), [], ['vtm'], accum=True,
       tag='vone0')
    OP('v', lambda e: e.memset(vtm[:, :, 129:130], 1.0), [], ['vtm'], accum=True,
       tag='vone1')

    def rope_tile(nm, qc, bank, pk, dst, dkey):
        OP('s', lambda e, bank=bank: e.copy(out=qtmp[:, :], in_=bank[:, :]),
           [pk], ['qtmp'], tag=f'{nm}cp{qc}')
        OP('t', lambda e: e.matmul(psP[:, :], permb[:, :], qtmp[:, :],
                                   start=True, stop=True),
           ['qtmp', 'permb'], ['psP'], kind='c', tag=f'{nm}pm{qc}')
        OP('v', lambda e, qc=qc: e.tensor_tensor(
            out=t1b[:, :], in0=qtmp[:, :], in1=rc_sb[:, qc * TC:(qc + 1) * TC],
            op=ALU.mult),
           ['qtmp', 'rc_sb'], ['t1b'], tag=f'{nm}t1{qc}')
        OP('v', lambda e, qc=qc: e.tensor_tensor(
            out=qtmp[:, :], in0=psP[:, :], in1=rs_sb[:, qc * TC:(qc + 1) * TC],
            op=ALU.mult),
           ['psP', 'rs_sb', 'qtmp'], ['qtmp'], accum=True, tag=f'{nm}t2{qc}')
        OP('v', lambda e, dst=dst: e.tensor_tensor(
            out=dst, in0=t1b[:, :], in1=qtmp[:, :], op=ALU.add),
           ['t1b', 'qtmp'], [dkey], accum=True, tag=f'{nm}t3{qc}')

    for qc in range(NCH):
        b = qc % 2
        rank, halfk = chunk_pos(qc)
        dma('q', axa[b][:, :, :],
            sag_o[rank * D:(rank + 1) * D, halfk * TC:(halfk + 1) * TC]
            .rearrange("(kt p) t -> p kt t", p=128),
            ['sAG'], [f'xa{b}'], tag=f'ldxa{qc}')
        for kt in range(16):
            OP('t', lambda e, kt=kt, b=b: e.matmul(
                psQ1[:, :], wkh_sb[:, kt, :], axa[b][:, kt, :],
                start=(kt == 0), stop=(kt == 15)),
               ['wkh_sb', f'xa{b}'], ['psQ1'], kind='c', accum=(kt > 0),
               tag=f'kmm{qc}_{kt}')
        rope_tile('k', qc, psQ1, 'psQ1', kbuf[:, qc * TC:(qc + 1) * TC], 'kbuf')
        for t4 in range(4):
            bank, pk = (psQ1, 'psQ1') if t4 % 2 else (psQ2, 'psQ2')
            for kt in range(16):
                OP('t', lambda e, kt=kt, t4=t4, b=b, bank=bank: e.matmul(
                    bank[:, :], wqh_sb[:, t4, kt, :], axa[b][:, kt, :],
                    start=(kt == 0), stop=(kt == 15)),
                   ['wqh_sb', f'xa{b}'], [pk], kind='c', accum=(kt > 0),
                   tag=f'qmm{qc}_{t4}_{kt}')
            rope_tile(f'q{t4}', qc, bank, pk,
                      aqb[:, t4, qc * TC:(qc + 1) * TC], 'aqb')
        for tb in range(4):
            bank, pk = (psQ1, 'psQ1') if tb % 2 else (psQ2, 'psQ2')
            for kt in range(16):
                OP('t', lambda e, kt=kt, tb=tb, b=b, bank=bank: e.matmul(
                    bank[:, 0:128], axa[b][:, kt, tb * 128:(tb + 1) * 128],
                    wvh_sb[:, kt, :], start=(kt == 0), stop=(kt == 15)),
                   ['wvh_sb', f'xa{b}'], [pk], kind='c', accum=(kt > 0),
                   tag=f'vmm{qc}_{tb}_{kt}')
            OP('s', lambda e, qc=qc, tb=tb, bank=bank: e.copy(
                out=vtm[:, qc * 4 + tb, 0:64], in_=bank[:, 0:64]),
               [pk, 'vtm'], ['vtm'], accum=True, tag=f'vev{qc}_{tb}a')
            OP('s', lambda e, qc=qc, tb=tb, bank=bank: e.copy(
                out=vtm[:, qc * 4 + tb, 65:129], in_=bank[:, 64:128]),
               [pk, 'vtm'], ['vtm'], accum=True, tag=f'vev{qc}_{tb}b')
        par = 1 if qc >= 4 else 0
        kt16b = (qc * 4) % 16
        for h8 in range(8):
            qbase = 0 if h8 < 4 else 64
            qtile = h8 % 4
            vb = 0 if h8 < 4 else 65
            cbank, ck = psCx[h8 % 2], f'psCx{h8 % 2}'
            NK = 4 * (qc + 1)

            def emit_smm(nk):
                kc, kt = divmod(nk, 4)
                i = nk % 3
                sbank, sk = psSc[nk % 2], f'psSc{nk % 2}'
                OP('t', lambda e, qbase=qbase, qtile=qtile, kc=kc, kt=kt,
                   qc=qc, sbank=sbank: e.matmul(
                    sbank[:, :],
                    kbuf[qbase:qbase + 64, kc * TC + kt * 128:kc * TC + (kt + 1) * 128],
                    aqb[qbase:qbase + 64, qtile, qc * TC:(qc + 1) * TC],
                    start=True, stop=True),
                   ['kbuf', 'aqb'], [sk], kind='c', tag=f'smm{qc}_{h8}_{kc}_{kt}')
                OP('s', lambda e, i=i, sbank=sbank: e.activation(
                    out=ptb[i][:, :], in_=sbank[:, :], func=AF.Exp, scale=0.125),
                   [sk], [f'ptb{i}'], tag=f'exp{qc}_{h8}_{kc}_{kt}')
                if kc == qc:
                    OP('g', lambda e, i=i, kt=kt: e.tensor_tensor(
                        out=ptb[i][:, :], in0=ptb[i][:, :],
                        in1=trib[:, kt, :], op=ALU.mult),
                       [f'ptb{i}', 'trib'], [f'ptb{i}'], accum=True,
                       tag=f'msk{qc}_{h8}_{kt}')

            def emit_av(nk):
                kc, kt = divmod(nk, 4)
                i = nk % 3
                OP('t', lambda e, i=i, kc=kc, kt=kt, vb=vb, cbank=cbank,
                   first=(nk == 0), last=(nk == NK - 1): e.matmul(
                    cbank[0:65, :], vtm[:, kc * 4 + kt, vb:vb + 65],
                    ptb[i][:, :], start=first, stop=last),
                   [f'ptb{i}', 'vtm'], [ck], kind='c', accum=(nk > 0),
                   tag=f'av{qc}_{h8}_{kc}_{kt}')

            emit_smm(0)
            for nk in range(NK):
                if nk + 1 < NK:
                    emit_smm(nk + 1)
                emit_av(nk)
            OP('v', lambda e, cbank=cbank: e.reciprocal(
                out=rrow[0:1, :], in_=cbank[64:65, :]),
               [ck], ['rrow'], tag=f'rec{qc}_{h8}')
            OP('t', lambda e: e.matmul(psB2[0:64, :], onesrowF[:, 0:64],
                                       rrow[0:1, :], start=True, stop=True),
               ['rrow', 'onesrowF'], ['psB2'], kind='c', tag=f'bcr{qc}_{h8}')
            OP('s', lambda e: e.copy(out=rcb[:, :], in_=psB2[0:64, :]),
               ['psB2'], ['rcb'], tag=f'rcb{qc}_{h8}')
            OP('v', lambda e, cbank=cbank: e.tensor_tensor(
                out=ctmp[:, :], in0=cbank[0:64, :], in1=rcb[:, :],
                op=ALU.mult),
               [ck, 'rcb'], ['ctmp'], tag=f'cev{qc}_{h8}')
            hg = 0 if h8 < 4 else 1
            hloc = h8 % 4
            for kt in range(4):
                OP('t', lambda e, kt=kt: e.transpose(
                    out=psP[0:128, 0:64], in_=ctmp[:, kt * 128:(kt + 1) * 128],
                    identity=identf[0:64, 0:64]),
                   ['ctmp', 'identb', 'psP'], ['psP'], kind='c',
                   tag=f'ctp{qc}_{h8}_{kt}')
                OP('v', lambda e, par=par, kt16b=kt16b, kt=kt, hg=hg,
                   hloc=hloc: e.tensor_copy(
                    out=ctq[par][:, kt16b + kt, hg * 256 + hloc * 64:
                                 hg * 256 + (hloc + 1) * 64],
                    in_=psP[0:128, 0:64]),
                   ['psP'], [f'ctq{par}'], accum=True, tag=f'ctc{qc}_{h8}_{kt}')

    p.fence()
    aA1.close()
    aA3, sbA3, psA3 = mk_arena()
    wob2 = [sbA3(f'wob2{i}', [128, 16, 128], BF16) for i in range(2)]
    rbuf = sbA3('rbufA', [128, 16, TC], F32)
    rb_bf = sbA3('rb_bfA', [128, TC], BF16)
    rb_sq = sbA3('rb_sqA', [128, TC], BF16)
    rowbuf = sbA3('rowbufA', [1, 2048], F32)
    psWo = psA3('psWo', [128, 512]); psWo2 = psA3('psWo2', [128, 512])
    psS = psA3('psS_A', [128, 512]); psM = psA3('psM_A', [128, 512])
    lnps = (psS, psM, rb_bf, rb_sq, rowbuf)
    for tc in range(2):
        for ot in range(16):
            wb = wob2[ot % 2]
            dma('g', wb[:, :, :], wo_d[ot, :, :].rearrange("p (kt c) -> p kt c", c=128),
                ['wo_d'], [f'wob2{ot % 2}'], tag=f'ldwo2_{tc}_{ot}')
            bank, pk = (psWo, 'psWo') if ot % 2 == 0 else (psWo2, 'psWo2')
            for par in range(2):
                for kt in range(16):
                    OP('t', lambda e, kt=kt, tc=tc, par=par, bank=bank, wb=wb:
                       e.matmul(
                        bank[:, par * 256:(par + 1) * 256], wb[:, kt, :],
                        ctq[par][:, kt, tc * 256:(tc + 1) * 256],
                        start=(kt == 0), stop=(kt == 15)),
                       [f'wob2{ot % 2}', f'ctq{par}'], [pk], kind='c',
                       accum=(par > 0 or kt > 0), tag=f'womm2{tc}_{ot}_{par}_{kt}')
            for par in range(2):
                OP('v', lambda e, ot=ot, tc=tc, par=par, bank=bank, rbuf=rbuf:
                   e.tensor_tensor(
                    out=rbuf[:, ot, par * 256:(par + 1) * 256],
                    in0=bank[:, par * 256:(par + 1) * 256],
                    in1=stream[:, ot, tc * TC + par:(tc + 1) * TC:2],
                    op=ALU.add),
                   [pk, f'stream_{ot}'], ['rA'], accum=(ot > 0 or par > 0),
                   tag=f'woev2{tc}_{ot}_{par}')

        def out_fn(kt, tc=tc, rbuf=rbuf, psM=psM):
            for par in range(2):
                OP('v', lambda e, kt=kt, tc=tc, par=par, rbuf=rbuf, psM=psM:
                   e.tensor_tensor(
                    out=stream[:, kt, tc * TC + par:(tc + 1) * TC:2],
                    in0=rbuf[:, kt, par * 256:(par + 1) * 256],
                    in1=psM[:, par * 256:(par + 1) * 256], op=ALU.mult),
                   ['rA', 'ln3_psm'], [f'stream_{kt}'], accum=(par > 0),
                   tag=f'ln3o{tc}_{kt}_{par}')
        emit_ln('ln3', 'rA', rbuf, out_fn, lnps)

    if upto == 'A':
        k = dump_stream(stream)
        return finish(k)

    p.fence()
    aA3.close()
    aA.close()
    aB, sbB, psB_ = mk_arena()
    lnw_sb = sbB('lnw_sb', [128, 64], F32)
    scol = sbB('scol', [128, 4], F32)
    dma('q', lnw_sb[:, :], lnw_d[:, :], ['lnw_d'], ['lnw_sb'])
    OP('v', lambda e: e.tensor_scalar(out=scol[:, :], in0=mcol[:, :],
                                      scalar1=1.0 / 127.0, scalar2=None,
                                      op0=ALU.mult),
       ['mcol'], ['scol'], tag='scol')
    qxb = sbB('qxb', [128, 16, TC], BF16)
    qgb = [sbB(f'qgb{i}', [128, 2048], BF16) for i in range(2)]
    qub = [sbB(f'qub{i}', [128, 2048], BF16) for i in range(2)]
    qdb = [sbB(f'qdb{i}', [128, 2048], BF16) for i in range(2)]
    hbuf1 = sbB('hbuf1', [128, 64, TC], BF16)
    guq = sbB('guq', [128, TC], F32)
    gsl = sbB('gsl', [128, TC], BF16)
    xa = [sbB(f'xaB{i}', [128, TC], BF16) for i in range(2)]
    xmax = sbB('xmax', [128, TC], F32)
    mxt = sbB('mxt', [128, 4], F32)
    mx4 = sbB('mx4', [4, 128], F32)
    sx4 = sbB('sx4', [4, 128], F32)
    sel4_sb = sbB('sel4_sb', [4, 4 * 128], F32)
    vrt = sbB('vrt', [128, 4], F32)
    rstd4 = sbB('rstd4', [128, 4], F32)
    ct4 = sbB('ct4', [128, 4], F32)
    epsB = sbB('epsB', [128, 1], F32)
    OP('v', lambda e: e.memset(epsB[:, :], RMS_EPS), [], ['epsB'])
    dma('q', sel4_sb[:, :], sel4_d[:, :], ['sel4_d'], ['sel4_sb'])
    hsqb = sbB('hsqb', [128, TC], BF16)
    bc1 = sbB('bc1', [128, TC], F32)
    bc2 = sbB('bc2', [128, TC], F32)
    qt = sbB('qt', [128, TC], F32)
    rowq = sbB('rowq', [1, TC], F32)
    rbuf = sbB('rbufB', [128, 16, TC], F32)
    rb_bf = sbB('rb_bfB', [128, TC], BF16)
    rb_sq = sbB('rb_sqB', [128, TC], BF16)
    rowbuf = sbB('rowbufB', [1, 2048], F32)
    obuf = sbB('obuf', [128, 4, 128], F32)
    psG = [psB_('psG0', [128, 512]), psB_('psG1', [128, 512])]
    psU = [psB_('psU0', [128, 512]), psB_('psU1', [128, 512])]
    psR = psB_('psR', [128, 512])
    psX = psB_('psX', [128, 512])
    psS = psB_('psS_B', [128, 512]); psM = psB_('psM_B', [128, 512])
    lnps = (psS, psM, rb_bf, rb_sq, rowbuf)

    def cross_max_bcasts(tag, rec_dst, rec_key, mx_dst, mx_key, rec_scale_tag):
        for tb in range(4):
            OP('t', lambda e, tb=tb: e.transpose(
                out=psX[:, tb * 128:(tb + 1) * 128],
                in_=xmax[:, tb * 128:(tb + 1) * 128],
                identity=identf[:, :]),
               ['xmax', 'identf', 'psX'], ['psX'], kind='c', accum=(tb > 0),
               tag=f'{tag}tp{tb}')
        for tb in range(4):
            OP('v', lambda e, tb=tb: e.reduce_max(
                out=mxt[:, tb:tb + 1], in_=psX[:, tb * 128:(tb + 1) * 128], axis=AXF),
               ['psX'], ['mxt'], accum=(tb > 0), tag=f'{tag}rm{tb}')
        OP('v', lambda e: e.tensor_scalar(out=mxt[:, :], in0=mxt[:, :],
                                          scalar1=EPS_Q, scalar2=None, op0=ALU.max),
           ['mxt'], ['mxt'], accum=True, tag=f'{tag}clip')
        OP('t', lambda e: e.transpose(out=psX[0:4, 0:128], in_=mxt[:, :],
                                      identity=identf[:, :]),
           ['mxt', 'identf', 'psX'], ['psX'], tag=f'{tag}tpm')
        OP('v', lambda e: e.tensor_copy(out=mx4[:, :], in_=psX[0:4, 0:128]),
           ['psX'], ['mx4'], tag=f'{tag}mx4')
        OP('v', lambda e: e.reciprocal(out=sx4[:, :], in_=mx4[:, :]),
           ['mx4'], ['sx4'], tag=f'{tag}rcp')
        for tb in range(4):
            OP('t', lambda e, tb=tb: e.matmul(
                psX[:, tb * 128:(tb + 1) * 128],
                sel4_sb[:, tb * 128:(tb + 1) * 128], sx4[:, :],
                start=True, stop=True),
               ['sx4', 'sel4_sb', 'psX'], ['psX'], kind='c', accum=(tb > 0),
               tag=f'{tag}bcr{tb}')
        OP('v', lambda e: e.tensor_scalar(out=rec_dst[:, :], in0=psX[:, :],
                                          scalar1=127.0, scalar2=None, op0=ALU.mult),
           ['psX'], [rec_key], tag=f'{tag}recb')
        for tb in range(4):
            OP('t', lambda e, tb=tb: e.matmul(
                psX[:, tb * 128:(tb + 1) * 128],
                sel4_sb[:, tb * 128:(tb + 1) * 128], mx4[:, :],
                start=True, stop=True),
               ['mx4', 'sel4_sb', 'psX'], ['psX'], kind='c', accum=(tb > 0),
               tag=f'{tag}bcm{tb}')
        OP('v', lambda e: e.tensor_copy(out=mx_dst[:, :], in_=psX[:, :]),
           ['psX'], [mx_key], tag=f'{tag}mxb')

    RA = slice(0, TC)

    outks = []
    for tc in range(2):
        for kt in range(16):
            OP('s', lambda e, kt=kt, tc=tc: e.activation(
                out=xa[kt % 2][:, :], in_=stream[:, kt, tc * TC:(tc + 1) * TC],
                func=AF.Abs),
               [f'stream_{kt}'], [f'xa{kt % 2}'], tag=f'xabs{tc}_{kt}')
            if kt == 0:
                OP('v', lambda e: e.tensor_copy(out=xmax[:, :], in_=xa[0][:, :]),
                   ['xa0'], ['xmax'], tag=f'xm0_{tc}')
            else:
                OP('v', lambda e, kt=kt: e.tensor_tensor(
                    out=xmax[:, :], in0=xmax[:, :], in1=xa[kt % 2][:, :],
                    op=ALU.max),
                   [f'xa{kt % 2}', 'xmax'], ['xmax'], accum=True, tag=f'xm{tc}_{kt}')
        cross_max_bcasts(f'xq{tc}_', bc1, 'bc1', bc2, 'bc2', tc)
        for kt in range(16):
            OP('v', lambda e, kt=kt, tc=tc: e.tensor_tensor(
                out=qt[:, :], in0=stream[:, kt, tc * TC:(tc + 1) * TC],
                in1=bc1[:, :], op=ALU.mult),
               [f'stream_{kt}', 'bc1'], ['qt'], tag=f'qxm{tc}_{kt}')
            OP('v', lambda e, kt=kt: e.tensor_scalar(
                out=qxb[:, kt, :], in0=qt[:, :], scalar1=MAGIC, scalar2=MAGIC,
                op0=ALU.add, op1=ALU.subtract),
               ['qt'], ['qxb'], accum=(kt > 0), tag=f'qxr{tc}_{kt}')
        if upto == 'F1d' and tc == 0:
            dma('q', dbg[:, :, :], qxb[:, :, :], ['qxb'], ['dbgd1'], tag='dqx')
            OP('v', lambda e: e.tensor_tensor(
                out=bc2[:, :], in0=bc2[:, :], in1=bc2[:, :], op=ALU.bypass)
               if False else e.tensor_copy(out=qt[:, :], in_=bc1[:, :]),
               ['bc1'], ['qt'], tag='dbc1c')
            dma('q', dbg4[:, 0:512], qt[:, :], ['qt'], ['dbgd4a'], tag='dbc1')
            OP('v', lambda e: e.tensor_copy(out=qt[:, :], in_=bc2[:, :]),
               ['bc2', 'qt'], ['qt'], tag='dbc2c')
            dma('q', dbg4[:, 512:1024], qt[:, :], ['qt'], ['dbgd4b'], tag='dbc2')
            dma('q', dbg4[:, 1024:1028], swcol[:, :], ['swcol'], ['dbgd4c'])
            dma('q', dbg4[:, 1028:1032], mcol[:, :], ['mcol'], ['dbgd4d'])
            dma('q', dbg4[:, 1032:1036], scol[:, :], ['scol'], ['dbgd4e'])
            dma('q', dbg5[:, :], qg_o[0:128, 0:2048], ['qAGg'], ['dbgd5'])
        for ot in range(64):
            rnk, ot8 = ot // 8, ot % 8
            gw, uw = qgb[ot % 2], qub[ot % 2]
            dma('q', gw[:, :], qg_o[rnk * 128:(rnk + 1) * 128,
                                    ot8 * 2048:(ot8 + 1) * 2048],
                ['qAGg'], [f'qgb{ot % 2}'], tag=f'ldqg{tc}_{ot}')
            dma('q', uw[:, :], qu_o[rnk * 128:(rnk + 1) * 128,
                                    ot8 * 2048:(ot8 + 1) * 2048],
                ['qAGu'], [f'qub{ot % 2}'], tag=f'ldqu{tc}_{ot}')
            bg, pg = psG[ot % 2], f'psG{ot % 2}'
            bu, pu = psU[ot % 2], f'psU{ot % 2}'
            for kt in range(16):
                OP('t', lambda e, kt=kt, bg=bg, gw=gw, ot=ot: e.matmul(
                    bg[:, :], gw[:, kt * 128:(kt + 1) * 128], qxb[:, kt, :],
                    start=(kt == 0), stop=(kt == 15)),
                   [f'qgb{ot % 2}', 'qxb'], [pg], kind='c', accum=(kt > 0),
                   tag=f'gmm{tc}_{ot}_{kt}')
            for kt in range(16):
                OP('t', lambda e, kt=kt, bu=bu, uw=uw, ot=ot: e.matmul(
                    bu[:, :], uw[:, kt * 128:(kt + 1) * 128], qxb[:, kt, :],
                    start=(kt == 0), stop=(kt == 15)),
                   [f'qub{ot % 2}', 'qxb'], [pu], kind='c', accum=(kt > 0),
                   tag=f'umm{tc}_{ot}_{kt}')
            OP('v', lambda e, bg=bg: e.scalar_tensor_tensor(
                out=guq[:, :], in0=bg[:, :], scalar=scol[:, 0:1], in1=bc2[:, :],
                op0=ALU.mult, op1=ALU.mult),
               [pg, 'scol', 'bc2'], ['guq'], tag=f'gev{tc}_{ot}')
            OP('s', lambda e: e.activation(out=gsl[:, :], in_=guq[:, :],
                                           func=AF.Silu),
               ['guq'], ['gsl'], tag=f'gsil{tc}_{ot}')
            OP('v', lambda e, bu=bu: e.scalar_tensor_tensor(
                out=guq[:, :], in0=bu[:, :], scalar=scol[:, 1:2], in1=bc2[:, :],
                op0=ALU.mult, op1=ALU.mult),
               [pu, 'scol', 'bc2', 'gsl'], ['guq'], tag=f'uev{tc}_{ot}')
            OP('v', lambda e, ot=ot: e.tensor_tensor(
                out=hbuf1[:, ot, :], in0=gsl[:, :], in1=guq[:, :], op=ALU.mult),
               ['gsl', 'guq'], [f'hb1_{ot}'], tag=f'hmul{tc}_{ot}')
        for ot in range(64):
            OP('s', lambda e, ot=ot: e.activation(
                out=xa[ot % 2][:, :], in_=hbuf1[:, ot, :], func=AF.Abs,
                scale=lnw_sb[:, ot:ot + 1]),
               [f'hb1_{ot}', 'lnw_sb'], [f'xa{ot % 2}'], tag=f'habs{tc}_{ot}')
            if ot == 0:
                OP('v', lambda e: e.tensor_copy(out=xmax[:, :], in_=xa[0][:, :]),
                   ['xa0'], ['xmax'], tag=f'hm0_{tc}')
            else:
                OP('v', lambda e, ot=ot: e.tensor_tensor(
                    out=xmax[:, :], in0=xmax[:, :], in1=xa[ot % 2][:, :],
                    op=ALU.max),
                   [f'xa{ot % 2}', 'xmax'], ['xmax'], accum=True, tag=f'hm{tc}_{ot}')
            OP('s', lambda e, ot=ot: e.activation(
                out=hsqb[:, :], in_=hbuf1[:, ot, :], func=AF.Square),
               [f'hb1_{ot}'], ['hsq'], tag=f'hsqr{tc}_{ot}')
            OP('t', lambda e, ot=ot: e.matmul(
                psR[0:1, :], onescolb[:, :], hsqb[:, :],
                start=(ot == 0), stop=(ot == 63)),
               ['hsq', 'onescolb'], ['psR'], kind='c', accum=(ot > 0),
               tag=f'hsmm{tc}_{ot}')
        for tb in range(4):
            OP('t', lambda e, tb=tb: e.transpose(
                out=psX[:, tb * 128:(tb + 1) * 128],
                in_=xmax[:, tb * 128:(tb + 1) * 128],
                identity=identf[:, :]),
               ['xmax', 'identf', 'psX'], ['psX'], kind='c', accum=(tb > 0),
               tag=f'htp{tc}_{tb}')
        for tb in range(4):
            OP('v', lambda e, tb=tb: e.reduce_max(
                out=mxt[:, tb:tb + 1], in_=psX[:, tb * 128:(tb + 1) * 128], axis=AXF),
               ['psX'], ['mxt'], accum=(tb > 0), tag=f'hrm{tc}_{tb}')
        OP('v', lambda e: e.tensor_scalar(out=rowq[0:1, RA], in0=psR[0:1, :],
                                          scalar1=1.0 / I1, scalar2=None,
                                          op0=ALU.mult),
           ['psR'], ['vrow'], tag=f'vrow{tc}')
        for tb in range(4):
            OP('t', lambda e, tb=tb: e.transpose(
                out=psR[:, tb:tb + 1],
                in_=rowq[0:1, tb * 128:(tb + 1) * 128],
                identity=identf[0:1, 0:1]),
               ['vrow', 'identf', 'psR'], ['psR'], kind='c', accum=True,
               tag=f'vtp{tc}_{tb}')
        OP('v', lambda e: e.tensor_copy(out=vrt[:, :], in_=psR[:, 0:4]),
           ['psR'], ['vrt'], tag=f'vrt{tc}')
        OP('s', lambda e: e.activation(out=vrt[:, :], in_=vrt[:, :],
                                       func=AF.Sqrt, bias=epsB[:, 0:1]),
           ['vrt', 'epsB'], ['vrt'], accum=True, tag=f'hsd{tc}')
        OP('v', lambda e: e.reciprocal(out=rstd4[:, :], in_=vrt[:, :]),
           ['vrt'], ['rstd4'], tag=f'hrstd{tc}')
        OP('v', lambda e: e.tensor_tensor(out=mxt[:, :], in0=mxt[:, :],
                                          in1=rstd4[:, :], op=ALU.mult),
           ['mxt', 'rstd4'], ['mxt'], accum=True, tag=f'hmh1_{tc}')
        OP('v', lambda e: e.tensor_scalar(out=mxt[:, :], in0=mxt[:, :],
                                          scalar1=EPS_Q, scalar2=None, op0=ALU.max),
           ['mxt'], ['mxt'], accum=True, tag=f'hmh2_{tc}')
        OP('v', lambda e: e.reciprocal(out=ct4[:, :], in_=mxt[:, :]),
           ['mxt'], ['ct4'], tag=f'hsh{tc}')
        OP('v', lambda e: e.tensor_tensor(out=ct4[:, :], in0=ct4[:, :],
                                          in1=rstd4[:, :], op=ALU.mult),
           ['ct4', 'rstd4'], ['ct4'], accum=True, tag=f'hct{tc}')
        OP('t', lambda e: e.transpose(out=psX[0:4, 0:128], in_=ct4[:, :],
                                      identity=identf[:, :]),
           ['ct4', 'identf', 'psX'], ['psX'], tag=f'hctp{tc}')
        OP('v', lambda e: e.tensor_copy(out=sx4[:, :], in_=psX[0:4, 0:128]),
           ['psX'], ['sx4'], tag=f'hc4_{tc}')
        OP('t', lambda e: e.transpose(out=psX[0:4, 0:128], in_=mxt[:, :],
                                      identity=identf[:, :]),
           ['mxt', 'identf', 'psX'], ['psX'], tag=f'hmtp{tc}')
        OP('v', lambda e: e.tensor_copy(out=mx4[:, :], in_=psX[0:4, 0:128]),
           ['psX'], ['mx4'], tag=f'hm4_{tc}')
        for tb in range(4):
            OP('t', lambda e, tb=tb: e.matmul(
                psX[:, tb * 128:(tb + 1) * 128],
                sel4_sb[:, tb * 128:(tb + 1) * 128], sx4[:, :],
                start=True, stop=True),
               ['sx4', 'sel4_sb', 'psX'], ['psX'], kind='c', accum=(tb > 0),
               tag=f'hbcr{tc}_{tb}')
        OP('v', lambda e: e.tensor_scalar(out=bc1[:, :], in0=psX[:, :],
                                          scalar1=127.0, scalar2=None, op0=ALU.mult),
           ['psX'], ['bc1'], tag=f'hbc1_{tc}')
        for tb in range(4):
            OP('t', lambda e, tb=tb: e.matmul(
                psX[:, tb * 128:(tb + 1) * 128],
                sel4_sb[:, tb * 128:(tb + 1) * 128], mx4[:, :],
                start=True, stop=True),
               ['mx4', 'sel4_sb', 'psX'], ['psX'], kind='c', accum=(tb > 0),
               tag=f'hbcm{tc}_{tb}')
        OP('v', lambda e: e.tensor_copy(out=bc2[:, :], in_=psX[:, :]),
           ['psX'], ['bc2'], tag=f'hbc2_{tc}')
        if upto == 'F1d' and tc == 0:
            dma('q', dbg2[:, :, :], hbuf1[:, 0:16, :],
                [f'hb1_{o}' for o in range(16)], ['dbgd2'], tag='dhpre')
        for ot in range(64):
            OP('v', lambda e, ot=ot: e.scalar_tensor_tensor(
                out=qt[:, :], in0=hbuf1[:, ot, :], scalar=lnw_sb[:, ot:ot + 1],
                in1=bc1[:, :], op0=ALU.mult, op1=ALU.mult),
               [f'hb1_{ot}', 'lnw_sb', 'bc1'], ['qt'], tag=f'qhm{tc}_{ot}')
            OP('v', lambda e, ot=ot: e.tensor_scalar(
                out=hbuf1[:, ot, :], in0=qt[:, :], scalar1=MAGIC, scalar2=MAGIC,
                op0=ALU.add, op1=ALU.subtract),
               ['qt'], [f'hb1_{ot}'], tag=f'qhr{tc}_{ot}')
        if upto == 'F1d' and tc == 0:
            dma('q', dbg3[:, :, :], hbuf1[:, 0:16, :],
                [f'hb1_{o}' for o in range(16)], ['dbgd3'], tag='dhpost')
        for ot in range(16):
            rnk, ot2 = ot // 2, ot % 2
            bank, pk = psG[ot % 2], f'psG{ot % 2}'
            for half in range(4):
                dw = qdb[half % 2]
                dma('q', dw[:, :], qd_o[rnk * 128:(rnk + 1) * 128,
                                        ot2 * 8192 + half * 2048:
                                        ot2 * 8192 + (half + 1) * 2048],
                    ['qAGd'], [f'qdb{half % 2}'], tag=f'ldqd{tc}_{ot}_{half}')
                for kt in range(16):
                    gk = half * 16 + kt
                    OP('t', lambda e, kt=kt, gk=gk, bank=bank, dw=dw,
                       first=(gk == 0), last=(gk == 63): e.matmul(
                        bank[:, :], dw[:, kt * 128:(kt + 1) * 128],
                        hbuf1[:, gk, :], start=first, stop=last),
                       [f'qdb{half % 2}', f'hb1_{gk}'], [pk], kind='c', accum=(gk > 0),
                       tag=f'dmm{tc}_{ot}_{gk}')
            OP('v', lambda e, bank=bank: e.scalar_tensor_tensor(
                out=qt[:, :], in0=bank[:, :], scalar=scol[:, 2:3], in1=bc2[:, :],
                op0=ALU.mult, op1=ALU.mult),
               [pk, 'scol', 'bc2'], ['qt'], tag=f'dev{tc}_{ot}')
            OP('v', lambda e, ot=ot, tc=tc: e.tensor_tensor(
                out=rbuf[:, ot, :], in0=qt[:, :],
                in1=stream[:, ot, tc * TC:(tc + 1) * TC], op=ALU.add),
               ['qt', f'stream_{ot}'], ['rB'], accum=(ot > 0), tag=f'drs{tc}_{ot}')

        def out_fn(kt, tc=tc):
            OP('v', lambda e, kt=kt: e.tensor_tensor(
                out=rbuf[:, kt, :], in0=rbuf[:, kt, :], in1=psM[:, :],
                op=ALU.mult),
               ['rB', 'ln4_psm'], ['rB'], accum=True, tag=f'ln4o{tc}_{kt}')
        emit_ln('ln4', 'rB', rbuf, out_fn, lnps)
        for tb in range(4):
            for hh in range(4):
                for kt in range(4):
                    gkt = hh * 4 + kt
                    OP('t', lambda e, gkt=gkt, tb=tb: e.transpose(
                        out=psX[:, 0:128],
                        in_=rbuf[:, gkt, tb * 128:(tb + 1) * 128],
                        identity=identf[:, :]),
                       ['rB', 'identf'], ['psX'], kind='c', tag=f'tp{tc}_{tb}_{gkt}')
                    OP('v', lambda e, kt=kt: e.tensor_copy(
                        out=obuf[:, kt, :], in_=psX[:, 0:128]),
                       ['psX'], ['obuf'], accum=(kt > 0), tag=f'tpc{tc}_{tb}_{gkt}')
                okey = f'outst{tc}_{tb}_{hh}'
                dma('q', out_t[(tc * 4 + tb) * 128:(tc * 4 + tb + 1) * 128,
                               hh * 512:(hh + 1) * 512]
                    .rearrange("p (kt c) -> p kt c", c=128),
                    obuf[:, :, :], ['obuf'], [okey], tag=okey)
                outks.append(okey)

    if upto == 'F1d':
        outks += ['dbgd1', 'dbgd2', 'dbgd3', 'dbgd4a', 'dbgd4b', 'dbgd4c',
                  'dbgd4d', 'dbgd4e', 'dbgd5']
    return finish(outks)



def kernel(**inputs):
    in_maps = stage_inputs(inputs)
    nc = build_program()
    res = run_bass_kernel_spmd(nc, in_maps, core_ids=list(range(NCORE)))
    out = np.empty((B, T, D), np.float32)
    for c in range(NCORE):
        g, j = c // 4, c % 4
        ch0, ch1 = own_chunks(j)
        o = np.asarray(res.results[c]['out'])
        out[g, ch0 * TC:(ch0 + 1) * TC] = o[:TC]
        out[g, ch1 * TC:(ch1 + 1) * TC] = o[TC:]
    return out


# version 09
# speedup vs baseline: 1.3432x; 1.1106x over previous
import contextlib
import numpy as np
import concourse.bass as bass
import concourse.mybir as mybir
from concourse.bass import ts as dslice
from concourse.bass_utils import run_bass_kernel_spmd

F32 = mybir.dt.float32
BF16 = mybir.dt.bfloat16
I32 = mybir.dt.int32
AF = mybir.ActivationFunctionType
ALU = mybir.AluOpType
AXF = mybir.AxisListType.X

B, T, D = 2, 4096, 2048
DI, DS, DCONV, DTR = 4096, 16, 4, 128
I0, I1 = 5632, 8192
H, HD, G, GRP = 32, 64, 8, 4
NCORE = 8
TC = 512
NCH = T // TC
DSL = DI // 4
EPS_LN = 1e-5
EPS_Q = 1e-5
RMS_EPS = 1e-6
MAGIC = 12582912.0
NEGB = -60.0
RG4 = [[0, 1, 2, 3], [4, 5, 6, 7]]
RG8 = [[0, 1, 2, 3, 4, 5, 6, 7]]

def own_chunks(j):
    return (j, NCH - 1 - j)

def chunk_pos(c):
    return (c, 0) if c < 4 else (7 - c, 1)



class Prog:
    NDMASEM = 8

    def __init__(self):
        self.ops = []
        self.last_write = {}
        self.readers = {}
        self.fence_deps = []

    def op(self, eng, fn, reads=(), writes=(), kind='c', accum=False, tag=''):
        deps = set(self.fence_deps)
        for k in reads:
            deps.update(self.last_write.get(k, ()))
        for k in writes:
            if not accum:
                deps.update(self.last_write.get(k, ()))
            deps.update(self.readers.get(k, ()))
        i = len(self.ops)
        deps.discard(i)
        self.ops.append(dict(eng=eng, fn=fn, deps=deps, kind=kind, tag=tag,
                             inc=False))
        for k in reads:
            self.readers.setdefault(k, []).append(i)
        for k in writes:
            if accum and k in self.last_write:
                self.last_write[k].append(i)
            else:
                self.last_write[k] = [i]
                self.readers[k] = []
        return i

    def fence(self):
        last = {}
        for i, o in enumerate(self.ops):
            last[(o['eng'], o['kind'])] = i
        self.fence_deps = sorted(last.values())

    def emit(self, nc, block):
        ops = self.ops
        for o in ops:
            for d in o['deps']:
                ops[d]['inc'] = True
        engs = sorted({(o['eng'], o['kind']) for o in ops})
        sems = {}
        self._sem_cms = []
        for e, k in engs:
            if k == 'c':
                cm = nc.semaphore(f"sem_{e}_{k}")
                s = cm.__enter__(); self._sem_cms.append(cm)
                sems[(e, k)] = [s]
            elif k == 'd':
                lst = []
                for n in range(self.NDMASEM):
                    cm = nc.semaphore(f"sem_{e}_{k}{n}")
                    lst.append(cm.__enter__()); self._sem_cms.append(cm)
                sems[(e, k)] = lst
            else:
                lst = []
                for n in range(2):
                    cm = nc.semaphore(f"sem_{e}_cc{n}")
                    lst.append(cm.__enter__()); self._sem_cms.append(cm)
                sems[(e, k)] = lst
        counts = {}
        for i, o in enumerate(ops):
            key = (o['eng'], o['kind'])
            pool = sems[key]
            n = counts.get(key, 0)
            counts[key] = n + 1
            if o['kind'] == 'c':
                o['sem'] = pool[0]
                o['val'] = None
            else:
                o['sem'] = pool[n % len(pool)]
                o['use'] = n // len(pool)
                o['step'] = 16 if o['kind'] == 'd' else 1
        ccount = {}
        for o in ops:
            if o['kind'] == 'c':
                key = (o['eng'], o['kind'])
                c = ccount.get(key, 0)
                if o['inc']:
                    c += 1
                ccount[key] = c
                o['val'] = c
        for o in ops:
            if o['kind'] != 'c':
                o['inc'] = True
                o['val'] = (o['use'] + 1) * o['step']
        stream_clock = {}
        for i, o in enumerate(ops):
            skey = o['eng']
            clk = stream_clock.setdefault(skey, {})
            waits = []
            for d in sorted(o['deps']):
                od = ops[d]
                sem, val = od['sem'], od['val']
                if od['kind'] == 'c' and val == 0:
                    continue
                if clk.get(id(sem), -1) >= val:
                    continue
                waits.append((sem, val, d))
                clk[id(sem)] = val
            if o['kind'] != 'c' and o['use'] > 0:
                prev_val = o['use'] * o['step']
                if clk.get(id(o['sem']), -1) < prev_val:
                    waits.append((o['sem'], prev_val, -1))
                    clk[id(o['sem'])] = prev_val
            o['waits'] = waits
            for (_, _, d) in waits:
                if d >= 0:
                    pclk = ops[d].get('clock')
                    if pclk:
                        for sid, v in pclk.items():
                            if clk.get(sid, -1) < v:
                                clk[sid] = v
            o['clock'] = dict(clk)
        by_eng = {}
        for o in ops:
            by_eng.setdefault(o['eng'], []).append(o)
        emap = {'t': block.tensor, 'v': block.vector, 's': block.scalar,
                'g': block.gpsimd, 'q': block.sync}
        for ename, elist in by_eng.items():
            def make(elist_):
                def body(eng):
                    for o in elist_:
                        for (sem, val, _) in o['waits']:
                            eng.wait_ge(sem, val)
                        instr = o['fn'](eng)
                        if o['inc']:
                            if o['kind'] == 'c':
                                instr.then_inc(o['sem'], 1)
                            else:
                                instr.then_inc(o['sem'], 16 if o['kind'] == 'd' else 1)
                return body
            emap[ename](make(elist))



def stage_inputs(inp):
    f32 = np.float32
    asf = lambda a: np.asarray(a, f32)

    x = asf(inp['L0_attn_x'])
    xT = [np.ascontiguousarray(x[g].T) for g in range(B)]

    winT = np.ascontiguousarray(asf(inp['L0_attn_W_in']).T)
    wxT = np.ascontiguousarray(asf(inp['L0_attn_W_x']).T)
    wdt = asf(inp['L0_attn_W_dt'])
    woutT = np.ascontiguousarray(asf(inp['L0_attn_W_out']).T)
    conv_w = asf(inp['L0_attn_conv_w'])
    conv_b = asf(inp['L0_attn_conv_b'])
    b_dt = asf(inp['L0_attn_b_dt'])
    A_log = asf(inp['L0_attn_A_log'])
    Dvec = asf(inp['L0_attn_D'])

    g0T = np.ascontiguousarray(asf(inp['L0_ffn_gate_w']).T)
    u0T = np.ascontiguousarray(asf(inp['L0_ffn_up_w']).T)
    d0T = np.ascontiguousarray(asf(inp['L0_ffn_down_w']).T)
    g0s = asf(inp['L0_ffn_gate_s'])
    u0s = asf(inp['L0_ffn_up_s'])
    d0s = asf(inp['L0_ffn_down_s'])

    wq = asf(inp['L1_attn_Wq'])
    wk = asf(inp['L1_attn_Wk'])
    wv = asf(inp['L1_attn_Wv'])
    wo = asf(inp['L1_attn_Wo'])

    g1T = np.ascontiguousarray(asf(inp['L1_ffn_w_gate']).T)
    u1T = np.ascontiguousarray(asf(inp['L1_ffn_w_up']).T)
    d1T = np.ascontiguousarray(asf(inp['L1_ffn_w_down']).T)
    lnw = asf(inp['L1_ffn_ln_weight'])

    def lhsT_stage(wT, K, O):
        t = wT.reshape(K // 128, 128, O // 128, 128)
        return np.ascontiguousarray(t.transpose(2, 1, 0, 3).reshape(O // 128, 128, K))

    def win_stage(j):
        cols = np.concatenate([winT[:, j * DSL:(j + 1) * DSL],
                               winT[:, DI + j * DSL:DI + (j + 1) * DSL]], axis=1)
        return np.ascontiguousarray(cols.reshape(16, 128, 2048))

    half = HD // 2
    inv_freq = (1.0 / (10000.0 ** (np.arange(half, dtype=np.float64) * 2.0 / HD)))
    ang = np.arange(T, dtype=np.float64)[:, None] * inv_freq[None, :]
    cosT = np.cos(ang).astype(f32).T
    sinT = np.sin(ang).astype(f32).T
    rc = np.concatenate([cosT, cosT, cosT, cosT], 0)
    rs = np.concatenate([-sinT, sinT, -sinT, sinT], 0)

    perm = np.zeros((128, 128), f32)
    for m in range(128):
        perm[m ^ 32, m] = 1.0
    ident = np.eye(128, dtype=f32)
    selb = np.zeros((32, 32, 128), f32)
    for r in range(32):
        selb[r, r, :] = 1.0
    selb = np.ascontiguousarray(selb.transpose(1, 0, 2).reshape(32, 32 * 128))
    sel4 = np.zeros((4, 4, 128), f32)
    for r in range(4):
        sel4[r, r, :] = 1.0
    sel4 = np.ascontiguousarray(sel4.transpose(1, 0, 2).reshape(4, 4 * 128))

    kk = np.arange(128)[:, None]
    qq = np.arange(TC)[None, :]
    tri = np.stack([(128 * o + kk <= qq).astype(f32) for o in range(4)])

    in_maps = []
    for c in range(NCORE):
        g, j = c // 4, c % 4
        sl = slice(j * DSL, (j + 1) * DSL)
        ch0, ch1 = own_chunks(j)
        own_t = np.r_[ch0 * TC:(ch0 + 1) * TC, ch1 * TC:(ch1 + 1) * TC]

        qcols = []
        for t4 in range(4):
            qcols.append(np.arange((4 * ch0 + t4) * 64, (4 * ch0 + t4 + 1) * 64))
            qcols.append(np.arange((4 * ch1 + t4) * 64, (4 * ch1 + t4 + 1) * 64))
        qcols = np.concatenate(qcols)
        kvcols = np.r_[ch0 * 64:(ch0 + 1) * 64, ch1 * 64:(ch1 + 1) * 64]
        wq_sl = np.ascontiguousarray(wq[:, qcols])
        wk_sl = np.ascontiguousarray(wk[:, kvcols])
        wv_sl = np.ascontiguousarray(wv[:, kvcols])

        def q_stage(wT, OTs):
            t = lhsT_stage(wT, wT.shape[0], wT.shape[1])[OTs]
            return np.ascontiguousarray(t.transpose(1, 0, 2).reshape(128, -1))

        xo = x[g][own_t].T
        m = {
            'xT': xT[g],
            'xown': np.ascontiguousarray(xo.reshape(16, 128, 1024).transpose(1, 0, 2)),
            'winT': win_stage(j),
            'wx': np.ascontiguousarray(wxT[sl].reshape(8, 128, 160).transpose(1, 0, 2)),
            'wdt': np.ascontiguousarray(
                wdt[sl].reshape(8, 128, DTR).transpose(2, 0, 1)),
            'convw': np.ascontiguousarray(conv_w[sl].reshape(8, 128, 4).transpose(1, 0, 2)),
            'convb': np.ascontiguousarray(conv_b[sl].reshape(8, 128).T)[:, :, None],
            'bdt': np.ascontiguousarray(b_dt[sl].reshape(8, 128).T)[:, :, None],
            'alog': np.ascontiguousarray(A_log[sl].reshape(8, 128, DS).transpose(1, 0, 2)),
            'dvec': np.ascontiguousarray(Dvec[sl].reshape(8, 128).T)[:, :, None],
            'wout': lhsT_stage(woutT, DI, D),
            'g0': lhsT_stage(g0T, D, I0),
            'u0': lhsT_stage(u0T, D, I0),
            'd0': lhsT_stage(d0T, I0, D),
            'g0s': np.ascontiguousarray(g0s.reshape(44, 128).T),
            'u0s': np.ascontiguousarray(u0s.reshape(44, 128).T),
            'd0s': np.ascontiguousarray(d0s.reshape(16, 128).T),
            'wqh': lhsT_stage(wq_sl, D, 512),
            'wkh': lhsT_stage(wk_sl, D, 128)[0],
            'wo': lhsT_stage(wo, D, D),
            'wvh': np.ascontiguousarray(
                wv_sl.reshape(16, 128, 128).transpose(1, 0, 2)),
            'qg': q_stage(g1T, list(range(8 * c, 8 * c + 8))),
            'qu': q_stage(u1T, list(range(8 * c, 8 * c + 8))),
            'qd': q_stage(d1T, [2 * c, 2 * c + 1]),
            'lnw': np.ascontiguousarray(lnw.reshape(64, 128).T),
            'rc': rc,
            'rsn': rs,
            'perm': perm,
            'ident': ident,
            'selb': selb,
            'tri': tri,
            'sel4': sel4,
        }
        in_maps.append(m)
    return in_maps



def build_program(upto='full'):
    nc = bass.Bass("TRN2", target_bir_lowering=False, debug=False)
    p = Prog()
    OP = p.op

    def din(name, shape):
        return nc.dram_tensor(name, list(shape), F32, kind="ExternalInput").ap()

    xT = din('xT', [D, T])
    xown_d = din('xown', [128, 16, 1024])
    winT = din('winT', [16, 128, 2048])
    wx = din('wx', [128, 8, 160])
    wdt = din('wdt', [128, 8, 128])
    convw = din('convw', [128, 8, 4])
    convb = din('convb', [128, 8, 1])
    bdt = din('bdt', [128, 8, 1])
    alog = din('alog', [128, 8, DS])
    dvec = din('dvec', [128, 8, 1])
    wout_d = din('wout', [16, 128, 4096])
    g0_d = din('g0', [44, 128, 2048]); u0_d = din('u0', [44, 128, 2048])
    d0_d = din('d0', [16, 128, 5632])
    g0s_d = din('g0s', [128, 44]); u0s_d = din('u0s', [128, 44]); d0s_d = din('d0s', [128, 16])
    wqh_d = din('wqh', [4, 128, 2048]); wkh_d = din('wkh', [128, 2048])
    wo_d = din('wo', [16, 128, 2048]); wvh_d = din('wvh', [128, 16, 128])
    qg_d = din('qg', [128, 8 * 2048]); qu_d = din('qu', [128, 8 * 2048])
    qd_d = din('qd', [128, 2 * 8192])
    lnw_d = din('lnw', [128, 64])
    rc_d = din('rc', [128, T]); rs_d = din('rsn', [128, T])
    perm_d = din('perm', [128, 128])
    selb_d = din('selb', [32, 32 * 128])
    ident_d = din('ident', [128, 128])
    tri_d = din('tri', [4, 128, TC])
    sel4_d = din('sel4', [4, 4 * 128])

    ar_in = [nc.dram_tensor(f'ar_in{i}', [160, TC], BF16).ap() for i in range(NCH)]
    ar_out = [nc.dram_tensor(f'ar_out{i}', [160, TC], BF16).ap() for i in range(NCH)]
    yag_in = [nc.dram_tensor(f'yag_in{i}', [DSL, TC], BF16).ap() for i in range(NCH)]
    yag_all = nc.dram_tensor('yag_all', [NCH, DI, TC], BF16).ap()
    qs_in = nc.dram_tensor('qs_in', [1, 4], F32).ap()
    qs_out = nc.dram_tensor('qs_out', [1, 4], F32).ap()
    qg_i = nc.dram_tensor('qg_i', [128, 16384], BF16).ap()
    qu_i = nc.dram_tensor('qu_i', [128, 16384], BF16).ap()
    qd_i = nc.dram_tensor('qd_i', [128, 16384], BF16).ap()
    qg_o = nc.dram_tensor('qg_o', [1024, 16384], BF16).ap()
    qu_o = nc.dram_tensor('qu_o', [1024, 16384], BF16).ap()
    qd_o = nc.dram_tensor('qd_o', [1024, 16384], BF16).ap()
    sag_i = nc.dram_tensor('sag_i', [D, 1024], BF16).ap()
    sag_o = nc.dram_tensor('sag_o', [4 * D, 1024], BF16).ap()

    out_t = nc.dram_tensor('out', [1024, D], F32, kind="ExternalOutput").ap()
    dbg = None
    if upto == 'F1d':
        dbg = nc.dram_tensor('dbg', [128, 16, TC], BF16, kind="ExternalOutput").ap()
        dbg2 = nc.dram_tensor('dbg2', [128, 16, TC], BF16, kind="ExternalOutput").ap()
        dbg3 = nc.dram_tensor('dbg3', [128, 16, TC], BF16, kind="ExternalOutput").ap()
        dbg4 = nc.dram_tensor('dbg4', [128, 2048], F32, kind="ExternalOutput").ap()
        dbg5 = nc.dram_tensor('dbg5', [128, 2048], BF16, kind="ExternalOutput").ap()
    if upto == 'M':
        dbg = nc.dram_tensor('dbg', [DI, 1024], BF16, kind="ExternalOutput").ap()
    elif upto not in ('full', 'F1d'):
        dbg = nc.dram_tensor('dbg', [128, 16, 1024], BF16, kind="ExternalOutput").ap()
        if upto == 'A2':
            dbgq = nc.dram_tensor('dbgq', [128, 16, 1024], BF16, kind="ExternalOutput").ap()
            dbgk = nc.dram_tensor('dbgk', [128, 16, 1024], BF16, kind="ExternalOutput").ap()
            dbgv = nc.dram_tensor('dbgv', [128, 32, 520], BF16, kind="ExternalOutput").ap()

    stack = contextlib.ExitStack()

    def sb(name, shape, dt):
        return stack.enter_context(nc.sbuf_tensor(name, list(shape), dt))

    open_arenas = []

    def mk_arena():
        st = contextlib.ExitStack()
        def sba(name, shape, dt):
            return st.enter_context(nc.sbuf_tensor(name, list(shape), dt))
        def psa(name, shape):
            return st.enter_context(nc.psum_tensor(name, list(shape), F32))
        class A:
            def close(self):
                if st in open_arenas:
                    open_arenas.remove(st)
                st.close()
        open_arenas.append(st)
        return A(), sba, psa

    def dma(q, out, in_, reads, writes, tag='', accum=False):
        return OP(q, lambda e, o=out, i=in_: e.dma_start(out=o, in_=i),
                  reads=reads, writes=writes, kind='d', tag=tag, accum=accum)

    identf = sb('identf', [128, 128], F32)
    identb = sb('identb', [128, 128], BF16)
    onesb = sb('onesb', [1, 128], BF16)
    onescolb = sb('onescolb', [128, 1], BF16)
    onesrowF = sb('onesrowF', [1, 128], F32)
    ones127 = sb('ones127', [1, 128], F32)
    onescolF = sb('onescolF', [128, 1], F32)
    swcol = sb('swcol', [128, 4], F32)
    mcol = sb('mcol', [128, 4], F32)
    qrows = sb('qrows', [1, 16], F32)
    epsc = sb('epsc', [1, 2], F32)

    dma('q', identf[:, :], ident_d[:, :], ['identd'], ['identf'])
    OP('v', lambda e: e.tensor_copy(out=identb[:, :], in_=identf[:, :]),
       ['identf'], ['identb'])
    OP('v', lambda e: e.memset(onesb[:, :], 1.0), [], ['onesb'])
    OP('v', lambda e: e.memset(onescolb[:, :], 1.0), [], ['onescolb'])
    OP('v', lambda e: e.memset(onesrowF[:, :], 1.0), [], ['onesrowF'])
    OP('v', lambda e: e.memset(ones127[:, :], 127.0), [], ['ones127'])
    OP('v', lambda e: e.memset(onescolF[:, :], 1.0), [], ['onescolF'])
    OP('v', lambda e: e.memset(epsc[0:1, 0:1], EPS_LN), [], ['epsc'], accum=True)
    OP('v', lambda e: e.memset(epsc[0:1, 1:2], RMS_EPS), [], ['epsc'], accum=True)

    a1, sb1, ps1 = mk_arena()

    qbuf = [sb1('qbuf0', [128, 512], F32)] * 2
    qout = sb1('qout', [128, 512], BF16)
    qint = sb1('qint', [128, 512], F32)
    qred = sb1('qred', [128, 1], F32)
    qacc = sb1('qacc', [128, 4], F32)

    winT_sb = sb1('winT_sb', [128, 16, 2048], BF16)
    wdt_sb = sb1('wdt_sb', [128, 8, 128], BF16)
    wx_sb = sb1('wx_sb', [128, 8, 160], BF16)
    convw_sb = sb1('convw_sb', [128, 8, 4], F32)
    convb_sb = sb1('convb_sb', [128, 8, 1], F32)
    bdt_sb = sb1('bdt_sb', [128, 8, 1], F32)
    dvec_sb = sb1('dvec_sb', [128, 8, 1], F32)
    aneg = sb1('aneg', [128, 8, DS], F32)
    state = sb1('state', [128, 8, DS], F32)
    xbf = sb1('xbf', [128, 16, TC], BF16)
    xse = sb1('xse', [128, 8, TC + 3], BF16)
    tails = sb1('tails', [128, 8, 3], BF16)
    zbuf = [sb1(f'zb{i}', [128, 8, TC], BF16) for i in range(2)]
    ubuf = [sb1(f'ub{i}', [128, 8, TC], BF16) for i in range(2)]
    dtbuf = [sb1(f'dtb{i}', [128, 8, TC], BF16) for i in range(2)]
    vtile = [sb1('vt0', [128, TC], BF16)] * 2
    dtrb = [sb1('dtrb0', [128, TC], BF16)] * 2
    bcb = [sb1('bcb0', [32, TC], BF16)] * 2
    bca = sb1('bca', [128, 2 * DS, TC], BF16)
    selb_sb = sb1('selb_sb', [32, 32, 128], BF16)
    xdl_dt = sb1('xdl_dt', [128, TC], BF16)
    xdl_bc = sb1('xdl_bc', [32, TC], BF16)
    convt = [sb1('convt0', [128, TC], BF16)] * 2
    dAb = [sb1(f'dAb{i}', [128, TC], BF16) for i in range(3)]
    dBub = [sb1(f'dBub{i}', [128, TC], BF16) for i in range(3)]
    hb = [sb1(f'hb{i}', [128, TC], BF16) for i in range(2)]
    hcb = [sb1(f'hcb{i}', [128, TC], BF16) for i in range(2)]
    ygb = sb1('ygb', [128, 1, TC], BF16)
    yev = sb1('yev', [128, TC], BF16)

    ps_front = [ps1('ps_front0', [128, 512]), ps1('ps_front1', [128, 512])]
    ps_bc = [ps1('ps_bc0', [128, 512]), ps1('ps_bc1', [128, 512])]
    ps_y = [ps1('ps_y0', [128, 512]), ps1('ps_y1', [128, 512])]
    ps_misc = [ps1('ps_misc0', [128, 512]), ps1('ps_misc1', [128, 512])]

    OP('v', lambda e: e.memset(qacc[:, :], 0.0), [], ['qacc'])
    qsrcs = [(qg_d, qg_i, 0), (qu_d, qu_i, 1), (qd_d, qd_i, 2)]
    NQT = 32
    for (src, _, m) in qsrcs:
        for t in range(NQT):
            b = t % 2
            dma('q', qbuf[b][:, :], src[:, t * 512:(t + 1) * 512],
                [f'q_src{m}'], ['qbuf0'], tag=f'qld{m}_{t}')
            OP('s', lambda e, b=b: e.activation(
                out=qout[:, :], in_=qbuf[b][:, :], func=AF.Abs,
                accum_out=qred[:, 0:1]),
               ['qbuf0'], ['qred', 'qout'], tag=f'qsum{m}_{t}')
            OP('v', lambda e, m=m: e.tensor_tensor(
                out=qacc[:, m:m + 1], in0=qacc[:, m:m + 1], in1=qred[:, 0:1],
                op=ALU.add),
               ['qred', 'qacc'], ['qacc'], accum=True, tag=f'qadd{m}_{t}')
    OP('t', lambda e: e.matmul(ps_misc[0][0:1, 0:4], onescolF[:, :], qacc[:, :],
                               start=True, stop=True),
       ['onescolF', 'qacc'], ['psm0'], tag='qcross')
    OP('v', lambda e: e.tensor_copy(out=qrows[:, 0:4], in_=ps_misc[0][0:1, 0:4]),
       ['psm0'], ['qrows03'], tag='qrow')
    dma('q', qs_in[:, :], qrows[:, 0:4], ['qrows03'], ['qs_in_d'])
    OP('g', lambda e: e.collective_compute(
        "AllReduce", ALU.add, replica_groups=RG8,
        ins=[qs_in.opt()], outs=[qs_out.opt()]),
       ['qs_in_d'], ['qs_out_d'], kind='cc', tag='qAR')
    dma('q', qrows[:, 4:8], qs_out[:, :], ['qs_out_d'], ['qrows47'])
    OP('v', lambda e: e.tensor_scalar(
        out=qrows[:, 8:12], in0=qrows[:, 4:8], scalar1=1.0 / 16777216.0,
        scalar2=EPS_Q, op0=ALU.mult, op1=ALU.max),
       ['qrows47'], ['qrows8'], tag='qmean')
    OP('v', lambda e: e.reciprocal(out=qrows[:, 12:16], in_=qrows[:, 8:12]),
       ['qrows8'], ['qrows12'], tag='qrecip')
    OP('t', lambda e: e.matmul(ps_misc[0][:, 4:8], onesrowF[:, :], qrows[:, 8:12],
                               start=True, stop=True),
       ['onesrowF', 'qrows8'], ['psm0'], tag='qbc1')
    OP('v', lambda e: e.tensor_copy(out=mcol[:, :], in_=ps_misc[0][:, 4:8]),
       ['psm0'], ['mcol'], tag='qbc1c')
    OP('t', lambda e: e.matmul(ps_misc[0][:, 8:12], onesrowF[:, :], qrows[:, 12:16],
                               start=True, stop=True),
       ['onesrowF', 'qrows12'], ['psm0'], tag='qbc2')
    OP('v', lambda e: e.tensor_copy(out=swcol[:, :], in_=ps_misc[0][:, 8:12]),
       ['psm0'], ['swcol'], tag='qbc2c')
    for (src, dst, m) in qsrcs:
        for t in range(NQT):
            b = t % 2
            dma('q', qbuf[b][:, :], src[:, t * 512:(t + 1) * 512],
                [f'q_src{m}'], ['qbuf0'], tag=f'qld2_{m}_{t}')
            OP('s', lambda e, b=b, m=m: e.activation(
                out=qint[:, :], in_=qbuf[b][:, :], func=AF.Copy,
                scale=swcol[:, m:m + 1]),
               ['qbuf0', 'swcol'], ['qint'], tag=f'qmul{m}_{t}')
            OP('v', lambda e: e.tensor_scalar(
                out=qint[:, :], in0=qint[:, :], scalar1=MAGIC, scalar2=MAGIC,
                op0=ALU.add, op1=ALU.subtract),
               ['qint'], ['qint'], accum=True, tag=f'qrnd{m}_{t}')
            OP('v', lambda e: e.tensor_scalar(
                out=qout[:, :], in0=qint[:, :], scalar1=1.0, scalar2=-1.0,
                op0=ALU.min, op1=ALU.max),
               ['qint'], ['qout'], tag=f'qclip{m}_{t}')
            dma('q', dst[:, t * 512:(t + 1) * 512], qout[:, :],
                ['qout'], [f'qst{m}'], tag=f'qst{m}_{t}', accum=(t > 0))

    for kc in range(16):
        dma('g', winT_sb[:, kc, :], winT[kc, :, :], ['winT_d'], [f'winsb{kc}'],
            tag=f'ldwin{kc}')
    dma('g', wdt_sb[:, :, :], wdt[:, :, :], ['wdt_d'], ['wdtsb'])
    dma('g', wx_sb[:, :, :], wx[:, :, :], ['wx_d'], ['wxsb'])
    dma('g', selb_sb[:, :, :],
        selb_d.rearrange("p (r c) -> p r c", c=128), ['selb_d'], ['selsb'])
    dma('q', convw_sb[:, :, :], convw[:, :, :], ['cw_d'], ['cwsb'])
    dma('q', convb_sb[:, :, :], convb[:, :, :], ['cb_d'], ['cbsb'])
    dma('q', bdt_sb[:, :, :], bdt[:, :, :], ['bdt_d'], ['bdtsb'])
    dma('q', dvec_sb[:, :, :], dvec[:, :, :], ['dv_d'], ['dvsb'])
    dma('q', aneg[:, :, :], alog[:, :, :], ['al_d'], ['alsb'])
    OP('s', lambda e: e.activation(out=aneg[:, :, :], in_=aneg[:, :, :], func=AF.Exp),
       ['alsb'], ['anegp'], accum=True)
    OP('v', lambda e: e.tensor_scalar(out=aneg[:, :, :], in0=aneg[:, :, :],
                                      scalar1=-1.0, scalar2=None, op0=ALU.mult),
       ['anegp'], ['aneg'], accum=True)
    OP('v', lambda e: e.memset(state[:, :, :], 0.0), [], ['state'])
    OP('v', lambda e: e.memset(xse[:, :, 0:3], 0.0), [], ['xse_h'])

    def front(tc):
        b = tc % 2
        if tc > 0:
            OP('v', lambda e: e.tensor_copy(
                out=tails[:, :, :], in_=xse[:, :, TC:TC + 3]),
               ['xse_x'], ['tails'], tag=f'tail{tc}')
        dma('g', xbf[:, :, :],
            xT.rearrange("(kc p) t -> p kc t", p=128)[:, :, tc * TC:(tc + 1) * TC],
            ['xT_d'], ['xbf'], tag=f'ldx{tc}')
        for o in range(16):
            bank = ps_front[o % 2]
            pk = f'psf{o % 2}'
            for k in range(16):
                OP('t', lambda e, o=o, k=k, bank=bank: e.matmul(
                    bank[:, :], winT_sb[:, k, o * 128:(o + 1) * 128],
                    xbf[:, k, :], start=(k == 0), stop=(k == 15)),
                   reads=[f'winsb{k}', 'xbf'], writes=[pk], kind='c',
                   accum=(k > 0), tag=f'proj{tc}_{o}')
            if o < 8:
                OP('v', lambda e, o=o, bank=bank: e.tensor_copy(
                    out=xse[:, o, 3:3 + TC], in_=bank[:, :]),
                   [pk, 'tails'], ['xse_x'], accum=(tc > 0 or o > 0), tag=f'evxs{tc}_{o}')
            else:
                OP('s', lambda e, o=o, bank=bank, b=b: e.activation(
                    out=zbuf[b][:, o - 8, :], in_=bank[:, :], func=AF.Silu),
                   [pk], [f'zb{b}_{o-8}'], tag=f'evz{tc}_{o}')
        if tc > 0:
            OP('v', lambda e: e.tensor_copy(
                out=xse[:, :, 0:3], in_=tails[:, :, :]),
               ['tails', 'xse_x'], ['xse_h'], accum=True, tag=f'head{tc}')
        for d in range(8):
            OP('v', lambda e, d=d: e.tensor_scalar(
                out=convt[0][:, :], in0=xse[:, d, 0:TC],
                scalar1=convw_sb[:, d, 0:1], scalar2=None, op0=ALU.mult),
               ['xse_x', 'xse_h', 'cwsb'], ['cvt0'], tag=f'cv{tc}_{d}_0')
            for k in (1, 2, 3):
                OP('v', lambda e, d=d, k=k: e.scalar_tensor_tensor(
                    out=convt[0][:, :], in0=xse[:, d, k:k + TC],
                    scalar=convw_sb[:, d, k:k + 1], in1=convt[0][:, :],
                    op0=ALU.mult, op1=ALU.add),
                   ['xse_x', 'xse_h', 'cvt0', 'cwsb'], ['cvt0'], accum=True,
                   tag=f'cv{tc}_{d}_{k}')
            OP('s', lambda e, d=d, b=b: e.activation(
                out=ubuf[b][:, d, :], in_=convt[0][:, :], func=AF.Silu,
                bias=convb_sb[:, d, 0:1]),
               ['cvt0', 'cbsb'], [f'ub{b}_{d}'], tag=f'silu_u{tc}_{d}')
        for k in range(8):
            OP('t', lambda e, k=k, b=b: e.matmul(
                ps_misc[0][:, :], wx_sb[:, k, 0:128], ubuf[b][:, k, :],
                start=(k == 0), stop=(k == 7)),
               [f'ub{b}_{k}', 'wxsb'], ['psm0'], kind='c', accum=(k > 0), tag=f'xdl{tc}')
        for k in range(8):
            OP('t', lambda e, k=k, b=b: e.matmul(
                ps_misc[1][0:32, :], wx_sb[:, k, 128:160], ubuf[b][:, k, :],
                start=(k == 0), stop=(k == 7)),
               [f'ub{b}_{k}', 'wxsb'], ['psm1'], kind='c', accum=(k > 0), tag=f'xdlbc{tc}')
        OP('v', lambda e: e.tensor_copy(out=xdl_dt[:, :], in_=ps_misc[0][:, :]),
           ['psm0'], ['xdl_dt'], tag=f'evxdl{tc}')
        OP('v', lambda e: e.tensor_copy(out=xdl_bc[:, :], in_=ps_misc[1][0:32, :]),
           ['psm1'], ['xdl_bc'], tag=f'evxdlbc{tc}')
        dma('q', ar_in[tc][0:128, :], xdl_dt[:, :], ['xdl_dt'], [f'arin{tc}a'])
        dma('q', ar_in[tc][128:160, :], xdl_bc[:, :], ['xdl_bc'], [f'arin{tc}b'])
        OP('g', lambda e, tc=tc: e.collective_compute(
            "AllReduce", ALU.add, replica_groups=RG4,
            ins=[ar_in[tc].opt()], outs=[ar_out[tc].opt()]),
           [f'arin{tc}a', f'arin{tc}b'], [f'arout{tc}'], kind='cc', tag=f'AR{tc}')
    def front_b(tc):
        b = tc % 2
        dma('q', dtrb[b][:, :], ar_out[tc][0:128, :], [f'arout{tc}'], ['dtrb0', 'dtrb1'])
        dma('q', bcb[b][:, :], ar_out[tc][128:160, :], [f'arout{tc}'], ['bcb0', 'bcb1'])
        for o in range(8):
            bank = ps_front[o % 2]
            OP('t', lambda e, o=o, bank=bank, b=b: e.matmul(
                bank[:, :], wdt_sb[:, o, :], dtrb[b][:, :], start=True, stop=True),
               [f'dtrb{b}', 'wdtsb'], [f'psf{o % 2}'], kind='c', tag=f'dtmm{tc}_{o}')
            OP('s', lambda e, o=o, bank=bank: e.activation(
                out=convt[0][:, :], in_=bank[:, :], func=AF.Exp,
                bias=bdt_sb[:, o, 0:1]),
               [f'psf{o % 2}', 'bdtsb', 'cvt0'], ['cvt0'], accum=True,
               tag=f'spe{tc}_{o}')
            OP('s', lambda e, o=o, b=b: e.activation(
                out=dtbuf[b][:, o, :], in_=convt[0][:, :], func=AF.Ln,
                bias=onescolF[:, 0:1]),
               ['cvt0', 'onescolF'], [f'dtb{b}_{o}'], tag=f'softp{tc}_{o}')

    def scanpart(tc):
        b = tc % 2
        for srow in range(32):
            bank = ps_bc[srow % 2]
            OP('t', lambda e, srow=srow, bank=bank, b=b: e.matmul(
                bank[:, :], selb_sb[:, srow, :], bcb[b][:, :],
                start=True, stop=True),
               [f'bcb{b}', 'selsb'], [f'psbc{srow % 2}'], kind='c', tag=f'bc{tc}_{srow}')
            OP('s', lambda e, srow=srow, bank=bank: e.copy(
                out=bca[:, srow, :], in_=bank[:, :]),
               [f'psbc{srow % 2}'], [f'bca_{srow}'], tag=f'bcev{tc}_{srow}')
        for d in range(8):
            OP('v', lambda e, d=d, b=b: e.tensor_tensor(
                out=vtile[d % 2][:, :], in0=dtbuf[b][:, d, :], in1=ubuf[b][:, d, :],
                op=ALU.mult),
               [f'dtb{b}_{d}', f'ub{b}_{d}'], ['vt0', 'vt1'], tag=f'vmul{tc}_{d}')
            ybank = ps_y[d % 2]
            ypk = f'psy{d % 2}'

            def emit_dA(s, d=d, b=b):
                i = s % 3
                OP('s', lambda e, d=d, s=s, i=i, b=b: e.activation(
                    out=dAb[i][:, :], in_=dtbuf[b][:, d, :], func=AF.Exp,
                    scale=aneg[:, d, s:s + 1]),
                   [f'dtb{b}_{d}', 'aneg'], [f'dA{i}'], tag=f'dA{tc}_{d}_{s}')

            def emit_dBu(s, d=d):
                i = s % 3
                OP('g', lambda e, d=d, s=s, i=i: e.tensor_tensor(
                    out=dBub[i][:, :], in0=vtile[d % 2][:, :],
                    in1=bca[:, s, :], op=ALU.mult),
                   ['vt0', f'bca_{s}'], [f'dBu{i}'], tag=f'dBu{tc}_{d}_{s}')

            emit_dA(0); emit_dA(1)
            emit_dBu(0); emit_dBu(1); emit_dBu(2)
            for s in range(DS):
                i = s % 3
                ih = s % 2
                OP('v', lambda e, d=d, s=s, i=i, ih=ih: e.tensor_tensor_scan(
                    out=hb[ih][:, :], data0=dAb[i][:, :], data1=dBub[i][:, :],
                    initial=state[:, d, s:s + 1], op0=ALU.mult, op1=ALU.add),
                   [f'dA{i}', f'dBu{i}', 'state'], [f'hb{ih}'], tag=f'scan{tc}_{d}_{s}')
                if s + 2 < DS:
                    emit_dA(s + 2)
                if s + 3 < DS:
                    emit_dBu(s + 3)
                OP('s', lambda e, d=d, s=s, ih=ih: e.copy(
                    out=state[:, d, s:s + 1], in_=hb[ih][:, TC - 1:TC]),
                   [f'hb{ih}'], ['state'], accum=True, tag=f'stup{tc}_{d}_{s}')
                eng_hc = 'v' if s % 3 else 'g'
                i2 = s % 2
                OP(eng_hc, lambda e, d=d, s=s, ih=ih, i2=i2: e.tensor_tensor(
                    out=hcb[i2][:, :], in0=hb[ih][:, :], in1=bca[:, DS + s, :],
                    op=ALU.mult),
                   [f'hb{ih}', f'bca_{DS+s}'], [f'hcb{i2}'], tag=f'ymul{tc}_{d}_{s}')
                OP('t', lambda e, d=d, s=s, i2=i2, ybank=ybank: e.matmul(
                    ybank[:, :], identb[:, :], hcb[i2][:, :],
                    start=(s == 0), stop=(s == DS - 1)),
                   [f'hcb{i2}', 'identb'], [ypk], kind='c', accum=(s > 0),
                   tag=f'yacc{tc}_{d}_{s}')
            OP('v', lambda e, d=d, b=b, ybank=ybank: e.scalar_tensor_tensor(
                out=yev[:, :], in0=ubuf[b][:, d, :], scalar=dvec_sb[:, d, 0:1],
                in1=ybank[:, :], op0=ALU.mult, op1=ALU.add),
               [f'ub{b}_{d}', ypk, 'dvsb'], ['yev'], tag=f'yg1{tc}_{d}')
            OP('v', lambda e, d=d, b=b: e.tensor_tensor(
                out=ygb[:, 0, :], in0=yev[:, :], in1=zbuf[b][:, d, :],
                op=ALU.mult),
               ['yev', f'zb{b}_{d}'], ['ygb_0'], tag=f'yg2{tc}_{d}')
            dma('q', yag_in[tc][d * 128:(d + 1) * 128, :], ygb[:, 0, :],
                ['ygb_0'], [f'ygst{tc}_{d}'], tag=f'styg{tc}_{d}')
        OP('g', lambda e, tc=tc: e.collective_compute(
            "AllGather", ALU.bypass, replica_groups=RG4,
            ins=[yag_in[tc].opt()], outs=[yag_all[tc, :, :].opt()]),
           [f'ygst{tc}_{hb2}' for hb2 in range(8)], [f'yago{tc}'], kind='cc',
           tag=f'AGy{tc}')

    qags = [('g', qg_i, qg_o, 0), ('u', qu_i, qu_o, 1), ('d', qd_i, qd_o, 2)]

    def emit_qag(k):
        nm, i_t, o_t, mi = qags[k]
        OP('g', lambda e, i_t=i_t, o_t=o_t: e.collective_compute(
            "AllGather", ALU.bypass, replica_groups=RG8,
            ins=[i_t.opt()], outs=[o_t.opt()]),
           [f'qst{mi}'], [f'qAG{nm}'], kind='cc', tag=f'qAG{nm}')

    front(0)
    front_b(0)
    for tc in range(NCH):
        if tc + 1 < NCH:
            front(tc + 1)
        scanpart(tc)
        if tc in (2, 4, 6):
            emit_qag(tc // 2 - 1)
        if tc + 1 < NCH:
            front_b(tc + 1)

    def finish(last_keys):
        OP('v', lambda e: e.memset(qrows[0:1, 0:1], 0.0), list(last_keys), [],
           tag='tailwait')
        with nc.Block() as block:
            p.emit(nc, block)
        for st in reversed(open_arenas):
            st.close()
        stack.close()
        return nc

    if upto == 'M':
        def ydump(e, half):
            rank = e.cc_rank(RG8)
            j = e.compute_val(rank % 4)
            ch = e.compute_val(j if half == 0 else 7 - j)
            return e.dma_start(out=dbg[:, half * TC:(half + 1) * TC],
                               in_=yag_all[dslice(ch, 1), :, :])
        OP('g', lambda e: ydump(e, 0), [f'yago{c}' for c in range(NCH)],
           ['dbg0'], kind='d', tag='ydump0')
        OP('g', lambda e: ydump(e, 1), [f'yago{c}' for c in range(NCH)],
           ['dbg1'], kind='d', tag='ydump1')
        a1.close()
        return finish(['dbg0', 'dbg1'])

    def dump_stream(stream):
        dma('q', dbg[:, :, :], stream[:, :, :],
            [f'stream_{kt}' for kt in range(16)], ['dbgd'], tag='dumpstream')
        return ['dbgd']

    def emit_ln(tag, rkey, rbuf, out_fn, lnps):
        ps_s, ps_m, rb_bf, rb_sq, rowbuf = lnps
        for kt in range(16):
            OP('s', lambda e, kt=kt: e.copy(out=rb_bf[:, :], in_=rbuf[:, kt, :]),
               [rkey], [f'{tag}_bf'], tag=f'{tag}_cp{kt}')
            OP('s', lambda e, kt=kt: e.activation(out=rb_sq[:, :], in_=rbuf[:, kt, :],
                                                  func=AF.Square),
               [rkey], [f'{tag}_sq'], tag=f'{tag}_sqr{kt}')
            OP('t', lambda e, kt=kt: e.matmul(ps_s[0:1, :], onescolb[:, :], rb_bf[:, :],
                                              start=(kt == 0), stop=(kt == 15)),
               [f'{tag}_bf', 'onescolb'], [f'{tag}_pss'], kind='c', accum=(kt > 0),
               tag=f'{tag}_mms{kt}')
            OP('t', lambda e, kt=kt: e.matmul(ps_m[0:1, :], onescolb[:, :], rb_sq[:, :],
                                              start=(kt == 0), stop=(kt == 15)),
               [f'{tag}_sq', 'onescolb'], [f'{tag}_psm'], kind='c', accum=(kt > 0),
               tag=f'{tag}_mmq{kt}')
        OP('v', lambda e: e.tensor_scalar(out=rowbuf[0:1, 0:512], in0=ps_s[0:1, :],
                                          scalar1=1.0 / D, scalar2=None, op0=ALU.mult),
           [f'{tag}_pss'], [f'{tag}_mrow'], tag=f'{tag}_mean')
        OP('v', lambda e: e.tensor_tensor(out=rowbuf[0:1, 512:1024],
                                          in0=rowbuf[0:1, 0:512],
                                          in1=rowbuf[0:1, 0:512], op=ALU.mult),
           [f'{tag}_mrow'], [f'{tag}_m2'], tag=f'{tag}_msq')
        OP('v', lambda e: e.tensor_scalar(out=rowbuf[0:1, 1024:1536], in0=ps_m[0:1, :],
                                          scalar1=1.0 / D, scalar2=None, op0=ALU.mult),
           [f'{tag}_psm'], [f'{tag}_vr1'], tag=f'{tag}_v1')
        OP('v', lambda e: e.tensor_tensor(out=rowbuf[0:1, 1024:1536],
                                          in0=rowbuf[0:1, 1024:1536],
                                          in1=rowbuf[0:1, 512:1024], op=ALU.subtract),
           [f'{tag}_vr1', f'{tag}_m2'], [f'{tag}_var'], accum=True, tag=f'{tag}_v2')
        OP('s', lambda e: e.activation(out=rowbuf[0:1, 1536:2048],
                                       in_=rowbuf[0:1, 1024:1536], func=AF.Sqrt,
                                       bias=epsc[0:1, 0:1]),
           [f'{tag}_var', 'epsc'], [f'{tag}_sd'], tag=f'{tag}_rsq')
        OP('v', lambda e: e.reciprocal(out=rowbuf[0:1, 512:1024],
                                       in_=rowbuf[0:1, 1536:2048]),
           [f'{tag}_sd', f'{tag}_m2'], [f'{tag}_rstd'], tag=f'{tag}_rcp')
        OP('t', lambda e: e.matmul(ps_s[:, :], onesrowF[:, :], rowbuf[0:1, 0:512],
                                   start=True, stop=True),
           [f'{tag}_mrow', 'onesrowF'], [f'{tag}_pss'], kind='c', tag=f'{tag}_bm')
        OP('t', lambda e: e.matmul(ps_m[:, :], onesrowF[:, :], rowbuf[0:1, 512:1024],
                                   start=True, stop=True),
           [f'{tag}_rstd', 'onesrowF'], [f'{tag}_psm'], kind='c', tag=f'{tag}_br')
        for kt in range(16):
            OP('v', lambda e, kt=kt: e.tensor_tensor(out=rbuf[:, kt, :],
                                                     in0=rbuf[:, kt, :],
                                                     in1=ps_s[:, :], op=ALU.subtract),
               [rkey, f'{tag}_pss'], [rkey], accum=True, tag=f'{tag}_sub{kt}')
            out_fn(kt)

    p.fence()
    a1.close()
    stream = sb('stream', [128, 16, 1024], BF16)
    aW, sbW, psW = mk_arena()
    ybuf = sbW('ybuf', [128, 32, 2 * TC], BF16)
    xres = sbW('xres', [128, 16, 2 * TC], BF16)
    wob = [sbW(f'wob{i}', [128, 32, 128], BF16) for i in range(2)]
    rbuf = sbW('rbuf', [128, 16, TC], F32)
    rb_bf = sbW('rb_bf', [128, TC], BF16)
    rb_sq = sbW('rb_sq', [128, TC], BF16)
    rowbuf = sbW('rowbuf', [1, 2048], F32)
    psA = psW('psA', [128, 512]); psB = psW('psB', [128, 512])
    psS = psW('psS', [128, 512]); psM = psW('psM', [128, 512])
    lnps = (psS, psM, rb_bf, rb_sq, rowbuf)

    def ypick(e, half):
        rank = e.cc_rank(RG8)
        j = e.compute_val(rank % 4)
        ch = e.compute_val(j if half == 0 else 7 - j)
        return e.dma_start(
            out=ybuf[:, :, half * TC:(half + 1) * TC],
            in_=yag_all[dslice(ch, 1), :, :]
            .rearrange("o (dt p) t -> p (o dt) t", p=128))
    OP('g', lambda e: ypick(e, 0), [f'yago{c}' for c in range(NCH)], ['ybuf'],
       kind='d', tag='ypick0')
    OP('g', lambda e: ypick(e, 1), [f'yago{c}' for c in range(NCH)], ['ybuf'],
       kind='d', accum=True, tag='ypick1')
    dma('g', xres[:, :, :], xown_d[:, :, :], ['xown_d'], ['xres'], tag='ldxres')

    for tc in range(2):
        for ot in range(16):
            wb = wob[ot % 2]
            dma('g', wb[:, :, :],
                wout_d[ot, :, :].rearrange("p (kt c) -> p kt c", c=128),
                ['wout_d'], [f'wob{ot % 2}'], tag=f'ldwo{tc}_{ot}')
            bank, pk = (psA, 'psA') if ot % 2 == 0 else (psB, 'psB')
            for kt in range(32):
                OP('t', lambda e, kt=kt, tc=tc, bank=bank, wb=wb, ot=ot: e.matmul(
                    bank[:, :], wb[:, kt, :], ybuf[:, kt, tc * TC:(tc + 1) * TC],
                    start=(kt == 0), stop=(kt == 31)),
                   [f'wob{ot % 2}', 'ybuf'], [pk], kind='c', accum=(kt > 0),
                   tag=f'womm{tc}_{ot}_{kt}')
            OP('v', lambda e, ot=ot, tc=tc, bank=bank, rbuf=rbuf: e.tensor_tensor(
                out=rbuf[:, ot, :], in0=bank[:, :],
                in1=xres[:, ot, tc * TC:(tc + 1) * TC], op=ALU.add),
               [pk, 'xres'], ['rW'], accum=(ot > 0), tag=f'woev{tc}_{ot}')

        def out_fn(kt, tc=tc, rbuf=rbuf, psM=psM):
            OP('v', lambda e, kt=kt, tc=tc, rbuf=rbuf, psM=psM: e.tensor_tensor(
                out=stream[:, kt, tc * TC:(tc + 1) * TC], in0=rbuf[:, kt, :],
                in1=psM[:, :], op=ALU.mult),
               ['rW', 'ln1_psm'], [f'stream_{kt}'], accum=(tc > 0),
               tag=f'ln1o{tc}_{kt}')
        emit_ln('ln1', 'rW', rbuf, out_fn, lnps)

    if upto == 'W':
        k = dump_stream(stream)
        return finish(k)

    p.fence()
    aW.close()
    aF, sbF, psF = mk_arena()
    g0s_sb = sbF('g0s_sb', [128, 44], F32)
    u0s_sb = sbF('u0s_sb', [128, 44], F32)
    d0s_sb = sbF('d0s_sb', [128, 16], F32)
    dma('q', g0s_sb[:, :], g0s_d[:, :], ['g0s_d'], ['g0s_sb'])
    dma('q', u0s_sb[:, :], u0s_d[:, :], ['u0s_d'], ['u0s_sb'])
    dma('q', d0s_sb[:, :], d0s_d[:, :], ['d0s_d'], ['d0s_sb'])
    gwb = [sbF(f'gwb{i}', [128, 16, 128], BF16) for i in range(2)]
    uwb = [sbF(f'uwb{i}', [128, 16, 128], BF16) for i in range(2)]
    dwb = [sbF(f'dwb{i}', [128, 44, 128], BF16) for i in range(2)]
    hbuf = sbF('hbuf', [128, 44, 2 * TC], BF16)
    gsil = sbF('gsil', [128, TC], BF16)
    rbuf = sbF('rbufF', [128, 16, TC], F32)
    rb_bf = sbF('rb_bfF', [128, TC], BF16)
    rb_sq = sbF('rb_sqF', [128, TC], BF16)
    rowbuf = sbF('rowbufF', [1, 2048], F32)
    psA = psF('psA_F', [128, 512]); psB = psF('psB_F', [128, 512])
    psC = psF('psC_F', [128, 512]); psD = psF('psD_F', [128, 512])
    psS = psF('psS_F', [128, 512]); psM = psF('psM_F', [128, 512])
    lnps = (psS, psM, rb_bf, rb_sq, rowbuf)

    for ot in range(44):
        gw, uw = gwb[ot % 2], uwb[ot % 2]
        dma('g', gw[:, :, :], g0_d[ot, :, :].rearrange("p (kt c) -> p kt c", c=128),
            ['g0_d'], [f'gwb{ot % 2}'], tag=f'ldg0_{ot}')
        dma('g', uw[:, :, :], u0_d[ot, :, :].rearrange("p (kt c) -> p kt c", c=128),
            ['u0_d'], [f'uwb{ot % 2}'], tag=f'ldu0_{ot}')
        for tc in range(2):
            bg, pg = (psA, 'psA_F') if ot % 2 == 0 else (psB, 'psB_F')
            bu, pu = (psC, 'psC_F') if ot % 2 == 0 else (psD, 'psD_F')
            for kt in range(16):
                OP('t', lambda e, kt=kt, tc=tc, bg=bg, gw=gw, ot=ot: e.matmul(
                    bg[:, :], gw[:, kt, :], stream[:, kt, tc * TC:(tc + 1) * TC],
                    start=(kt == 0), stop=(kt == 15)),
                   [f'gwb{ot % 2}', f'stream_{kt}'], [pg], kind='c', accum=(kt > 0),
                   tag=f'g0mm{ot}_{tc}_{kt}')
            for kt in range(16):
                OP('t', lambda e, kt=kt, tc=tc, bu=bu, uw=uw, ot=ot: e.matmul(
                    bu[:, :], uw[:, kt, :], stream[:, kt, tc * TC:(tc + 1) * TC],
                    start=(kt == 0), stop=(kt == 15)),
                   [f'uwb{ot % 2}', f'stream_{kt}'], [pu], kind='c', accum=(kt > 0),
                   tag=f'u0mm{ot}_{tc}_{kt}')
            OP('s', lambda e, ot=ot, bg=bg: e.activation(
                out=gsil[:, :], in_=bg[:, :], func=AF.Silu,
                scale=g0s_sb[:, ot:ot + 1]),
               [pg, 'g0s_sb'], ['gsil'], tag=f'g0sil{ot}_{tc}')
            OP('v', lambda e, ot=ot, tc=tc, bu=bu: e.scalar_tensor_tensor(
                out=hbuf[:, ot, tc * TC:(tc + 1) * TC], in0=bu[:, :],
                scalar=u0s_sb[:, ot:ot + 1], in1=gsil[:, :],
                op0=ALU.mult, op1=ALU.mult),
               [pu, 'gsil', 'u0s_sb'], [f'hb0_{ot}'], accum=(tc > 0),
               tag=f'h0{ot}_{tc}')
    for tc in range(2):
        for ot in range(16):
            dw = dwb[ot % 2]
            dma('g', dw[:, :, :], d0_d[ot, :, :].rearrange("p (kt c) -> p kt c", c=128),
                ['d0_d'], [f'dwb{ot % 2}'], tag=f'ldd0_{tc}_{ot}')
            bank, pk = (psA, 'psA_F') if ot % 2 == 0 else (psB, 'psB_F')
            for kt in range(44):
                OP('t', lambda e, kt=kt, tc=tc, bank=bank, dw=dw, ot=ot: e.matmul(
                    bank[:, :], dw[:, kt, :], hbuf[:, kt, tc * TC:(tc + 1) * TC],
                    start=(kt == 0), stop=(kt == 43)),
                   [f'dwb{ot % 2}', f'hb0_{kt}'], [pk], kind='c', accum=(kt > 0),
                   tag=f'd0mm{ot}_{tc}_{kt}')
            OP('v', lambda e, ot=ot, tc=tc, bank=bank, rbuf=rbuf: e.scalar_tensor_tensor(
                out=rbuf[:, ot, :], in0=bank[:, :], scalar=d0s_sb[:, ot:ot + 1],
                in1=stream[:, ot, tc * TC:(tc + 1) * TC],
                op0=ALU.mult, op1=ALU.add),
               [pk, f'stream_{ot}', 'd0s_sb'], ['rF'], accum=(ot > 0),
               tag=f'd0ev{ot}_{tc}')

        def out_fn(kt, tc=tc, rbuf=rbuf, psM=psM):
            OP('v', lambda e, kt=kt, tc=tc, rbuf=rbuf, psM=psM: e.tensor_tensor(
                out=stream[:, kt, tc * TC:(tc + 1) * TC], in0=rbuf[:, kt, :],
                in1=psM[:, :], op=ALU.mult),
               ['rF', 'ln2_psm'], [f'stream_{kt}'], tag=f'ln2o{tc}_{kt}')
        emit_ln('ln2', 'rF', rbuf, out_fn, lnps)

    if upto == 'F0':
        k = dump_stream(stream)
        return finish(k)

    p.fence()
    aF.close()
    aA, sbA, psAo = mk_arena()
    ctq = [sbA(f'ctq{i}', [128, 16, TC], BF16) for i in range(2)]
    dma('q', sag_i.rearrange("(kt p) t -> p kt t", p=128), stream[:, :, :],
        [f'stream_{kt}' for kt in range(16)], ['sag_i_d'], tag='stsag')
    OP('g', lambda e: e.collective_compute(
        "AllGather", ALU.bypass, replica_groups=RG4,
        ins=[sag_i.opt()], outs=[sag_o.opt()]),
       ['sag_i_d'], ['sAG'], kind='cc', tag='sAG')

    aA1, sbA1, psA1 = mk_arena()
    rc_sb = sbA1('rc_sb', [128, T], BF16)
    rs_sb = sbA1('rs_sb', [128, T], BF16)
    permb = sbA1('permb', [128, 128], BF16)
    trib = sbA1('trib', [128, 4, TC], BF16)
    wqh_sb = sbA1('wqh_sb', [128, 4, 16, 128], BF16)
    wkh_sb = sbA1('wkh_sb', [128, 16, 128], BF16)
    wvh_sb = sbA1('wvh_sb', [128, 16, 128], BF16)
    dma('g', rc_sb[:, :], rc_d[:, :], ['rc_d'], ['rc_sb'])
    dma('g', rs_sb[:, :], rs_d[:, :], ['rs_d'], ['rs_sb'])
    dma('g', permb[:, :], perm_d[:, :], ['perm_d'], ['permb'])
    dma('g', trib[:, :, :], tri_d.rearrange("kt p t -> p kt t"), ['tri_d'], ['trib'])
    for t4 in range(4):
        dma('g', wqh_sb[:, t4, :, :],
            wqh_d[t4, :, :].rearrange("p (kt c) -> p kt c", c=128),
            ['wqh_d'], ['wqh_sb'], tag=f'ldwqh{t4}', accum=(t4 > 0))
    dma('g', wkh_sb[:, :, :], wkh_d.rearrange("p (kt c) -> p kt c", c=128),
        ['wkh_d'], ['wkh_sb'])
    dma('g', wvh_sb[:, :, :], wvh_d[:, :, :], ['wvh_d'], ['wvh_sb'])
    aqb = sbA1('aqb_a', [128, 4, T], BF16)
    kbuf = sbA1('kbuf_a', [128, T], BF16)
    vtm = sbA1('vtm_a', [128, 32, 130], BF16)
    axa = [sbA1(f'xa{i}', [128, 16, TC], BF16) for i in range(2)]
    qtmp = sbA1('qtmp', [128, TC], BF16)
    t1b = sbA1('t1b', [128, TC], BF16)
    ptb = [sbA1(f'ptb{i}', [128, TC], BF16) for i in range(3)]
    ctmp = sbA1('ctmp', [64, TC], F32)
    rcb = sbA1('rcb', [64, TC], F32)
    rrow = sbA1('rrow', [1, TC], F32)
    psQ1 = psA1('psQ1', [128, 512]); psQ2 = psA1('psQ2', [128, 512])
    psP = psA1('psP', [128, 512])
    psSc = [psA1('psSc0', [128, 512]), psA1('psSc1', [128, 512])]
    psCx = [psA1('psCx0', [128, 512]), psA1('psCx1', [128, 512])]
    psB2 = psA1('psB2', [128, 512])
    OP('v', lambda e: e.memset(vtm[:, :, 64:65], 1.0), [], ['vtm'], accum=True,
       tag='vone0')
    OP('v', lambda e: e.memset(vtm[:, :, 129:130], 1.0), [], ['vtm'], accum=True,
       tag='vone1')

    def rope_tile(nm, qc, bank, pk, dst, dkey):
        OP('s', lambda e, bank=bank: e.copy(out=qtmp[:, :], in_=bank[:, :]),
           [pk], ['qtmp'], tag=f'{nm}cp{qc}')
        OP('t', lambda e: e.matmul(psP[:, :], permb[:, :], qtmp[:, :],
                                   start=True, stop=True),
           ['qtmp', 'permb'], ['psP'], kind='c', tag=f'{nm}pm{qc}')
        OP('v', lambda e, qc=qc: e.tensor_tensor(
            out=t1b[:, :], in0=qtmp[:, :], in1=rc_sb[:, qc * TC:(qc + 1) * TC],
            op=ALU.mult),
           ['qtmp', 'rc_sb'], ['t1b'], tag=f'{nm}t1{qc}')
        OP('v', lambda e, qc=qc: e.tensor_tensor(
            out=qtmp[:, :], in0=psP[:, :], in1=rs_sb[:, qc * TC:(qc + 1) * TC],
            op=ALU.mult),
           ['psP', 'rs_sb', 'qtmp'], ['qtmp'], accum=True, tag=f'{nm}t2{qc}')
        OP('v', lambda e, dst=dst: e.tensor_tensor(
            out=dst, in0=t1b[:, :], in1=qtmp[:, :], op=ALU.add),
           ['t1b', 'qtmp'], [dkey], accum=True, tag=f'{nm}t3{qc}')

    for qc in range(NCH):
        b = qc % 2
        rank, halfk = chunk_pos(qc)
        dma('q', axa[b][:, :, :],
            sag_o[rank * D:(rank + 1) * D, halfk * TC:(halfk + 1) * TC]
            .rearrange("(kt p) t -> p kt t", p=128),
            ['sAG'], [f'xa{b}'], tag=f'ldxa{qc}')
        for kt in range(16):
            OP('t', lambda e, kt=kt, b=b: e.matmul(
                psQ1[:, :], wkh_sb[:, kt, :], axa[b][:, kt, :],
                start=(kt == 0), stop=(kt == 15)),
               ['wkh_sb', f'xa{b}'], ['psQ1'], kind='c', accum=(kt > 0),
               tag=f'kmm{qc}_{kt}')
        rope_tile('k', qc, psQ1, 'psQ1', kbuf[:, qc * TC:(qc + 1) * TC], 'kbuf')
        for t4 in range(4):
            bank, pk = (psQ1, 'psQ1') if t4 % 2 else (psQ2, 'psQ2')
            for kt in range(16):
                OP('t', lambda e, kt=kt, t4=t4, b=b, bank=bank: e.matmul(
                    bank[:, :], wqh_sb[:, t4, kt, :], axa[b][:, kt, :],
                    start=(kt == 0), stop=(kt == 15)),
                   ['wqh_sb', f'xa{b}'], [pk], kind='c', accum=(kt > 0),
                   tag=f'qmm{qc}_{t4}_{kt}')
            rope_tile(f'q{t4}', qc, bank, pk,
                      aqb[:, t4, qc * TC:(qc + 1) * TC], 'aqb')
        for tb in range(4):
            bank, pk = (psQ1, 'psQ1') if tb % 2 else (psQ2, 'psQ2')
            for kt in range(16):
                OP('t', lambda e, kt=kt, tb=tb, b=b, bank=bank: e.matmul(
                    bank[:, 0:128], axa[b][:, kt, tb * 128:(tb + 1) * 128],
                    wvh_sb[:, kt, :], start=(kt == 0), stop=(kt == 15)),
                   ['wvh_sb', f'xa{b}'], [pk], kind='c', accum=(kt > 0),
                   tag=f'vmm{qc}_{tb}_{kt}')
            OP('s', lambda e, qc=qc, tb=tb, bank=bank: e.copy(
                out=vtm[:, qc * 4 + tb, 0:64], in_=bank[:, 0:64]),
               [pk, 'vtm'], ['vtm'], accum=True, tag=f'vev{qc}_{tb}a')
            OP('s', lambda e, qc=qc, tb=tb, bank=bank: e.copy(
                out=vtm[:, qc * 4 + tb, 65:129], in_=bank[:, 64:128]),
               [pk, 'vtm'], ['vtm'], accum=True, tag=f'vev{qc}_{tb}b')
        par = 1 if qc >= 4 else 0
        kt16b = (qc * 4) % 16
        for h8 in range(8):
            qbase = 0 if h8 < 4 else 64
            qtile = h8 % 4
            vb = 0 if h8 < 4 else 65
            cbank, ck = psCx[h8 % 2], f'psCx{h8 % 2}'
            NK = 4 * (qc + 1)

            def emit_smm(nk):
                kc, kt = divmod(nk, 4)
                i = nk % 3
                sbank, sk = psSc[nk % 2], f'psSc{nk % 2}'
                OP('t', lambda e, qbase=qbase, qtile=qtile, kc=kc, kt=kt,
                   qc=qc, sbank=sbank: e.matmul(
                    sbank[:, :],
                    kbuf[qbase:qbase + 64, kc * TC + kt * 128:kc * TC + (kt + 1) * 128],
                    aqb[qbase:qbase + 64, qtile, qc * TC:(qc + 1) * TC],
                    start=True, stop=True),
                   ['kbuf', 'aqb'], [sk], kind='c', tag=f'smm{qc}_{h8}_{kc}_{kt}')
                OP('s', lambda e, i=i, sbank=sbank: e.activation(
                    out=ptb[i][:, :], in_=sbank[:, :], func=AF.Exp, scale=0.125),
                   [sk], [f'ptb{i}'], tag=f'exp{qc}_{h8}_{kc}_{kt}')
                if kc == qc:
                    OP('g', lambda e, i=i, kt=kt: e.tensor_tensor(
                        out=ptb[i][:, :], in0=ptb[i][:, :],
                        in1=trib[:, kt, :], op=ALU.mult),
                       [f'ptb{i}', 'trib'], [f'ptb{i}'], accum=True,
                       tag=f'msk{qc}_{h8}_{kt}')

            def emit_av(nk):
                kc, kt = divmod(nk, 4)
                i = nk % 3
                OP('t', lambda e, i=i, kc=kc, kt=kt, vb=vb, cbank=cbank,
                   first=(nk == 0), last=(nk == NK - 1): e.matmul(
                    cbank[0:65, :], vtm[:, kc * 4 + kt, vb:vb + 65],
                    ptb[i][:, :], start=first, stop=last),
                   [f'ptb{i}', 'vtm'], [ck], kind='c', accum=(nk > 0),
                   tag=f'av{qc}_{h8}_{kc}_{kt}')

            emit_smm(0)
            for nk in range(NK):
                if nk + 1 < NK:
                    emit_smm(nk + 1)
                emit_av(nk)
            OP('v', lambda e, cbank=cbank: e.reciprocal(
                out=rrow[0:1, :], in_=cbank[64:65, :]),
               [ck], ['rrow'], tag=f'rec{qc}_{h8}')
            OP('t', lambda e: e.matmul(psB2[0:64, :], onesrowF[:, 0:64],
                                       rrow[0:1, :], start=True, stop=True),
               ['rrow', 'onesrowF'], ['psB2'], kind='c', tag=f'bcr{qc}_{h8}')
            OP('s', lambda e: e.copy(out=rcb[:, :], in_=psB2[0:64, :]),
               ['psB2'], ['rcb'], tag=f'rcb{qc}_{h8}')
            OP('v', lambda e, cbank=cbank: e.tensor_tensor(
                out=ctmp[:, :], in0=cbank[0:64, :], in1=rcb[:, :],
                op=ALU.mult),
               [ck, 'rcb'], ['ctmp'], tag=f'cev{qc}_{h8}')
            hg = 0 if h8 < 4 else 1
            hloc = h8 % 4
            for kt in range(4):
                OP('t', lambda e, kt=kt: e.transpose(
                    out=psP[0:128, 0:64], in_=ctmp[:, kt * 128:(kt + 1) * 128],
                    identity=identf[0:64, 0:64]),
                   ['ctmp', 'identb', 'psP'], ['psP'], kind='c',
                   tag=f'ctp{qc}_{h8}_{kt}')
                OP('v', lambda e, par=par, kt16b=kt16b, kt=kt, hg=hg,
                   hloc=hloc: e.tensor_copy(
                    out=ctq[par][:, kt16b + kt, hg * 256 + hloc * 64:
                                 hg * 256 + (hloc + 1) * 64],
                    in_=psP[0:128, 0:64]),
                   ['psP'], [f'ctq{par}'], accum=True, tag=f'ctc{qc}_{h8}_{kt}')

    p.fence()
    aA1.close()
    aA3, sbA3, psA3 = mk_arena()
    wob2 = [sbA3(f'wob2{i}', [128, 16, 128], BF16) for i in range(2)]
    rbuf = sbA3('rbufA', [128, 16, TC], F32)
    rb_bf = sbA3('rb_bfA', [128, TC], BF16)
    rb_sq = sbA3('rb_sqA', [128, TC], BF16)
    rowbuf = sbA3('rowbufA', [1, 2048], F32)
    psWo = psA3('psWo', [128, 512]); psWo2 = psA3('psWo2', [128, 512])
    psS = psA3('psS_A', [128, 512]); psM = psA3('psM_A', [128, 512])
    lnps = (psS, psM, rb_bf, rb_sq, rowbuf)
    for tc in range(2):
        for ot in range(16):
            wb = wob2[ot % 2]
            dma('g', wb[:, :, :], wo_d[ot, :, :].rearrange("p (kt c) -> p kt c", c=128),
                ['wo_d'], [f'wob2{ot % 2}'], tag=f'ldwo2_{tc}_{ot}')
            bank, pk = (psWo, 'psWo') if ot % 2 == 0 else (psWo2, 'psWo2')
            for par in range(2):
                for kt in range(16):
                    OP('t', lambda e, kt=kt, tc=tc, par=par, bank=bank, wb=wb:
                       e.matmul(
                        bank[:, par * 256:(par + 1) * 256], wb[:, kt, :],
                        ctq[par][:, kt, tc * 256:(tc + 1) * 256],
                        start=(kt == 0), stop=(kt == 15)),
                       [f'wob2{ot % 2}', f'ctq{par}'], [pk], kind='c',
                       accum=(par > 0 or kt > 0), tag=f'womm2{tc}_{ot}_{par}_{kt}')
            for par in range(2):
                OP('v', lambda e, ot=ot, tc=tc, par=par, bank=bank, rbuf=rbuf:
                   e.tensor_tensor(
                    out=rbuf[:, ot, par * 256:(par + 1) * 256],
                    in0=bank[:, par * 256:(par + 1) * 256],
                    in1=stream[:, ot, tc * TC + par:(tc + 1) * TC:2],
                    op=ALU.add),
                   [pk, f'stream_{ot}'], ['rA'], accum=(ot > 0 or par > 0),
                   tag=f'woev2{tc}_{ot}_{par}')

        def out_fn(kt, tc=tc, rbuf=rbuf, psM=psM):
            for par in range(2):
                OP('v', lambda e, kt=kt, tc=tc, par=par, rbuf=rbuf, psM=psM:
                   e.tensor_tensor(
                    out=stream[:, kt, tc * TC + par:(tc + 1) * TC:2],
                    in0=rbuf[:, kt, par * 256:(par + 1) * 256],
                    in1=psM[:, par * 256:(par + 1) * 256], op=ALU.mult),
                   ['rA', 'ln3_psm'], [f'stream_{kt}'], accum=(par > 0),
                   tag=f'ln3o{tc}_{kt}_{par}')
        emit_ln('ln3', 'rA', rbuf, out_fn, lnps)

    if upto == 'A':
        k = dump_stream(stream)
        return finish(k)

    p.fence()
    aA3.close()
    aA.close()
    aB, sbB, psB_ = mk_arena()
    lnw_sb = sbB('lnw_sb', [128, 64], F32)
    scol = sbB('scol', [128, 4], F32)
    dma('q', lnw_sb[:, :], lnw_d[:, :], ['lnw_d'], ['lnw_sb'])
    OP('v', lambda e: e.tensor_scalar(out=scol[:, :], in0=mcol[:, :],
                                      scalar1=1.0 / 127.0, scalar2=None,
                                      op0=ALU.mult),
       ['mcol'], ['scol'], tag='scol')
    qxb = sbB('qxb', [128, 16, TC], BF16)
    qgb = [sbB(f'qgb{i}', [128, 2048], BF16) for i in range(2)]
    qub = [sbB(f'qub{i}', [128, 2048], BF16) for i in range(2)]
    qdb = [sbB(f'qdb{i}', [128, 2048], BF16) for i in range(2)]
    hbuf1 = sbB('hbuf1', [128, 64, TC], BF16)
    guq = sbB('guq', [128, TC], F32)
    gsl = sbB('gsl', [128, TC], BF16)
    xa = [sbB(f'xaB{i}', [128, TC], BF16) for i in range(2)]
    xmax = sbB('xmax', [128, TC], F32)
    mxt = sbB('mxt', [128, 4], F32)
    mx4 = sbB('mx4', [4, 128], F32)
    sx4 = sbB('sx4', [4, 128], F32)
    sel4_sb = sbB('sel4_sb', [4, 4 * 128], F32)
    vrt = sbB('vrt', [128, 4], F32)
    rstd4 = sbB('rstd4', [128, 4], F32)
    ct4 = sbB('ct4', [128, 4], F32)
    epsB = sbB('epsB', [128, 1], F32)
    OP('v', lambda e: e.memset(epsB[:, :], RMS_EPS), [], ['epsB'])
    dma('q', sel4_sb[:, :], sel4_d[:, :], ['sel4_d'], ['sel4_sb'])
    hsqb = sbB('hsqb', [128, TC], BF16)
    bc1 = sbB('bc1', [128, TC], F32)
    bc2 = sbB('bc2', [128, TC], F32)
    qt = sbB('qt', [128, TC], F32)
    rowq = sbB('rowq', [1, TC], F32)
    rbuf = sbB('rbufB', [128, 16, TC], F32)
    rb_bf = sbB('rb_bfB', [128, TC], BF16)
    rb_sq = sbB('rb_sqB', [128, TC], BF16)
    rowbuf = sbB('rowbufB', [1, 2048], F32)
    obuf = sbB('obuf', [128, 4, 128], F32)
    psG = [psB_('psG0', [128, 512]), psB_('psG1', [128, 512])]
    psU = [psB_('psU0', [128, 512]), psB_('psU1', [128, 512])]
    psR = psB_('psR', [128, 512])
    psX = psB_('psX', [128, 512])
    psS = psB_('psS_B', [128, 512]); psM = psB_('psM_B', [128, 512])
    lnps = (psS, psM, rb_bf, rb_sq, rowbuf)

    def cross_max_bcasts(tag, rec_dst, rec_key, mx_dst, mx_key, rec_scale_tag):
        for tb in range(4):
            OP('t', lambda e, tb=tb: e.transpose(
                out=psX[:, tb * 128:(tb + 1) * 128],
                in_=xmax[:, tb * 128:(tb + 1) * 128],
                identity=identf[:, :]),
               ['xmax', 'identf', 'psX'], ['psX'], kind='c', accum=(tb > 0),
               tag=f'{tag}tp{tb}')
        for tb in range(4):
            OP('v', lambda e, tb=tb: e.reduce_max(
                out=mxt[:, tb:tb + 1], in_=psX[:, tb * 128:(tb + 1) * 128], axis=AXF),
               ['psX'], ['mxt'], accum=(tb > 0), tag=f'{tag}rm{tb}')
        OP('v', lambda e: e.tensor_scalar(out=mxt[:, :], in0=mxt[:, :],
                                          scalar1=EPS_Q, scalar2=None, op0=ALU.max),
           ['mxt'], ['mxt'], accum=True, tag=f'{tag}clip')
        OP('t', lambda e: e.transpose(out=psX[0:4, 0:128], in_=mxt[:, :],
                                      identity=identf[:, :]),
           ['mxt', 'identf', 'psX'], ['psX'], tag=f'{tag}tpm')
        OP('v', lambda e: e.tensor_copy(out=mx4[:, :], in_=psX[0:4, 0:128]),
           ['psX'], ['mx4'], tag=f'{tag}mx4')
        OP('v', lambda e: e.reciprocal(out=sx4[:, :], in_=mx4[:, :]),
           ['mx4'], ['sx4'], tag=f'{tag}rcp')
        for tb in range(4):
            OP('t', lambda e, tb=tb: e.matmul(
                psX[:, tb * 128:(tb + 1) * 128],
                sel4_sb[:, tb * 128:(tb + 1) * 128], sx4[:, :],
                start=True, stop=True),
               ['sx4', 'sel4_sb', 'psX'], ['psX'], kind='c', accum=(tb > 0),
               tag=f'{tag}bcr{tb}')
        OP('v', lambda e: e.tensor_scalar(out=rec_dst[:, :], in0=psX[:, :],
                                          scalar1=127.0, scalar2=None, op0=ALU.mult),
           ['psX'], [rec_key], tag=f'{tag}recb')
        for tb in range(4):
            OP('t', lambda e, tb=tb: e.matmul(
                psX[:, tb * 128:(tb + 1) * 128],
                sel4_sb[:, tb * 128:(tb + 1) * 128], mx4[:, :],
                start=True, stop=True),
               ['mx4', 'sel4_sb', 'psX'], ['psX'], kind='c', accum=(tb > 0),
               tag=f'{tag}bcm{tb}')
        OP('v', lambda e: e.tensor_copy(out=mx_dst[:, :], in_=psX[:, :]),
           ['psX'], [mx_key], tag=f'{tag}mxb')

    RA = slice(0, TC)

    outks = []
    for tc in range(2):
        for kt in range(16):
            OP('s', lambda e, kt=kt, tc=tc: e.activation(
                out=xa[kt % 2][:, :], in_=stream[:, kt, tc * TC:(tc + 1) * TC],
                func=AF.Abs),
               [f'stream_{kt}'], [f'xa{kt % 2}'], tag=f'xabs{tc}_{kt}')
            if kt == 0:
                OP('v', lambda e: e.tensor_copy(out=xmax[:, :], in_=xa[0][:, :]),
                   ['xa0'], ['xmax'], tag=f'xm0_{tc}')
            else:
                OP('v', lambda e, kt=kt: e.tensor_tensor(
                    out=xmax[:, :], in0=xmax[:, :], in1=xa[kt % 2][:, :],
                    op=ALU.max),
                   [f'xa{kt % 2}', 'xmax'], ['xmax'], accum=True, tag=f'xm{tc}_{kt}')
        cross_max_bcasts(f'xq{tc}_', bc1, 'bc1', bc2, 'bc2', tc)
        for kt in range(16):
            OP('v', lambda e, kt=kt, tc=tc: e.tensor_tensor(
                out=qt[:, :], in0=stream[:, kt, tc * TC:(tc + 1) * TC],
                in1=bc1[:, :], op=ALU.mult),
               [f'stream_{kt}', 'bc1'], ['qt'], tag=f'qxm{tc}_{kt}')
            OP('v', lambda e, kt=kt: e.tensor_scalar(
                out=qxb[:, kt, :], in0=qt[:, :], scalar1=MAGIC, scalar2=MAGIC,
                op0=ALU.add, op1=ALU.subtract),
               ['qt'], ['qxb'], accum=(kt > 0), tag=f'qxr{tc}_{kt}')
        if upto == 'F1d' and tc == 0:
            dma('q', dbg[:, :, :], qxb[:, :, :], ['qxb'], ['dbgd1'], tag='dqx')
            OP('v', lambda e: e.tensor_tensor(
                out=bc2[:, :], in0=bc2[:, :], in1=bc2[:, :], op=ALU.bypass)
               if False else e.tensor_copy(out=qt[:, :], in_=bc1[:, :]),
               ['bc1'], ['qt'], tag='dbc1c')
            dma('q', dbg4[:, 0:512], qt[:, :], ['qt'], ['dbgd4a'], tag='dbc1')
            OP('v', lambda e: e.tensor_copy(out=qt[:, :], in_=bc2[:, :]),
               ['bc2', 'qt'], ['qt'], tag='dbc2c')
            dma('q', dbg4[:, 512:1024], qt[:, :], ['qt'], ['dbgd4b'], tag='dbc2')
            dma('q', dbg4[:, 1024:1028], swcol[:, :], ['swcol'], ['dbgd4c'])
            dma('q', dbg4[:, 1028:1032], mcol[:, :], ['mcol'], ['dbgd4d'])
            dma('q', dbg4[:, 1032:1036], scol[:, :], ['scol'], ['dbgd4e'])
            dma('q', dbg5[:, :], qg_o[0:128, 0:2048], ['qAGg'], ['dbgd5'])
        for ot in range(64):
            rnk, ot8 = ot // 8, ot % 8
            gw, uw = qgb[ot % 2], qub[ot % 2]
            dma('q', gw[:, :], qg_o[rnk * 128:(rnk + 1) * 128,
                                    ot8 * 2048:(ot8 + 1) * 2048],
                ['qAGg'], [f'qgb{ot % 2}'], tag=f'ldqg{tc}_{ot}')
            dma('q', uw[:, :], qu_o[rnk * 128:(rnk + 1) * 128,
                                    ot8 * 2048:(ot8 + 1) * 2048],
                ['qAGu'], [f'qub{ot % 2}'], tag=f'ldqu{tc}_{ot}')
            bg, pg = psG[ot % 2], f'psG{ot % 2}'
            bu, pu = psU[ot % 2], f'psU{ot % 2}'
            for kt in range(16):
                OP('t', lambda e, kt=kt, bg=bg, gw=gw, ot=ot: e.matmul(
                    bg[:, :], gw[:, kt * 128:(kt + 1) * 128], qxb[:, kt, :],
                    start=(kt == 0), stop=(kt == 15)),
                   [f'qgb{ot % 2}', 'qxb'], [pg], kind='c', accum=(kt > 0),
                   tag=f'gmm{tc}_{ot}_{kt}')
            for kt in range(16):
                OP('t', lambda e, kt=kt, bu=bu, uw=uw, ot=ot: e.matmul(
                    bu[:, :], uw[:, kt * 128:(kt + 1) * 128], qxb[:, kt, :],
                    start=(kt == 0), stop=(kt == 15)),
                   [f'qub{ot % 2}', 'qxb'], [pu], kind='c', accum=(kt > 0),
                   tag=f'umm{tc}_{ot}_{kt}')
            OP('v', lambda e, bg=bg: e.scalar_tensor_tensor(
                out=guq[:, :], in0=bg[:, :], scalar=scol[:, 0:1], in1=bc2[:, :],
                op0=ALU.mult, op1=ALU.mult),
               [pg, 'scol', 'bc2'], ['guq'], tag=f'gev{tc}_{ot}')
            OP('s', lambda e: e.activation(out=gsl[:, :], in_=guq[:, :],
                                           func=AF.Silu),
               ['guq'], ['gsl'], tag=f'gsil{tc}_{ot}')
            OP('v', lambda e, bu=bu: e.scalar_tensor_tensor(
                out=guq[:, :], in0=bu[:, :], scalar=scol[:, 1:2], in1=bc2[:, :],
                op0=ALU.mult, op1=ALU.mult),
               [pu, 'scol', 'bc2', 'gsl'], ['guq'], tag=f'uev{tc}_{ot}')
            OP('v', lambda e, ot=ot: e.tensor_tensor(
                out=hbuf1[:, ot, :], in0=gsl[:, :], in1=guq[:, :], op=ALU.mult),
               ['gsl', 'guq'], [f'hb1_{ot}'], tag=f'hmul{tc}_{ot}')
        for ot in range(64):
            OP('s', lambda e, ot=ot: e.activation(
                out=xa[ot % 2][:, :], in_=hbuf1[:, ot, :], func=AF.Abs,
                scale=lnw_sb[:, ot:ot + 1]),
               [f'hb1_{ot}', 'lnw_sb'], [f'xa{ot % 2}'], tag=f'habs{tc}_{ot}')
            if ot == 0:
                OP('v', lambda e: e.tensor_copy(out=xmax[:, :], in_=xa[0][:, :]),
                   ['xa0'], ['xmax'], tag=f'hm0_{tc}')
            else:
                OP('v', lambda e, ot=ot: e.tensor_tensor(
                    out=xmax[:, :], in0=xmax[:, :], in1=xa[ot % 2][:, :],
                    op=ALU.max),
                   [f'xa{ot % 2}', 'xmax'], ['xmax'], accum=True, tag=f'hm{tc}_{ot}')
            OP('s', lambda e, ot=ot: e.activation(
                out=hsqb[:, :], in_=hbuf1[:, ot, :], func=AF.Square),
               [f'hb1_{ot}'], ['hsq'], tag=f'hsqr{tc}_{ot}')
            OP('t', lambda e, ot=ot: e.matmul(
                psR[0:1, :], onescolb[:, :], hsqb[:, :],
                start=(ot == 0), stop=(ot == 63)),
               ['hsq', 'onescolb'], ['psR'], kind='c', accum=(ot > 0),
               tag=f'hsmm{tc}_{ot}')
        for tb in range(4):
            OP('t', lambda e, tb=tb: e.transpose(
                out=psX[:, tb * 128:(tb + 1) * 128],
                in_=xmax[:, tb * 128:(tb + 1) * 128],
                identity=identf[:, :]),
               ['xmax', 'identf', 'psX'], ['psX'], kind='c', accum=(tb > 0),
               tag=f'htp{tc}_{tb}')
        for tb in range(4):
            OP('v', lambda e, tb=tb: e.reduce_max(
                out=mxt[:, tb:tb + 1], in_=psX[:, tb * 128:(tb + 1) * 128], axis=AXF),
               ['psX'], ['mxt'], accum=(tb > 0), tag=f'hrm{tc}_{tb}')
        OP('v', lambda e: e.tensor_scalar(out=rowq[0:1, RA], in0=psR[0:1, :],
                                          scalar1=1.0 / I1, scalar2=None,
                                          op0=ALU.mult),
           ['psR'], ['vrow'], tag=f'vrow{tc}')
        for tb in range(4):
            OP('t', lambda e, tb=tb: e.transpose(
                out=psR[:, tb:tb + 1],
                in_=rowq[0:1, tb * 128:(tb + 1) * 128],
                identity=identf[0:1, 0:1]),
               ['vrow', 'identf', 'psR'], ['psR'], kind='c', accum=True,
               tag=f'vtp{tc}_{tb}')
        OP('v', lambda e: e.tensor_copy(out=vrt[:, :], in_=psR[:, 0:4]),
           ['psR'], ['vrt'], tag=f'vrt{tc}')
        OP('s', lambda e: e.activation(out=vrt[:, :], in_=vrt[:, :],
                                       func=AF.Sqrt, bias=epsB[:, 0:1]),
           ['vrt', 'epsB'], ['vrt'], accum=True, tag=f'hsd{tc}')
        OP('v', lambda e: e.reciprocal(out=rstd4[:, :], in_=vrt[:, :]),
           ['vrt'], ['rstd4'], tag=f'hrstd{tc}')
        OP('v', lambda e: e.tensor_tensor(out=mxt[:, :], in0=mxt[:, :],
                                          in1=rstd4[:, :], op=ALU.mult),
           ['mxt', 'rstd4'], ['mxt'], accum=True, tag=f'hmh1_{tc}')
        OP('v', lambda e: e.tensor_scalar(out=mxt[:, :], in0=mxt[:, :],
                                          scalar1=EPS_Q, scalar2=None, op0=ALU.max),
           ['mxt'], ['mxt'], accum=True, tag=f'hmh2_{tc}')
        OP('v', lambda e: e.reciprocal(out=ct4[:, :], in_=mxt[:, :]),
           ['mxt'], ['ct4'], tag=f'hsh{tc}')
        OP('v', lambda e: e.tensor_tensor(out=ct4[:, :], in0=ct4[:, :],
                                          in1=rstd4[:, :], op=ALU.mult),
           ['ct4', 'rstd4'], ['ct4'], accum=True, tag=f'hct{tc}')
        OP('t', lambda e: e.transpose(out=psX[0:4, 0:128], in_=ct4[:, :],
                                      identity=identf[:, :]),
           ['ct4', 'identf', 'psX'], ['psX'], tag=f'hctp{tc}')
        OP('v', lambda e: e.tensor_copy(out=sx4[:, :], in_=psX[0:4, 0:128]),
           ['psX'], ['sx4'], tag=f'hc4_{tc}')
        OP('t', lambda e: e.transpose(out=psX[0:4, 0:128], in_=mxt[:, :],
                                      identity=identf[:, :]),
           ['mxt', 'identf', 'psX'], ['psX'], tag=f'hmtp{tc}')
        OP('v', lambda e: e.tensor_copy(out=mx4[:, :], in_=psX[0:4, 0:128]),
           ['psX'], ['mx4'], tag=f'hm4_{tc}')
        for tb in range(4):
            OP('t', lambda e, tb=tb: e.matmul(
                psX[:, tb * 128:(tb + 1) * 128],
                sel4_sb[:, tb * 128:(tb + 1) * 128], sx4[:, :],
                start=True, stop=True),
               ['sx4', 'sel4_sb', 'psX'], ['psX'], kind='c', accum=(tb > 0),
               tag=f'hbcr{tc}_{tb}')
        OP('v', lambda e: e.tensor_scalar(out=bc1[:, :], in0=psX[:, :],
                                          scalar1=127.0, scalar2=None, op0=ALU.mult),
           ['psX'], ['bc1'], tag=f'hbc1_{tc}')
        for tb in range(4):
            OP('t', lambda e, tb=tb: e.matmul(
                psX[:, tb * 128:(tb + 1) * 128],
                sel4_sb[:, tb * 128:(tb + 1) * 128], mx4[:, :],
                start=True, stop=True),
               ['mx4', 'sel4_sb', 'psX'], ['psX'], kind='c', accum=(tb > 0),
               tag=f'hbcm{tc}_{tb}')
        OP('v', lambda e: e.tensor_copy(out=bc2[:, :], in_=psX[:, :]),
           ['psX'], ['bc2'], tag=f'hbc2_{tc}')
        if upto == 'F1d' and tc == 0:
            dma('q', dbg2[:, :, :], hbuf1[:, 0:16, :],
                [f'hb1_{o}' for o in range(16)], ['dbgd2'], tag='dhpre')
        for ot in range(64):
            OP('v', lambda e, ot=ot: e.scalar_tensor_tensor(
                out=qt[:, :], in0=hbuf1[:, ot, :], scalar=lnw_sb[:, ot:ot + 1],
                in1=bc1[:, :], op0=ALU.mult, op1=ALU.mult),
               [f'hb1_{ot}', 'lnw_sb', 'bc1'], ['qt'], tag=f'qhm{tc}_{ot}')
            OP('v', lambda e, ot=ot: e.tensor_scalar(
                out=hbuf1[:, ot, :], in0=qt[:, :], scalar1=MAGIC, scalar2=MAGIC,
                op0=ALU.add, op1=ALU.subtract),
               ['qt'], [f'hb1_{ot}'], tag=f'qhr{tc}_{ot}')
        if upto == 'F1d' and tc == 0:
            dma('q', dbg3[:, :, :], hbuf1[:, 0:16, :],
                [f'hb1_{o}' for o in range(16)], ['dbgd3'], tag='dhpost')
        for ot in range(16):
            rnk, ot2 = ot // 2, ot % 2
            bank, pk = psG[ot % 2], f'psG{ot % 2}'
            for half in range(4):
                dw = qdb[half % 2]
                dma('q', dw[:, :], qd_o[rnk * 128:(rnk + 1) * 128,
                                        ot2 * 8192 + half * 2048:
                                        ot2 * 8192 + (half + 1) * 2048],
                    ['qAGd'], [f'qdb{half % 2}'], tag=f'ldqd{tc}_{ot}_{half}')
                for kt in range(16):
                    gk = half * 16 + kt
                    OP('t', lambda e, kt=kt, gk=gk, bank=bank, dw=dw,
                       first=(gk == 0), last=(gk == 63): e.matmul(
                        bank[:, :], dw[:, kt * 128:(kt + 1) * 128],
                        hbuf1[:, gk, :], start=first, stop=last),
                       [f'qdb{half % 2}', f'hb1_{gk}'], [pk], kind='c', accum=(gk > 0),
                       tag=f'dmm{tc}_{ot}_{gk}')
            OP('v', lambda e, bank=bank: e.scalar_tensor_tensor(
                out=qt[:, :], in0=bank[:, :], scalar=scol[:, 2:3], in1=bc2[:, :],
                op0=ALU.mult, op1=ALU.mult),
               [pk, 'scol', 'bc2'], ['qt'], tag=f'dev{tc}_{ot}')
            OP('v', lambda e, ot=ot, tc=tc: e.tensor_tensor(
                out=rbuf[:, ot, :], in0=qt[:, :],
                in1=stream[:, ot, tc * TC:(tc + 1) * TC], op=ALU.add),
               ['qt', f'stream_{ot}'], ['rB'], accum=(ot > 0), tag=f'drs{tc}_{ot}')

        def out_fn(kt, tc=tc):
            OP('v', lambda e, kt=kt: e.tensor_tensor(
                out=rbuf[:, kt, :], in0=rbuf[:, kt, :], in1=psM[:, :],
                op=ALU.mult),
               ['rB', 'ln4_psm'], ['rB'], accum=True, tag=f'ln4o{tc}_{kt}')
        emit_ln('ln4', 'rB', rbuf, out_fn, lnps)
        for tb in range(4):
            for hh in range(4):
                for kt in range(4):
                    gkt = hh * 4 + kt
                    OP('t', lambda e, gkt=gkt, tb=tb: e.transpose(
                        out=psX[:, 0:128],
                        in_=rbuf[:, gkt, tb * 128:(tb + 1) * 128],
                        identity=identf[:, :]),
                       ['rB', 'identf'], ['psX'], kind='c', tag=f'tp{tc}_{tb}_{gkt}')
                    OP('v', lambda e, kt=kt: e.tensor_copy(
                        out=obuf[:, kt, :], in_=psX[:, 0:128]),
                       ['psX'], ['obuf'], accum=(kt > 0), tag=f'tpc{tc}_{tb}_{gkt}')
                okey = f'outst{tc}_{tb}_{hh}'
                dma('q', out_t[(tc * 4 + tb) * 128:(tc * 4 + tb + 1) * 128,
                               hh * 512:(hh + 1) * 512]
                    .rearrange("p (kt c) -> p kt c", c=128),
                    obuf[:, :, :], ['obuf'], [okey], tag=okey)
                outks.append(okey)

    if upto == 'F1d':
        outks += ['dbgd1', 'dbgd2', 'dbgd3', 'dbgd4a', 'dbgd4b', 'dbgd4c',
                  'dbgd4d', 'dbgd4e', 'dbgd5']
    return finish(outks)



def kernel(**inputs):
    in_maps = stage_inputs(inputs)
    nc = build_program()
    res = run_bass_kernel_spmd(nc, in_maps, core_ids=list(range(NCORE)))
    out = np.empty((B, T, D), np.float32)
    for c in range(NCORE):
        g, j = c // 4, c % 4
        ch0, ch1 = own_chunks(j)
        o = np.asarray(res.results[c]['out'])
        out[g, ch0 * TC:(ch0 + 1) * TC] = o[:TC]
        out[g, ch1 * TC:(ch1 + 1) * TC] = o[TC:]
    return out
